# version 04
# baseline (speedup 1.0000x reference)
import numpy as np
from contextlib import ExitStack

import concourse.bass as bass
import concourse.mybir as mybir
from concourse.bass_utils import run_bass_kernel_spmd

F32 = mybir.dt.float32
BF16 = mybir.dt.bfloat16
AF = mybir.ActivationFunctionType
ALU = mybir.AluOpType
AX = mybir.AxisListType

H = 1024
IN = 2048
FF = 4096
NST = 16
RK = 64
KC = 4
NCH = IN // 128
NKH = H // 128
NFF = FF // 128
C_RND = 12582912.0
EPS = 1e-5

TA = 512
TB = 1024
TC = 1024
NGRP = 4


class P:
    def __init__(self, nc):
        self.nc = nc
        self.order = ("sync", "gpsimd", "vector", "scalar", "tensor")
        self.ops = {k: [] for k in self.order}
        self.cnt = {}

    def sem(self, ctx, name):
        s = ctx.enter_context(self.nc.semaphore(name))
        self.cnt[id(s)] = 0
        return s

    def emit(self, engine, fn):
        self.ops[engine].append(fn)

    def tok(self, sem, amt):
        self.cnt[id(sem)] += amt
        return (sem, self.cnt[id(sem)])

    def cur(self, sem):
        return (sem, self.cnt[id(sem)])

    def run_block(self):
        with self.nc.Block() as block:
            for eng_name in self.order:
                ops = self.ops[eng_name]
                if not ops:
                    continue

                def body(eng, ops=list(ops)):
                    for fn in ops:
                        fn(eng)

                getattr(block, eng_name)(body)
        self.ops = {k: [] for k in self.order}


def dma(p, queue, out_ap, in_ap, sem, waits=(), slow=False):
    waits = [w for w in waits if w is not None]
    tok = p.tok(sem, 16)

    def fn(eng, out_ap=out_ap, in_ap=in_ap, waits=tuple(waits)):
        for (ws, wv) in waits:
            eng.wait_ge(ws, wv)
        kw = {"allow_slow_non_contiguous": True} if slow else {}
        eng.dma_start(out=out_ap, in_=in_ap, **kw).then_inc(tok[0], 16)

    p.emit(queue, fn)
    return tok


def comp(p, engine, build, sem=None, waits=()):
    waits = [w for w in waits if w is not None]
    tok = p.tok(sem, 1) if sem is not None else None

    def fn(eng, waits=tuple(waits)):
        for (ws, wv) in waits:
            eng.wait_ge(ws, wv)
        inst = build(eng)
        if tok is not None and inst is not None:
            inst.then_inc(tok[0], 1)

    p.emit(engine, fn)
    return tok


def fwait(p, engine, waits):
    def fn(eng, waits=tuple(waits)):
        for (ws, wv) in waits:
            eng.wait_ge(ws, wv)

    p.emit(engine, fn)


def vgather(t, ncols):
    return bass.AP(t, 0, [[1, 128], [128, ncols]])


def build_program(L=8192, debug=False):
    nc = bass.Bass("TRN2", target_bir_lowering=False, debug=False)
    NT_A = L // TA
    NT_B = L // TB
    NT_C = L // TC

    hid = nc.dram_tensor("hidden", [L, H], F32, kind="ExternalInput")
    ln_w = nc.dram_tensor("ln_w", [H], F32, kind="ExternalInput")
    ln_b = nc.dram_tensor("ln_b", [H], F32, kind="ExternalInput")
    in_w = nc.dram_tensor("in_w", [2 * IN, H], F32, kind="ExternalInput")
    in_b = nc.dram_tensor("in_b", [2 * IN], F32, kind="ExternalInput")
    conv_w = nc.dram_tensor("conv_w", [IN, 1, KC], F32, kind="ExternalInput")
    conv_b = nc.dram_tensor("conv_b", [IN], F32, kind="ExternalInput")
    xp_w = nc.dram_tensor("xp_w", [RK + 2 * NST, IN], F32, kind="ExternalInput")
    dtp_w = nc.dram_tensor("dtp_w", [IN, RK], F32, kind="ExternalInput")
    dtp_b = nc.dram_tensor("dtp_b", [IN], F32, kind="ExternalInput")
    A_log = nc.dram_tensor("A_log", [IN, NST], F32, kind="ExternalInput")
    D_vec = nc.dram_tensor("D", [IN], F32, kind="ExternalInput")
    out_w = nc.dram_tensor("out_w", [H, IN], F32, kind="ExternalInput")
    out_b = nc.dram_tensor("out_b", [H], F32, kind="ExternalInput")
    w_gate = nc.dram_tensor("w_gate", [FF, H], F32, kind="ExternalInput")
    w_up = nc.dram_tensor("w_up", [FF, H], F32, kind="ExternalInput")
    w_down = nc.dram_tensor("w_down", [H, FF], F32, kind="ExternalInput")
    out = nc.dram_tensor("out", [L, H], F32, kind="ExternalOutput")

    dk = "ExternalOutput" if debug else "Internal"
    dt_s = nc.dram_tensor("dt_s", [NCH, 128, L], F32, kind=dk)
    dtx_s = nc.dram_tensor("dtx_s", [NCH, 128, L], F32, kind=dk)
    ys_s = nc.dram_tensor("ys_s", [NCH, 128, L], BF16, kind=dk)
    g_s = nc.dram_tensor("g_s", [NCH, 128, L], BF16, kind=dk)
    x1_s = nc.dram_tensor("x1_s", [L, H], F32, kind=dk)
    hp_s = nc.dram_tensor("hp_s", [2, TC, FF], F32, kind=dk)
    bcT_s = nc.dram_tensor("bcT_s", [32, L], F32, kind=dk)
    wgT_s = nc.dram_tensor("wgT_s", [NKH, 128, FF], BF16, kind=dk)
    if debug:
        inwT_d = nc.dram_tensor("inwT_d", [128, NKH * 2 * IN], BF16,
                                kind="ExternalOutput")
        btld_d = nc.dram_tensor("btld_d", [128, 32], F32, kind="ExternalOutput")
        mw_d = nc.dram_tensor("mw_d", [128, 3], F32, kind="ExternalOutput")
    wuT_s = nc.dram_tensor("wuT_s", [NKH, 128, FF], BF16, kind=dk)
    wdT_s = nc.dram_tensor("wdT_s", [NFF, 128, H], BF16, kind=dk)

    with ExitStack() as per:
        def sbp(name, shape, dt=F32):
            return per.enter_context(nc.sbuf_tensor(name, shape, dt))

        s_id = sbp("s_id", [128, 128], BF16)
        s_lnw = sbp("s_lnw", [128, NKH])
        s_lnb_bf = sbp("s_lnb_bf", [128, NKH], BF16)
        s_inb = sbp("s_inb", [128, 32])
        s_btld = sbp("s_btld", [128, 32])
        s_cw = sbp("s_cw", [128, NCH * KC])
        s_cb = sbp("s_cb", [128, NCH])
        s_dtpb = sbp("s_dtpb", [128, NCH])
        s_dcol = sbp("s_dcol", [128, NCH])
        s_acol = sbp("s_acol", [128, NCH * NST])
        s_mw = sbp("s_mw", [128, 3])
        s_outb_row = sbp("s_outb_row", [1, H], BF16)
        s_ones_bf = sbp("s_ones_bf", [1, 128], BF16)
        s_ones_f32 = sbp("s_ones_f32", [1, 128])
        s_onecol = sbp("s_onecol", [128, 1])
        s_epscol = sbp("s_epscol", [128, 1])
        s_mhalf = sbp("s_mhalf", [128, 1])
        s_carry = sbp("s_carry", [128, NCH * NST])
        s_halo = sbp("s_halo", [128, NCH * (KC - 1)])

        p = P(nc)
        sm_ld = p.sem(per, "ld")
        sm_st = p.sem(per, "st")
        sm_v = p.sem(per, "v")
        sm_a = p.sem(per, "a")
        sm_t = p.sem(per, "t")
        sm_q = p.sem(per, "q")
        sm_o = p.sem(per, "o")

        with ExitStack() as ph:
            t_iop = ph.enter_context(nc.sbuf_tensor("t_iop", [128, 128], F32))
            t_iof = ph.enter_context(nc.sbuf_tensor("t_iof", [128, 128], F32))
            t_al = ph.enter_context(nc.sbuf_tensor("t_al", [128, NST], F32))
            t_lnb = ph.enter_context(nc.sbuf_tensor("t_lnb", [128, NKH], F32))
            t_ob = ph.enter_context(nc.sbuf_tensor("t_ob", [1, H], F32))

            def g_init(g):
                g.iota(t_iop[:, :], [[0, 128]], channel_multiplier=1,
                       allow_small_or_imprecise_dtypes=True)
                g.iota(t_iof[:, :], [[1, 128]], channel_multiplier=0,
                       allow_small_or_imprecise_dtypes=True)
                g.memset(s_ones_bf[:, :], 1.0)
                g.memset(s_ones_f32[:, :], 1.0)
                g.memset(s_onecol[:, :], 1.0)
                g.memset(s_epscol[:, :], 1e-5)
                g.memset(s_mhalf[:, :], -0.5)
                g.memset(s_carry[:, :], 0.0)
                return g.memset(s_halo[:, :], 0.0)

            tg = comp(p, "gpsimd", g_init, sm_q)
            tl2 = dma(p, "sync", t_lnb[:, :], vgather(ln_b, NKH), sm_ld, slow=True)
            dma(p, "sync", s_lnw[:, :], vgather(ln_w, NKH), sm_ld, slow=True)
            dma(p, "sync", s_inb[:, :], vgather(in_b, 32), sm_ld, slow=True)
            dma(p, "sync", s_cb[:, :], vgather(conv_b, NCH), sm_ld, slow=True)
            dma(p, "sync", s_dtpb[:, :], vgather(dtp_b, NCH), sm_ld, slow=True)
            dma(p, "sync", s_dcol[:, :], vgather(D_vec, NCH), sm_ld, slow=True)
            dma(p, "sync", s_cw[:, :],
                bass.AP(conv_w, 0, [[KC, 128], [KC * 128, NCH], [1, KC]]),
                sm_ld, slow=True)
            tob = dma(p, "sync", t_ob[:, :],
                      bass.AP(out_b, 0, [[H, 1], [1, H]]), sm_ld)
            comp(p, "vector", lambda v: v.tensor_tensor(
                s_id[:, :], t_iop[:, :], t_iof[:, :], ALU.is_equal), sm_v,
                waits=[tg])
            comp(p, "vector", lambda v: v.tensor_copy(
                s_lnb_bf[:, :], t_lnb[:, :]), sm_v, waits=[tl2])
            comp(p, "vector", lambda v: v.tensor_copy(
                s_outb_row[:, :], t_ob[:, :]), sm_v, waits=[tob])
            for ch in range(NCH):
                tok = dma(p, "sync", t_al[:, :],
                          bass.AP(A_log, ch * 128 * NST, [[NST, 128], [1, NST]]),
                          sm_ld, waits=[p.cur(sm_a)] if ch else [])
                ta = comp(p, "scalar", lambda s, ch=ch: s.activation(
                    out=s_acol[:, ch * NST:(ch + 1) * NST], in_=t_al[:, :],
                    func=AF.Exp), sm_a, waits=[tok])
                comp(p, "vector", lambda v, ch=ch: v.tensor_scalar(
                    s_acol[:, ch * NST:(ch + 1) * NST],
                    s_acol[:, ch * NST:(ch + 1) * NST], -1.0, None, ALU.mult),
                    sm_v, waits=[ta])
            p.run_block()

        with ExitStack() as phA:
            s_inwT = phA.enter_context(
                nc.sbuf_tensor("s_inwT", [128, NKH * 2 * IN], BF16))

            with ExitStack() as ph:
                t_w = [ph.enter_context(nc.sbuf_tensor(f"t_w{i}", [128, FF], F32))
                       for i in range(2)]
                t_wq = ph.enter_context(nc.sbuf_tensor("t_wq", [128, FF], BF16))
                t_red = [ph.enter_context(nc.sbuf_tensor(f"t_red{i}", [128, 1], F32))
                         for i in range(3)]
                t_sc = ph.enter_context(nc.sbuf_tensor("t_sc", [1, 4], F32))
                t_scb = ph.enter_context(nc.sbuf_tensor("t_scb", [128, 1], F32))
                t_ev = [ph.enter_context(nc.sbuf_tensor(f"t_ev{i}", [128, 512], BF16))
                        for i in range(2)]
                p_tr = [ph.enter_context(nc.psum_tensor(f"p_tr{i}", [128, 512], BF16))
                        for i in range(2)]
                p_sc = ph.enter_context(nc.psum_tensor("p_sc", [128, 4], F32))

                def quant_matrix(w_t, rows, cols, wT_s, mw_idx):
                    nchr = rows // 128
                    for cch in range(nchr):
                        buf = t_w[cch % 2]
                        tok = dma(p, "sync", buf[:, :cols],
                                  bass.AP(w_t, cch * 128 * cols,
                                          [[cols, 128], [1, cols]]), sm_ld,
                                  waits=[p.cur(sm_v), p.cur(sm_t)]
                                  if cch < 2 else [p.cur(sm_v)])
                        red = t_red[cch % 2]
                        tr = comp(p, "vector", lambda v, buf=buf, red=red, cols=cols:
                                  v.reduce_sum(red[:, :], buf[:, :cols], axis=AX.X,
                                               apply_absolute_value=True),
                                  sm_v, waits=[tok])
                        if cch == 0:
                            comp(p, "vector", lambda v, red=red: v.tensor_copy(
                                t_red[2][:, :], red[:, :]), sm_v,
                                waits=[tr, p.cur(sm_t)])
                        else:
                            comp(p, "vector", lambda v, red=red: v.tensor_add(
                                t_red[2][:, :], t_red[2][:, :], red[:, :]), sm_v,
                                waits=[tr])
                    tm = comp(p, "tensor", lambda t: t.matmul(
                        p_sc[:1, 0:1], t_red[2][:, :], s_onecol[:, :],
                        start=True, stop=True), sm_t, waits=[p.cur(sm_v)])
                    tv1 = comp(p, "vector", lambda v, rows=rows, cols=cols:
                               v.tensor_scalar(
                                   t_sc[:1, 0:1], p_sc[:1, 0:1],
                                   1.0 / (rows * cols), 1e-5, ALU.mult, ALU.max),
                               sm_v, waits=[tm])
                    tv2 = comp(p, "vector", lambda v: v.reciprocal(
                        t_sc[:1, 1:2], t_sc[:1, 0:1]), sm_v, waits=[tv1])
                    tb = comp(p, "tensor", lambda t: t.matmul(
                        p_sc[:, 0:2], s_ones_f32[:1, :], t_sc[:1, 0:2],
                        start=True, stop=True), sm_t, waits=[tv2])
                    tvb = comp(p, "vector", lambda v: v.tensor_copy(
                        t_scb[:, :], p_sc[:, 1:2]), sm_v, waits=[tb])
                    comp(p, "vector", lambda v, mw_idx=mw_idx: v.tensor_copy(
                        s_mw[:, mw_idx:mw_idx + 1], p_sc[:, 0:1]), sm_v)
                    for cch in range(nchr):
                        buf = t_w[cch % 2]
                        tok = dma(p, "sync", buf[:, :cols],
                                  bass.AP(w_t, cch * 128 * cols,
                                          [[cols, 128], [1, cols]]), sm_ld,
                                  waits=[p.cur(sm_v)])
                        tq1 = comp(p, "vector", lambda v, buf=buf, cols=cols:
                                   v.tensor_scalar(
                                       buf[:, :cols], buf[:, :cols], t_scb[:, :],
                                       C_RND, ALU.mult, ALU.add), sm_v,
                                   waits=[tok, tvb, p.cur(sm_t)])
                        tq2 = comp(p, "vector", lambda v, buf=buf, cols=cols:
                                   v.tensor_scalar(
                                       buf[:, :cols], buf[:, :cols], C_RND, -1.0,
                                       ALU.subtract, ALU.max), sm_v, waits=[tq1])
                        tq3 = comp(p, "vector", lambda v, buf=buf, cols=cols:
                                   v.tensor_scalar(
                                       t_wq[:, :cols], buf[:, :cols], 1.0, None,
                                       ALU.min), sm_v, waits=[tq2])
                        for grp in range(cols // 512):
                            pt = p_tr[grp % 2]
                            tt = None
                            for b4 in range(4):
                                bd = grp * 4 + b4
                                tt = comp(p, "tensor",
                                          lambda t, bd=bd, b4=b4, pt=pt: t.transpose(
                                              pt[:, b4 * 128:(b4 + 1) * 128],
                                              t_wq[:, bd * 128:(bd + 1) * 128],
                                              s_id[:, :]),
                                          sm_t if b4 == 3 else None,
                                          waits=[tq3, p.cur(sm_st)]
                                          if b4 == 0 else [])
                            ev = t_ev[grp % 2]
                            ta = comp(p, "scalar", lambda s, ev=ev, pt=pt:
                                      s.activation(out=ev[:, :], in_=pt[:, :],
                                                   func=AF.Identity),
                                      sm_a, waits=[tt])
                            dma(p, "scalar",
                                bass.AP(wT_s, grp * 4 * 128 * rows + cch * 128,
                                        [[rows, 128], [128 * rows, 4], [1, 128]]),
                                ev[:, :], sm_st, waits=[ta])

                quant_matrix(w_gate, FF, H, wgT_s, 0)
                quant_matrix(w_up, FF, H, wuT_s, 1)
                quant_matrix(w_down, H, FF, wdT_s, 2)

                for cch in range(2 * IN // 128):
                    buf = t_w[cch % 2]
                    tok = dma(p, "sync", buf[:, :H],
                              bass.AP(in_w, cch * 128 * H, [[H, 128], [1, H]]),
                              sm_ld, waits=[p.cur(sm_v)])
                    tc = comp(p, "vector", lambda v, buf=buf: v.tensor_copy(
                        t_wq[:, :H], buf[:, :H]), sm_v,
                        waits=[tok, p.cur(sm_t)])
                    for grp in range(NKH // 4):
                        pt = p_tr[grp % 2]
                        tt = None
                        for b4 in range(4):
                            bd = grp * 4 + b4
                            tt = comp(p, "tensor",
                                      lambda t, bd=bd, b4=b4, pt=pt: t.transpose(
                                          pt[:, b4 * 128:(b4 + 1) * 128],
                                          t_wq[:, bd * 128:(bd + 1) * 128],
                                          s_id[:, :]),
                                      sm_t if b4 == 3 else None,
                                      waits=[tc, p.cur(sm_a)] if b4 == 0 else [])
                        for b4 in range(4):
                            bd = grp * 4 + b4
                            comp(p, "scalar",
                                 lambda s, bd=bd, b4=b4, pt=pt, cch=cch:
                                 s.activation(
                                     out=s_inwT[:, bd * 2 * IN + cch * 128:
                                                bd * 2 * IN + (cch + 1) * 128],
                                     in_=pt[:, b4 * 128:(b4 + 1) * 128],
                                     func=AF.Identity),
                                 sm_a, waits=[tt] if b4 == 0 else [])
                for ob in range(32):
                    tprev = None
                    for bd in range(NKH):
                        tprev = comp(p, "tensor", lambda t, ob=ob, bd=bd: t.matmul(
                            p_sc[:, 2:3],
                            s_inwT[:, bd * 2 * IN + ob * 128:
                                   bd * 2 * IN + ob * 128 + 128],
                            s_lnb_bf[:, bd:bd + 1],
                            start=(bd == 0), stop=(bd == NKH - 1)),
                            sm_t if bd == NKH - 1 else None,
                            waits=[p.cur(sm_a), p.cur(sm_v)] if bd == 0 else [])
                    tvb2 = comp(p, "vector", lambda v, ob=ob: v.tensor_tensor(
                        s_btld[:, ob:ob + 1], p_sc[:, 2:3], s_inb[:, ob:ob + 1],
                        ALU.add), sm_v, waits=[tprev])
                    fwait(p, "tensor", [tvb2])
                for bd in range(NKH):
                    comp(p, "vector", lambda v, bd=bd: v.tensor_scalar(
                        s_inwT[:, bd * 2 * IN:(bd + 1) * 2 * IN],
                        s_inwT[:, bd * 2 * IN:(bd + 1) * 2 * IN],
                        s_lnw[:, bd:bd + 1], None, ALU.mult), sm_v,
                        waits=[p.cur(sm_t)] if bd == 0 else [])
                if debug:
                    dma(p, "scalar", inwT_d.ap(), s_inwT[:, :], sm_st,
                        waits=[p.cur(sm_v)])
                    dma(p, "scalar", btld_d.ap(), s_btld[:, :], sm_st,
                        waits=[p.cur(sm_v)])
                    dma(p, "scalar", mw_d.ap(), s_mw[:, :], sm_st,
                        waits=[p.cur(sm_v)])
                    fwait(p, "scalar", [p.cur(sm_st)])
                p.run_block()

            with ExitStack() as ph:
                s_xpT = ph.enter_context(
                    nc.sbuf_tensor("s_xpT", [128, NCH * 96], BF16))
                s_dtpT = ph.enter_context(
                    nc.sbuf_tensor("s_dtpT", [64, NCH * 128], BF16))
                NB = TA // 128
                s_h = [ph.enter_context(
                    nc.sbuf_tensor(f"s_h{i}", [128, NB * H], F32))
                    for i in range(2)]
                s_hn2 = [ph.enter_context(
                    nc.sbuf_tensor(f"s_hn{i}", [128, NB * H], BF16))
                    for i in range(2)]
                s_hn = s_hn2[0]
                s_hnT2 = [ph.enter_context(
                    nc.sbuf_tensor(f"s_hnT{i}", [128, NKH * TA], BF16))
                    for i in range(2)]
                s_hnT = s_hnT2[0]
                s_xc = ph.enter_context(
                    nc.sbuf_tensor("s_xc", [128, NCH * (TA + 3)], F32))
                s_par = ph.enter_context(nc.sbuf_tensor("s_par", [96, TA], F32))
                s_drb = ph.enter_context(nc.sbuf_tensor("s_drb", [64, TA], BF16))
                s_stat = ph.enter_context(
                    nc.sbuf_tensor("s_stat", [128, 8 * NB], F32))
                s_sq = ph.enter_context(nc.sbuf_tensor("s_sq", [128, H], F32))
                s_ca = [ph.enter_context(
                    nc.sbuf_tensor(f"s_ca{i}", [128, TA], F32)) for i in range(2)]
                s_xcb = [ph.enter_context(
                    nc.sbuf_tensor(f"s_xcb{i}", [128, TA], BF16)) for i in range(2)]
                s_dt = [ph.enter_context(
                    nc.sbuf_tensor(f"s_dt{i}", [128, TA], F32)) for i in range(2)]
                s_z = [ph.enter_context(
                    nc.sbuf_tensor(f"s_z{i}", [128, TA], F32)) for i in range(2)]
                s_sp = [ph.enter_context(
                    nc.sbuf_tensor(f"s_sp{i}", [128, TA], F32)) for i in range(2)]
                s_gt = [ph.enter_context(
                    nc.sbuf_tensor(f"s_gt{i}", [128, TA], BF16)) for i in range(2)]
                s_dx = [ph.enter_context(
                    nc.sbuf_tensor(f"s_dx{i}", [128, TA], BF16)) for i in range(2)]
                p_mm = [ph.enter_context(
                    nc.psum_tensor(f"p_mm{i}", [128, TA], F32)) for i in range(2)]
                p_xp = ph.enter_context(nc.psum_tensor("p_xp", [96, TA], F32))
                p_dt = [ph.enter_context(
                    nc.psum_tensor(f"p_dt{i}", [128, TA], F32)) for i in range(2)]
                p_t2 = ph.enter_context(nc.psum_tensor("p_t2", [128, 512], BF16))

                t_xw = s_h[0]
                t_xq2 = s_h[1]
                tok = dma(p, "sync", t_xw[:96, :IN],
                          bass.AP(xp_w, 0, [[IN, 96], [1, IN]]), sm_ld)
                tcx = comp(p, "vector", lambda v: v.tensor_copy(
                    s_hn[:96, :IN], t_xw[:96, :IN]), sm_v, waits=[tok])
                for ch in range(NCH):
                    sl4 = (ch % 2) * 128
                    tt = comp(p, "tensor", lambda t, ch=ch, sl4=sl4: t.transpose(
                        p_t2[:, sl4:sl4 + 96],
                        s_hn[:96, ch * 128:(ch + 1) * 128], s_id[:96, :96]),
                        sm_t, waits=[tcx, p.cur(sm_a)])
                    comp(p, "scalar", lambda s, ch=ch, sl4=sl4: s.activation(
                        out=s_xpT[:, ch * 96:(ch + 1) * 96],
                        in_=p_t2[:, sl4:sl4 + 96], func=AF.Identity), sm_a,
                        waits=[tt])
                tdw = dma(p, "sync", t_xq2[:, :RK * NCH],
                          bass.AP(dtp_w, 0, [[RK, 128], [128 * RK, NCH], [1, RK]]),
                          sm_ld)
                tcd = comp(p, "vector", lambda v: v.tensor_copy(
                    s_hn[:, :RK * NCH].bitcast(BF16)
                    if False else s_hn[:, :RK * NCH], t_xq2[:, :RK * NCH]),
                    sm_v, waits=[tdw, p.cur(sm_t)])
                for ch in range(NCH):
                    sl4 = (ch % 2) * 128
                    tt = comp(p, "tensor", lambda t, ch=ch, sl4=sl4: t.transpose(
                        p_t2[:RK, sl4:sl4 + 128],
                        s_hn[:, ch * RK:(ch + 1) * RK], s_id[:, :]),
                        sm_t, waits=[tcd, p.cur(sm_a)])
                    comp(p, "scalar", lambda s, ch=ch, sl4=sl4: s.activation(
                        out=s_dtpT[:, ch * 128:(ch + 1) * 128],
                        in_=p_t2[:RK, sl4:sl4 + 128], func=AF.Identity), sm_a,
                        waits=[tt])
                p.run_block()

                h_used = [None, None]
                cons_x = {}
                g_used = [None, None]
                cons_hn = [None, None]
                cons_hnT = [None, None]
                xp_evac = None
                dtp_last = None
                evac_tok = {}
                for it in range(NT_A):
                    base = it * TA
                    s_hn = s_hn2[it % 2]
                    s_hnT = s_hnT2[it % 2]
                    hb = s_h[it % 2]
                    htok = dma(
                        p, "sync", hb[:, :],
                        bass.AP(hid, base * H, [[H, 128], [128 * H, NB], [1, H]]),
                        sm_ld, waits=[h_used[it % 2]])
                    hn_toks = []
                    red_toks = []
                    sq_toks = []
                    for j in range(NB):
                        hbj = hb[:, j * H:(j + 1) * H]
                        c0 = 8 * j
                        red_toks.append(comp(
                            p, "vector", lambda v, hbj=hbj, c0=c0:
                            v.reduce_sum(s_stat[:, c0:c0 + 1], hbj, axis=AX.X),
                            sm_v, waits=[htok] + ([cons_hn[it % 2]] if j == 0 else [])))
                    for j in range(NB):
                        hbj = hb[:, j * H:(j + 1) * H]
                        c0 = 8 * j
                        sq_toks.append(comp(
                            p, "scalar", lambda s, hbj=hbj, c0=c0:
                            s.activation(out=s_sq[:, :], in_=hbj,
                                         func=AF.Square,
                                         accum_out=s_stat[:, c0 + 1:c0 + 2]),
                            sm_a, waits=[htok] if j else [htok, p.cur(sm_v)]))
                    def sv(col):
                        return bass.AP(s_stat, col, [[8 * NB, 128], [8, NB]])
                    tv1 = comp(p, "vector", lambda v: v.tensor_scalar(
                        sv(0), sv(0), 1.0 / H, None, ALU.mult), sm_v,
                        waits=[sq_toks[-1], red_toks[-1]])
                    tv2 = comp(p, "vector", lambda v: v.tensor_tensor(
                        sv(2), sv(0), sv(0), ALU.mult), sm_v, waits=[tv1])
                    tv3 = comp(p, "vector", lambda v: v.scalar_tensor_tensor(
                        sv(3), sv(1), 1.0 / H, sv(2), ALU.mult, ALU.subtract),
                        sm_v, waits=[tv2])
                    ta1 = comp(p, "scalar", lambda s: s.activation(
                        out=sv(4), in_=sv(3), func=AF.Ln,
                        bias=s_epscol[:, :]), sm_a, waits=[tv3])
                    ta2 = comp(p, "scalar", lambda s: s.activation(
                        out=sv(5), in_=sv(4), func=AF.Exp,
                        scale=s_mhalf[:, :]), sm_a, waits=[ta1])
                    tv4 = comp(p, "vector", lambda v: v.tensor_tensor(
                        sv(6), sv(0), sv(5), ALU.mult), sm_v, waits=[ta2])
                    for j in range(NB):
                        hbj = hb[:, j * H:(j + 1) * H]
                        c0 = 8 * j
                        hn_toks.append(comp(
                            p, "vector", lambda v, hbj=hbj, j=j, c0=c0, s_hn=s_hn:
                            v.tensor_scalar(
                                s_hn[:, j * H:(j + 1) * H], hbj,
                                s_stat[:, c0 + 5:c0 + 6],
                                s_stat[:, c0 + 6:c0 + 7],
                                ALU.mult, ALU.subtract), sm_v,
                            waits=[tv4, cons_hn[it % 2]] if j == 0 else []))
                    h_used[it % 2] = hn_toks[-1]
                    for j in range(NB):
                        for g2 in range(2):
                            tt = None
                            for b4 in range(4):
                                bd = g2 * 4 + b4
                                k = j * NKH + bd
                                w = []
                                if b4 == 0:
                                    w = [hn_toks[j], p.cur(sm_a),
                                         cons_hnT[it % 2] if k == 0 else None]
                                tt = comp(
                                    p, "tensor",
                                    lambda t, j=j, bd=bd, b4=b4, s_hn=s_hn:
                                    t.transpose(
                                        p_t2[:, b4 * 128:(b4 + 1) * 128],
                                        s_hn[:, j * H + bd * 128:
                                             j * H + (bd + 1) * 128],
                                        s_id[:, :]),
                                    sm_t if b4 == 3 else None, waits=w)
                            evac_tok[("hnT", it, j, g2)] = comp(
                                p, "scalar", lambda s, j=j, g2=g2, s_hnT=s_hnT:
                                s.activation(
                                    out=bass.AP(
                                        s_hnT, (g2 * 4) * TA + j * 128,
                                        [[NKH * TA, 128], [TA, 4], [1, 128]]),
                                    in_=p_t2[:, :], func=AF.Identity),
                                sm_a, waits=[tt])
                    cons_hn[it % 2] = p.cur(sm_t)
                    hnT_ready = evac_tok[("hnT", it, NB - 1, 1)]
                    for ob in range(32):
                        pm = p_mm[ob % 2]
                        tmm = None
                        for bd in range(NKH):
                            w = []
                            if bd == 0:
                                w = [hnT_ready, evac_tok.get(("ev", it, ob - 2))]
                            tmm = comp(p, "tensor",
                                       lambda t, ob=ob, bd=bd, pm=pm,
                                       s_hnT=s_hnT: t.matmul(
                                           pm[:, :],
                                           s_inwT[:, bd * 2 * IN + ob * 128:
                                                  bd * 2 * IN + ob * 128 + 128],
                                           s_hnT[:, bd * TA:(bd + 1) * TA],
                                           start=(bd == 0), stop=(bd == NKH - 1)),
                                       sm_t if bd == NKH - 1 else None, waits=w)
                        if ob < 16:
                            ch = ob
                            evac_tok[("ev", it, ob)] = comp(
                                p, "scalar", lambda s, ch=ch, pm=pm, ob=ob:
                                s.activation(
                                    out=s_xc[:, ch * (TA + 3) + 3:
                                             (ch + 1) * (TA + 3)],
                                    in_=pm[:, :], func=AF.Identity,
                                    bias=s_btld[:, ob:ob + 1]), sm_a,
                                waits=[tmm, cons_x.get(ch)])
                        else:
                            ch = ob - 16
                            gb = s_gt[ch % 2]
                            tev = comp(
                                p, "scalar", lambda s, gb=gb, pm=pm, ob=ob:
                                s.activation(out=gb[:, :], in_=pm[:, :],
                                             func=AF.Silu,
                                             bias=s_btld[:, ob:ob + 1]), sm_a,
                                waits=[tmm, g_used[ch % 2]])
                            evac_tok[("ev", it, ob)] = tev
                            g_used[ch % 2] = dma(
                                p, "scalar",
                                bass.AP(g_s, ch * 128 * L + base,
                                        [[L, 128], [1, TA]]),
                                gb[:, :], sm_st, waits=[tev])
                    for ch in range(NCH):
                        xs = ch * (TA + 3)
                        ev = evac_tok[("ev", it, ch)]
                        th0 = comp(p, "vector", lambda v, ch=ch, xs=xs:
                                   v.tensor_copy(s_xc[:, xs:xs + 3],
                                                 s_halo[:, ch * 3:(ch + 1) * 3]),
                                   sm_v, waits=[ev])
                        th1 = comp(p, "vector", lambda v, ch=ch, xs=xs:
                                   v.tensor_copy(s_halo[:, ch * 3:(ch + 1) * 3],
                                                 s_xc[:, xs + TA:xs + TA + 3]),
                                   sm_v, waits=[th0])
                        tcv = comp(p, "vector", lambda v, ch=ch, xs=xs:
                                   v.tensor_scalar(
                                       s_ca[0][:, :], s_xc[:, xs:xs + TA],
                                       s_cw[:, ch * KC:ch * KC + 1], None,
                                       ALU.mult), sm_v, waits=[th1])
                        for k in range(1, KC):
                            src = s_ca[(k - 1) % 2]
                            dst = s_ca[k % 2]
                            tcv = comp(p, "vector",
                                       lambda v, ch=ch, xs=xs, k=k, src=src,
                                       dst=dst:
                                       v.scalar_tensor_tensor(
                                           dst[:, :], s_xc[:, xs + k:xs + k + TA],
                                           s_cw[:, ch * KC + k:ch * KC + k + 1],
                                           src[:, :], ALU.mult, ALU.add),
                                       sm_v, waits=[tcv])
                        tsl = comp(p, "scalar", lambda s, ch=ch, xs=xs:
                                   s.activation(
                                       out=s_xc[:, xs + 3:xs + 3 + TA],
                                       in_=s_ca[(KC - 1) % 2][:, :],
                                       func=AF.Silu, bias=s_cb[:, ch:ch + 1]),
                                   sm_a, waits=[tcv])
                        xb = s_xcb[ch % 2]
                        tcb = comp(p, "vector", lambda v, ch=ch, xs=xs, xb=xb:
                                   v.tensor_copy(xb[:, :],
                                                 s_xc[:, xs + 3:xs + 3 + TA]),
                                   sm_v, waits=[tsl])
                        comp(p, "tensor", lambda t, ch=ch, xb=xb: t.matmul(
                            p_xp[:, :], s_xpT[:, ch * 96:(ch + 1) * 96],
                            xb[:, :], start=(ch == 0), stop=(ch == NCH - 1)),
                            sm_t if ch == NCH - 1 else None,
                            waits=[tcb] + ([xp_evac] if ch == 0 else []))
                        dxb = s_dx[ch % 2]
                        tdx = comp(p, "vector", lambda v, ch=ch, xs=xs, dxb=dxb:
                                   v.tensor_scalar(
                                       dxb[:, :], s_xc[:, xs + 3:xs + 3 + TA],
                                       s_dcol[:, ch:ch + 1], None, ALU.mult),
                                   sm_v, waits=[tsl])
                        dma(p, "scalar",
                            bass.AP(ys_s, ch * 128 * L + base,
                                    [[L, 128], [1, TA]]),
                            dxb[:, :], sm_st, waits=[tdx])
                    xp_mm_tok = p.cur(sm_t)
                    xp_evac = comp(p, "scalar", lambda s: s.activation(
                        out=s_par[:, :], in_=p_xp[:, :], func=AF.Identity),
                        sm_a, waits=[xp_mm_tok, dtp_last])
                    dma(p, "sync",
                        bass.AP(bcT_s, base, [[L, 32], [1, TA]]),
                        bass.AP(s_par, 64 * TA, [[TA, 32], [1, TA]]),
                        sm_ld, waits=[xp_evac])
                    tdr = comp(p, "vector", lambda v: v.tensor_copy(
                        s_drb[:, :], s_par[:64, :]), sm_v, waits=[xp_evac])
                    for ch in range(NCH):
                        pd = p_dt[ch % 2]
                        zb = s_z[ch % 2]
                        spb = s_sp[ch % 2]
                        db = s_dt[ch % 2]
                        tdm = comp(p, "tensor", lambda t, ch=ch, pd=pd: t.matmul(
                            pd[:, :], s_dtpT[:, ch * 128:(ch + 1) * 128],
                            s_drb[:, :], start=True, stop=True), sm_t,
                            waits=[tdr] + ([evac_tok.get(("z", it, ch - 2))]
                                           if ch >= 2 else []))
                        tz = comp(p, "vector", lambda v, pd=pd, ch=ch, zb=zb:
                                  v.tensor_scalar(zb[:, :], pd[:, :],
                                                  s_dtpb[:, ch:ch + 1], None,
                                                  ALU.add), sm_v, waits=[tdm])
                        evac_tok[("z", it, ch)] = tz
                        nzb = s_ca[ch % 2]
                        tsp0 = comp(p, "vector", lambda v, zb=zb, nzb=nzb:
                                    v.tensor_scalar(nzb[:, :], zb[:, :], -1.0,
                                                    None, ALU.mult),
                                    sm_v, waits=[tz])
                        tsp1 = comp(p, "vector", lambda v, zb=zb, spb=spb,
                                    nzb=nzb:
                                    v.tensor_tensor(spb[:, :], zb[:, :],
                                                    nzb[:, :], ALU.min),
                                    sm_v, waits=[tsp0])
                        tsp2 = comp(p, "scalar", lambda s, spb=spb: s.activation(
                            out=spb[:, :], in_=spb[:, :], func=AF.Exp), sm_a,
                            waits=[tsp1])
                        tsp3 = comp(p, "scalar", lambda s, spb=spb: s.activation(
                            out=spb[:, :], in_=spb[:, :], func=AF.Ln, bias=1.0),
                            sm_a, waits=[tsp2])
                        tsp4 = comp(p, "vector",
                                    lambda v, zb=zb, spb=spb, db=db:
                                    v.scalar_tensor_tensor(
                                        db[:, :], zb[:, :], 0.0, spb[:, :],
                                        ALU.max, ALU.add), sm_v, waits=[tsp3])
                        dma(p, "scalar",
                            bass.AP(dt_s, ch * 128 * L + base,
                                    [[L, 128], [1, TA]]),
                            db[:, :], sm_st, waits=[tsp4])
                        xs = ch * (TA + 3)
                        tdtx = comp(p, "vector",
                                    lambda v, db=db, ch=ch, xs=xs, zb=zb:
                                    v.tensor_tensor(
                                        zb[:, :], db[:, :],
                                        s_xc[:, xs + 3:xs + 3 + TA], ALU.mult),
                                    sm_v, waits=[tsp4])
                        cons_x[ch] = tdtx
                        dma(p, "scalar",
                            bass.AP(dtx_s, ch * 128 * L + base,
                                    [[L, 128], [1, TA]]),
                            zb[:, :], sm_st, waits=[tdtx])
                    dtp_last = p.cur(sm_v)
                    cons_hnT[it % 2] = p.cur(sm_t)
                fwait(p, "scalar", [p.cur(sm_st)])
                fwait(p, "sync", [p.cur(sm_ld)])
                p.run_block()

        with ExitStack() as phB:
            s_owT = phB.enter_context(
                nc.sbuf_tensor("s_owT", [128, NCH * H], BF16))
            with ExitStack() as ph:
                t_w = [ph.enter_context(nc.sbuf_tensor(f"u_w{i}", [128, IN], F32))
                       for i in range(2)]
                t_wq = ph.enter_context(nc.sbuf_tensor("u_wq", [128, IN], BF16))
                p_tr = [ph.enter_context(nc.psum_tensor(f"q_tr{i}", [128, 512], BF16))
                        for i in range(2)]
                for oc in range(NKH):
                    buf = t_w[oc % 2]
                    tok = dma(p, "sync", buf[:, :],
                              bass.AP(out_w, oc * 128 * IN, [[IN, 128], [1, IN]]),
                              sm_ld, waits=[p.cur(sm_v)] if oc >= 2 else [])
                    tc = comp(p, "vector", lambda v, buf=buf: v.tensor_copy(
                        t_wq[:, :], buf[:, :]), sm_v, waits=[tok, p.cur(sm_t)])
                    for grp in range(NCH // 4):
                        pt = p_tr[grp % 2]
                        tt = None
                        for b4 in range(4):
                            cc = grp * 4 + b4
                            tt = comp(p, "tensor",
                                      lambda t, cc=cc, b4=b4, pt=pt: t.transpose(
                                          pt[:, b4 * 128:(b4 + 1) * 128],
                                          t_wq[:, cc * 128:(cc + 1) * 128],
                                          s_id[:, :]),
                                      sm_t if b4 == 3 else None,
                                      waits=[tc, p.cur(sm_a)] if b4 == 0 else [])
                        comp(p, "scalar", lambda s, grp=grp, oc=oc, pt=pt:
                             s.activation(
                                 out=bass.AP(s_owT, (4 * grp) * H + oc * 128,
                                             [[NCH * H, 128], [H, 4], [1, 128]]),
                                 in_=pt[:, :], func=AF.Identity), sm_a,
                             waits=[tt])
                p.run_block()

            with ExitStack() as ph:
                s_br = [ph.enter_context(
                    nc.sbuf_tensor(f"s_br{i}", [128, TB], F32))
                    for i in range(NGRP)]
                s_cr = [ph.enter_context(
                    nc.sbuf_tensor(f"s_cr{i}", [128, TB], BF16))
                    for i in range(NGRP)]
                s_dtc = [ph.enter_context(
                    nc.sbuf_tensor(f"s_dtc{i}", [128, TB], F32)) for i in range(2)]
                s_dxc = [ph.enter_context(
                    nc.sbuf_tensor(f"s_dxc{i}", [128, TB], F32)) for i in range(2)]
                s_al = [ph.enter_context(
                    nc.sbuf_tensor(f"s_al{i}", [128, TB], F32)) for i in range(2)]
                s_u = [ph.enter_context(
                    nc.sbuf_tensor(f"s_u{i}", [128, TB], F32)) for i in range(2)]
                s_sc = [ph.enter_context(
                    nc.sbuf_tensor(f"s_sc{i}", [128, TB], BF16)) for i in range(2)]
                s_tmp = [ph.enter_context(
                    nc.sbuf_tensor(f"s_tmp{i}", [128, TB], BF16)) for i in range(2)]
                s_y = [ph.enter_context(
                    nc.sbuf_tensor(f"s_y{i}", [128, TB], BF16)) for i in range(4)]
                s_gch = [ph.enter_context(
                    nc.sbuf_tensor(f"s_gch{i}", [128, TB], BF16)) for i in range(2)]
                s_ygT = ph.enter_context(
                    nc.sbuf_tensor("s_ygT", [128, NCH * TB], BF16))
                s_hres = [ph.enter_context(
                    nc.sbuf_tensor(f"s_hres{i}", [128, H], F32)) for i in range(2)]
                s_t2 = [ph.enter_context(
                    nc.sbuf_tensor(f"s_t2{i}", [128, H], F32)) for i in range(2)]
                s_x1 = [ph.enter_context(
                    nc.sbuf_tensor(f"s_x1{i}", [128, H], F32)) for i in range(2)]
                s_st2 = ph.enter_context(nc.sbuf_tensor("s_st2", [128, 16], F32))
                s_bcrow = ph.enter_context(
                    nc.sbuf_tensor("s_bcrow", [1, 2 * TB], F32))
                s_sq2 = ph.enter_context(nc.sbuf_tensor("s_sq2", [128, H], F32))
                p_bcb = ph.enter_context(nc.psum_tensor("p_bcb", [128, TB], F32))
                p_bcc = ph.enter_context(nc.psum_tensor("p_bcc", [128, TB], F32))
                p_op = [ph.enter_context(
                    nc.psum_tensor(f"p_op{i}", [128, H], F32)) for i in range(2)]

                br_evac = {}
                bcrow_used = {}
                br_used = {}
                cr_used = {}
                dt_used = [None, None]
                dtx_used = [None, None]
                al_used = [None, None]
                u_used = [None, None]
                sc_used = [None, None]
                tmp_used = [None, None]
                y_used = [None, None]
                g_used2 = [None, None]
                ygT_cons = None
                hres_used = [None, None]
                t2_used = [None, None]
                x1_used = [None, None]
                pop_used = [None, None]
                for bt in range(NT_B):
                    base = bt * TB
                    for ng in range(NGRP):
                        nlist = list(range(ng * NGRP, (ng + 1) * NGRP))
                        prev_evac = {0: br_evac.get((NGRP - 1, 0)),
                                     1: br_evac.get((NGRP - 1, 1))}
                        for i, n in enumerate(nlist):
                            for half, (s_dst, rowoff, pbc) in enumerate(
                                    [(s_br[i], n, p_bcb), (s_cr[i], 16 + n, p_bcc)]):
                                roff = ((2 * i + half) % 2) * TB
                                tstg = dma(
                                    p, "sync",
                                    bass.AP(s_bcrow, roff, [[2 * TB, 1], [1, TB]]),
                                    bass.AP(bcT_s, rowoff * L + base,
                                            [[L, 1], [1, TB]]),
                                    sm_ld, waits=[bcrow_used.get(roff)])
                                tmmb = None
                                for q in range(TB // 512):
                                    w = [tstg]
                                    if q == 0:
                                        w += [prev_evac[half],
                                              (br_used if half == 0 else
                                               cr_used).get(i)]
                                    tmmb = comp(
                                        p, "tensor",
                                        lambda t, roff=roff, q=q, pbc=pbc:
                                        t.matmul(
                                            pbc[:, q * 512:(q + 1) * 512],
                                            s_ones_f32[:1, :],
                                            bass.AP(s_bcrow, roff + q * 512,
                                                    [[2 * TB, 1], [1, 512]]),
                                            start=True, stop=True),
                                        sm_t if q == TB // 512 - 1 else None,
                                        waits=w)
                                bcrow_used[roff] = tmmb
                                ev = comp(
                                    p, "scalar",
                                    lambda s, s_dst=s_dst, pbc=pbc:
                                    s.activation(out=s_dst[:, :], in_=pbc[:, :],
                                                 func=AF.Identity), sm_a,
                                    waits=[tmmb])
                                br_evac[(i, half)] = ev
                                prev_evac[half] = ev
                        for ch in range(NCH):
                            sl = ch % 2
                            dtb = s_dtc[sl]
                            tok_dt = dma(p, "sync", dtb[:, :],
                                         bass.AP(dt_s, ch * 128 * L + base,
                                                 [[L, 128], [1, TB]]), sm_ld,
                                         waits=[dt_used[sl]])
                            dxb = s_dxc[sl]
                            tok_dx = dma(p, "sync", dxb[:, :],
                                         bass.AP(dtx_s, ch * 128 * L + base,
                                                 [[L, 128], [1, TB]]), sm_ld,
                                         waits=[dtx_used[sl]])
                            yb, yb2 = s_y[2 * sl], s_y[2 * sl + 1]
                            tok_y = dma(p, "sync", yb[:, :],
                                        bass.AP(ys_s, ch * 128 * L + base,
                                                [[L, 128], [1, TB]]), sm_ld,
                                        waits=[y_used[sl]])
                            ylast = tok_y
                            ycur, yoth = yb, yb2
                            pend = []
                            for i, n in enumerate(nlist):
                                isl = i % 2
                                ab = s_al[isl]
                                ta = comp(p, "scalar",
                                          lambda s, ab=ab, ch=ch, n=n, dtb=dtb:
                                          s.activation(
                                              out=ab[:, :], in_=dtb[:, :],
                                              func=AF.Exp,
                                              scale=s_acol[:, ch * NST + n:
                                                           ch * NST + n + 1]),
                                          sm_a, waits=[tok_dt, al_used[isl]])
                                ub = s_u[isl]
                                tu = comp(p, "gpsimd",
                                          lambda g, ub=ub, dxb=dxb, i=i:
                                          g.tensor_tensor(ub[:, :], dxb[:, :],
                                                          s_br[i][:, :],
                                                          ALU.mult), sm_q,
                                          waits=[tok_dx, br_evac[(i, 0)],
                                                 u_used[isl]])
                                scb = s_sc[isl]
                                cidx = ch * NST + n
                                tsc = comp(p, "vector",
                                           lambda v, scb=scb, ab=ab, ub=ub,
                                           cidx=cidx:
                                           v.tensor_tensor_scan(
                                               scb[:, :], ab[:, :], ub[:, :],
                                               s_carry[:, cidx:cidx + 1],
                                               ALU.mult, ALU.add), sm_v,
                                           waits=[ta, tu, sc_used[isl]])
                                al_used[isl] = tsc
                                u_used[isl] = tsc
                                comp(p, "vector",
                                     lambda v, scb=scb, cidx=cidx:
                                     v.tensor_copy(s_carry[:, cidx:cidx + 1],
                                                   scb[:, TB - 1:TB]), sm_v)
                                tmb = s_tmp[isl]
                                tcm = comp(p, "vector",
                                           lambda v, tmb=tmb, scb=scb, i=i:
                                           v.tensor_tensor(tmb[:, :], scb[:, :],
                                                           s_cr[i][:, :],
                                                           ALU.mult), sm_v,
                                           waits=[br_evac[(i, 1)],
                                                  tmp_used[isl]])
                                sc_used[isl] = tcm
                                br_used[i] = tcm
                                cr_used[i] = tcm
                                pend.append((tcm, tmb))
                                if len(pend) >= 2:
                                    _, tmb0 = pend.pop(0)
                                    ylast = comp(
                                        p, "vector",
                                        lambda v, yoth=yoth, ycur=ycur,
                                        tmb0=tmb0:
                                        v.tensor_add(yoth[:, :], ycur[:, :],
                                                     tmb0[:, :]), sm_v,
                                        waits=[ylast, tok_y])
                                    tmp_used[1 - isl] = ylast
                                    ycur, yoth = yoth, ycur
                            while pend:
                                _, tmb0 = pend.pop(0)
                                ylast = comp(
                                    p, "vector",
                                    lambda v, yoth=yoth, ycur=ycur, tmb0=tmb0:
                                    v.tensor_add(yoth[:, :], ycur[:, :],
                                                 tmb0[:, :]), sm_v,
                                    waits=[ylast, tok_y])
                                ycur, yoth = yoth, ycur
                            tmp_used[0] = ylast
                            tmp_used[1] = ylast
                            dt_used[sl] = p.cur(sm_a)
                            dtx_used[sl] = p.cur(sm_q)
                            if ng < NGRP - 1:
                                y_used[sl] = dma(
                                    p, "scalar",
                                    bass.AP(ys_s, ch * 128 * L + base,
                                            [[L, 128], [1, TB]]),
                                    ycur[:, :], sm_st, waits=[ylast])
                            else:
                                gb = s_gch[sl]
                                tok_g = dma(p, "sync", gb[:, :],
                                            bass.AP(g_s, ch * 128 * L + base,
                                                    [[L, 128], [1, TB]]),
                                            sm_ld, waits=[g_used2[sl]])
                                tyg = comp(p, "vector",
                                           lambda v, ycur=ycur, gb=gb, ch=ch:
                                           v.tensor_tensor(
                                               s_ygT[:, ch * TB:(ch + 1) * TB],
                                               ycur[:, :], gb[:, :], ALU.mult),
                                           sm_v,
                                           waits=[ylast, tok_g,
                                                  ygT_cons if ch == 0 else None])
                                g_used2[sl] = tyg
                                y_used[sl] = tyg
                    yg_ready = p.cur(sm_v)
                    for tb in range(TB // 128):
                        sl = tb % 2
                        po = p_op[sl]
                        hr = s_hres[sl]
                        tok_h = dma(p, "sync", hr[:, :],
                                    bass.AP(hid, (base + tb * 128) * H,
                                            [[H, 128], [1, H]]), sm_ld,
                                    waits=[hres_used[sl]])
                        tmm = None
                        for os_ in range(2):
                            for ch in range(NCH):
                                w = []
                                if ch == 0 and os_ == 0:
                                    w = [yg_ready, pop_used[sl]]
                                tmm = comp(
                                    p, "tensor",
                                    lambda t, tb=tb, os_=os_, ch=ch, po=po:
                                    t.matmul(
                                        po[:, os_ * 512:(os_ + 1) * 512],
                                        s_ygT[:, ch * TB + tb * 128:
                                              ch * TB + tb * 128 + 128],
                                        s_owT[:, ch * H + os_ * 512:
                                              ch * H + os_ * 512 + 512],
                                        start=(ch == 0), stop=False),
                                    None, waits=w)
                            tmm = comp(p, "tensor",
                                       lambda t, os_=os_, po=po: t.matmul(
                                           po[:, os_ * 512:(os_ + 1) * 512],
                                           s_ones_bf[:1, :],
                                           s_outb_row[:1, os_ * 512:
                                                      (os_ + 1) * 512],
                                           start=False, stop=True),
                                       sm_t if os_ == 1 else None)
                        t2b = s_t2[sl]
                        tt2 = comp(p, "vector", lambda v, t2b=t2b, hr=hr, po=po:
                                   v.scalar_tensor_tensor(
                                       t2b[:, :], hr[:, :], 2.0, po[:, :],
                                       ALU.mult, ALU.add), sm_v,
                                   waits=[tmm, tok_h, t2_used[sl]])
                        hres_used[sl] = tt2
                        pop_used[sl] = tt2
                        c0 = 8 * sl
                        tvs = comp(p, "vector", lambda v, t2b=t2b, c0=c0:
                                   v.reduce_sum(s_st2[:, c0:c0 + 1], t2b[:, :],
                                                axis=AX.X), sm_v, waits=[tt2])
                        tas = comp(p, "scalar", lambda s, t2b=t2b, c0=c0:
                                   s.activation(
                                       out=s_sq2[:, :], in_=t2b[:, :],
                                       func=AF.Square,
                                       accum_out=s_st2[:, c0 + 1:c0 + 2]),
                                   sm_a, waits=[tvs])
                        tv1 = comp(p, "vector", lambda v, c0=c0: v.tensor_scalar(
                            s_st2[:, c0:c0 + 1], s_st2[:, c0:c0 + 1],
                            1.0 / H, None, ALU.mult), sm_v, waits=[tas])
                        tv2 = comp(p, "vector", lambda v, c0=c0: v.tensor_tensor(
                            s_st2[:, c0 + 2:c0 + 3], s_st2[:, c0:c0 + 1],
                            s_st2[:, c0:c0 + 1], ALU.mult), sm_v, waits=[tv1])
                        tv3 = comp(p, "vector", lambda v, c0=c0:
                                   v.scalar_tensor_tensor(
                                       s_st2[:, c0 + 3:c0 + 4],
                                       s_st2[:, c0 + 1:c0 + 2], 1.0 / H,
                                       s_st2[:, c0 + 2:c0 + 3],
                                       ALU.mult, ALU.subtract), sm_v,
                                   waits=[tv2])
                        ta1 = comp(p, "scalar", lambda s, c0=c0: s.activation(
                            out=s_st2[:, c0 + 4:c0 + 5],
                            in_=s_st2[:, c0 + 3:c0 + 4], func=AF.Ln, bias=s_epscol[:, :]),
                            sm_a, waits=[tv3])
                        ta2 = comp(p, "scalar", lambda s, c0=c0: s.activation(
                            out=s_st2[:, c0 + 5:c0 + 6],
                            in_=s_st2[:, c0 + 4:c0 + 5], func=AF.Exp,
                            scale=s_mhalf[:, :]), sm_a, waits=[ta1])
                        tv4 = comp(p, "vector", lambda v, c0=c0: v.tensor_tensor(
                            s_st2[:, c0 + 6:c0 + 7], s_st2[:, c0:c0 + 1],
                            s_st2[:, c0 + 5:c0 + 6], ALU.mult), sm_v,
                            waits=[ta2])
                        x1b = s_x1[sl]
                        tx1 = comp(p, "vector", lambda v, x1b=x1b, t2b=t2b, c0=c0:
                                   v.tensor_scalar(
                                       x1b[:, :], t2b[:, :],
                                       s_st2[:, c0 + 5:c0 + 6],
                                       s_st2[:, c0 + 6:c0 + 7],
                                       ALU.mult, ALU.subtract), sm_v,
                                   waits=[tv4, x1_used[sl]])
                        t2_used[sl] = tx1
                        x1_used[sl] = dma(
                            p, "scalar",
                            bass.AP(x1_s, (base + tb * 128) * H,
                                    [[H, 128], [1, H]]),
                            x1b[:, :], sm_st, waits=[tx1])
                    ygT_cons = p.cur(sm_t)
                fwait(p, "scalar", [p.cur(sm_st)])
                fwait(p, "sync", [p.cur(sm_ld)])
                p.run_block()

        with ExitStack() as ph:
            NTB = TC // 128
            s_wdT = ph.enter_context(
                nc.sbuf_tensor("c_wdT", [128, NFF * H], BF16))
            s_xqT2 = [ph.enter_context(
                nc.sbuf_tensor(f"c_xqT{i}", [128, NKH * TC], BF16))
                for i in range(2)]
            s_xqT = s_xqT2[0]
            s_wg = [ph.enter_context(
                nc.sbuf_tensor("c_wg0", [128, NKH * 512], BF16))]
            s_wu = [ph.enter_context(
                nc.sbuf_tensor("c_wu0", [128, NKH * 512], BF16))]
            s_x1c = [ph.enter_context(
                nc.sbuf_tensor(f"c_x1{i}", [128, H], F32)) for i in range(2)]
            s_xq = ph.enter_context(nc.sbuf_tensor("c_xq", [128, H], BF16))
            s_tg = [ph.enter_context(
                nc.sbuf_tensor(f"c_tg{i}", [128, 512], F32)) for i in range(2)]
            s_hp = [ph.enter_context(
                nc.sbuf_tensor(f"c_hp{i}", [128, 512], F32)) for i in range(2)]
            s_hpb = [ph.enter_context(
                nc.sbuf_tensor(f"c_hpb{i}", [128, FF], F32)) for i in range(2)]
            s_hq = ph.enter_context(nc.sbuf_tensor("c_hq", [128, FF], BF16))
            s_hqT = [ph.enter_context(
                nc.sbuf_tensor(f"c_hqT{i}", [128, NFF * 128], BF16))
                for i in range(2)]
            s_rmx2 = [ph.enter_context(
                nc.sbuf_tensor(f"c_rmx{i}", [128, NTB * 8], F32))
                for i in range(2)]
            s_rmx = s_rmx2[0]
            s_fcols2 = [ph.enter_context(
                nc.sbuf_tensor(f"c_fc{i}", [128, NTB * 2], F32))
                for i in range(2)]
            s_fcols = s_fcols2[0]
            s_stc = ph.enter_context(nc.sbuf_tensor("c_st", [128, 32], F32))
            s_sq3 = ph.enter_context(nc.sbuf_tensor("c_sq3", [128, H], F32))
            s_ot = [ph.enter_context(
                nc.sbuf_tensor(f"c_ot{i}", [128, H], F32)) for i in range(2)]
            p_g = [ph.enter_context(
                nc.psum_tensor(f"c_pg{i}", [128, 512], F32)) for i in range(2)]
            p_u2 = [ph.enter_context(
                nc.psum_tensor(f"c_pu{i}", [128, 512], F32)) for i in range(2)]
            p_t3 = [ph.enter_context(
                nc.psum_tensor(f"c_pt{i}", [128, 512], BF16)) for i in range(2)]
            p_d = ph.enter_context(nc.psum_tensor("c_pd", [128, H], F32))

            wd_tok = dma(p, "sync",
                         bass.AP(s_wdT, 0, [[NFF * H, 128], [H, NFF], [1, H]]),
                         bass.AP(wdT_s, 0, [[H, 128], [128 * H, NFF], [1, H]]),
                         sm_ld)

            x1_used3 = [None, None]
            xq_free = [None, None]
            tg_used = [None, None]
            hp_ps = [None, None]
            hp_st = [None, None]
            hpb_used = [None, None]
            hq_free = None
            hqT_used = [None, None]
            ot_used = [None, None]
            pd_used = None
            wg_used = [None, None]
            wu_used = [None, None]
            for sc_i in range(NT_C):
                sbase = sc_i * TC
                spar = sc_i % 2
                s_xqT = s_xqT2[spar]
                s_rmx = s_rmx2[spar]
                s_fcols = s_fcols2[spar]
                hp_off = spar * TC * FF
                for tb in range(NTB):
                    sl = tb % 2
                    c0 = 16 + 8 * sl
                    xb = s_x1c[sl]
                    tokx = dma(p, "sync", xb[:, :],
                               bass.AP(x1_s, (sbase + tb * 128) * H,
                                       [[H, 128], [1, H]]), sm_ld,
                               waits=[x1_used3[sl]])
                    trm = comp(p, "vector", lambda v, xb=xb, c0=c0: v.reduce_max(
                        s_stc[:, c0:c0 + 1], xb[:, :], axis=AX.X,
                        apply_absolute_value=True), sm_v, waits=[tokx])
                    tm2 = comp(p, "vector", lambda v, c0=c0: v.tensor_scalar(
                        s_stc[:, c0 + 1:c0 + 2], s_stc[:, c0:c0 + 1], 1e-5,
                        None, ALU.max), sm_v, waits=[trm])
                    trc = comp(p, "vector", lambda v, c0=c0: v.reciprocal(
                        s_stc[:, c0 + 2:c0 + 3], s_stc[:, c0 + 1:c0 + 2]),
                        sm_v, waits=[tm2])
                    tsc2 = comp(p, "vector", lambda v, c0=c0: v.tensor_scalar(
                        s_stc[:, c0 + 3:c0 + 4], s_stc[:, c0 + 2:c0 + 3],
                        127.0, None, ALU.mult), sm_v, waits=[trc])
                    tf1 = comp(p, "vector", lambda v, tb=tb, c0=c0,
                               s_fcols=s_fcols:
                               v.scalar_tensor_tensor(
                                   s_fcols[:, 2 * tb:2 * tb + 1],
                                   s_mw[:, 0:1], 1.0 / 127.0,
                                   s_stc[:, c0 + 1:c0 + 2],
                                   ALU.mult, ALU.mult), sm_v, waits=[tsc2])
                    tf2 = comp(p, "vector", lambda v, tb=tb, c0=c0,
                               s_fcols=s_fcols:
                               v.scalar_tensor_tensor(
                                   s_fcols[:, 2 * tb + 1:2 * tb + 2],
                                   s_mw[:, 1:2], 1.0 / 127.0,
                                   s_stc[:, c0 + 1:c0 + 2],
                                   ALU.mult, ALU.mult), sm_v, waits=[tf1])
                    tq1 = comp(p, "vector", lambda v, xb=xb, c0=c0:
                               v.tensor_scalar(
                                   xb[:, :], xb[:, :],
                                   s_stc[:, c0 + 3:c0 + 4], C_RND,
                                   ALU.mult, ALU.add), sm_v,
                               waits=[tf2] +
                               ([hq_free, p.cur(sm_a)] if tb == 0 else []))
                    tq2 = comp(p, "vector", lambda v, xb=xb: v.tensor_scalar(
                        xb[:, :], xb[:, :], C_RND, -128.0,
                        ALU.subtract, ALU.max), sm_v, waits=[tq1])
                    tq3 = comp(p, "vector", lambda v, xb=xb: v.tensor_scalar(
                        s_xq[:, :], xb[:, :], 127.0, None, ALU.min),
                        sm_v, waits=[tq2])
                    x1_used3[sl] = tq3
                    for grp in range(2):
                        pt = p_t3[grp]
                        tt = None
                        for b4 in range(4):
                            bd = grp * 4 + b4
                            w = []
                            if b4 == 0:
                                w = [tq3, p.cur(sm_a)]
                                if tb == 0 and grp == 0:
                                    w.append(xq_free[spar])
                            tt = comp(p, "tensor",
                                      lambda t, bd=bd, b4=b4, pt=pt: t.transpose(
                                          pt[:, b4 * 128:(b4 + 1) * 128],
                                          s_xq[:, bd * 128:(bd + 1) * 128],
                                          s_id[:, :]),
                                      sm_t if b4 == 3 else None, waits=w)
                        comp(p, "scalar", lambda s, grp=grp, tb=tb, pt=pt,
                             s_xqT=s_xqT:
                             s.activation(
                                 out=bass.AP(s_xqT, (4 * grp) * TC + tb * 128,
                                             [[NKH * TC, 128], [TC, 4],
                                              [1, 128]]),
                                 in_=pt[:, :], func=AF.Identity), sm_a,
                             waits=[tt])
                xq_ready = p.cur(sm_a)
                for os_ in range(8):
                    wsl = 0
                    wgb = s_wg[wsl]
                    wub = s_wu[wsl]
                    tokwg = dma(p, "sync", wgb[:, :],
                                bass.AP(wgT_s, os_ * 512,
                                        [[FF, 128], [128 * FF, NKH], [1, 512]]),
                                sm_ld, waits=[wg_used[wsl]])
                    tokwu = dma(p, "sync", wub[:, :],
                                bass.AP(wuT_s, os_ * 512,
                                        [[FF, 128], [128 * FF, NKH], [1, 512]]),
                                sm_ld, waits=[wu_used[wsl]])
                    for tb in range(NTB):
                        sl = tb % 2
                        pg = p_g[sl]
                        pu = p_u2[sl]
                        tmm = None
                        for ch in range(NKH):
                            w = []
                            if ch == 0:
                                w = [xq_ready, tokwg, tokwu, hp_ps[sl]]
                            tmm = comp(p, "tensor",
                                       lambda t, ch=ch, tb=tb, pg=pg, wgb=wgb,
                                       s_xqT=s_xqT:
                                       t.matmul(
                                           pg[:, :],
                                           s_xqT[:, ch * TC + tb * 128:
                                                 ch * TC + tb * 128 + 128],
                                           wgb[:, ch * 512:(ch + 1) * 512],
                                           start=(ch == 0),
                                           stop=(ch == NKH - 1)),
                                       None, waits=w)
                            tmm = comp(p, "tensor",
                                       lambda t, ch=ch, tb=tb, pu=pu, wub=wub,
                                       s_xqT=s_xqT:
                                       t.matmul(
                                           pu[:, :],
                                           s_xqT[:, ch * TC + tb * 128:
                                                 ch * TC + tb * 128 + 128],
                                           wub[:, ch * 512:(ch + 1) * 512],
                                           start=(ch == 0),
                                           stop=(ch == NKH - 1)),
                                       sm_t if ch == NKH - 1 else None)
                        wg_used[wsl] = tmm
                        wu_used[wsl] = tmm
                        tgb = s_tg[sl]
                        ttg = comp(p, "scalar", lambda s, tgb=tgb, pg=pg, tb=tb,
                                   s_fcols=s_fcols:
                                   s.activation(
                                       out=tgb[:, :], in_=pg[:, :],
                                       func=AF.Silu,
                                       scale=s_fcols[:, 2 * tb:2 * tb + 1]),
                                   sm_a, waits=[tmm, tg_used[sl]])
                        hpb = s_hp[sl]
                        thp = comp(p, "vector",
                                   lambda v, hpb=hpb, tgb=tgb, pu=pu:
                                   v.tensor_tensor(hpb[:, :], tgb[:, :],
                                                   pu[:, :], ALU.mult), sm_v,
                                   waits=[ttg, hp_st[sl]])
                        tg_used[sl] = thp
                        hp_ps[sl] = thp
                        trm2 = comp(p, "vector",
                                    lambda v, hpb=hpb, tb=tb, os_=os_,
                                    s_rmx=s_rmx:
                                    v.reduce_max(
                                        s_rmx[:, tb * 8 + os_:
                                              tb * 8 + os_ + 1],
                                        hpb[:, :], axis=AX.X,
                                        apply_absolute_value=True), sm_v,
                                    waits=[thp])
                        hp_st[sl] = dma(
                            p, "scalar",
                            bass.AP(hp_s, hp_off + (tb * 128) * FF + os_ * 512,
                                    [[FF, 128], [1, 512]]),
                            hpb[:, :], sm_st, waits=[trm2])
                hp_done = p.cur(sm_st)
                for tb in range(NTB):
                    sl = tb % 2
                    c0 = 8 * sl
                    hb = s_hpb[sl]
                    tokh = dma(p, "sync", hb[:, :],
                               bass.AP(hp_s, hp_off + (tb * 128) * FF,
                                       [[FF, 128], [1, FF]]), sm_ld,
                               waits=[hp_done if tb == 0 else None,
                                      hpb_used[sl]])
                    xb = s_x1c[sl]
                    tokx = dma(p, "sync", xb[:, :],
                               bass.AP(x1_s, (sbase + tb * 128) * H,
                                       [[H, 128], [1, H]]), sm_ld,
                               waits=[x1_used3[sl]])
                    trm = comp(p, "vector", lambda v, tb=tb, c0=c0,
                               s_rmx=s_rmx: v.reduce_max(
                        s_stc[:, c0:c0 + 1], s_rmx[:, tb * 8:(tb + 1) * 8],
                        axis=AX.X), sm_v)
                    tt1 = comp(p, "vector", lambda v, tb=tb, c0=c0,
                               s_fcols=s_fcols:
                               v.tensor_tensor(
                                   s_stc[:, c0 + 1:c0 + 2],
                                   s_fcols[:, 2 * tb + 1:2 * tb + 2],
                                   s_stc[:, c0:c0 + 1], ALU.mult), sm_v,
                               waits=[trm])
                    tt2 = comp(p, "vector", lambda v, c0=c0: v.tensor_scalar(
                        s_stc[:, c0 + 2:c0 + 3], s_stc[:, c0 + 1:c0 + 2],
                        1e-5, None, ALU.max), sm_v, waits=[tt1])
                    tt3 = comp(p, "vector", lambda v, c0=c0: v.reciprocal(
                        s_stc[:, c0 + 3:c0 + 4], s_stc[:, c0 + 2:c0 + 3]),
                        sm_v, waits=[tt2])
                    tt4 = comp(p, "vector", lambda v, tb=tb, c0=c0,
                               s_fcols=s_fcols:
                               v.scalar_tensor_tensor(
                                   s_stc[:, c0 + 4:c0 + 5],
                                   s_fcols[:, 2 * tb + 1:2 * tb + 2], 127.0,
                                   s_stc[:, c0 + 3:c0 + 4],
                                   ALU.mult, ALU.mult), sm_v, waits=[tt3])
                    tt5 = comp(p, "vector", lambda v, c0=c0:
                               v.scalar_tensor_tensor(
                                   s_stc[:, c0 + 5:c0 + 6],
                                   s_stc[:, c0 + 2:c0 + 3], 1.0 / 127.0,
                                   s_mw[:, 2:3], ALU.mult, ALU.mult), sm_v,
                               waits=[tt4])
                    tq1 = comp(p, "vector", lambda v, hb=hb, c0=c0:
                               v.tensor_scalar(
                                   hb[:, :], hb[:, :],
                                   s_stc[:, c0 + 4:c0 + 5], C_RND,
                                   ALU.mult, ALU.add), sm_v,
                               waits=[tokh, tt5, hq_free])
                    tq2 = comp(p, "vector", lambda v, hb=hb: v.tensor_scalar(
                        hb[:, :], hb[:, :], C_RND, -128.0,
                        ALU.subtract, ALU.max), sm_v, waits=[tq1])
                    tq3 = comp(p, "vector", lambda v, hb=hb: v.tensor_scalar(
                        s_hq[:, :], hb[:, :], 127.0, None, ALU.min), sm_v,
                        waits=[tq2])
                    hpb_used[sl] = tq3
                    hqT = s_hqT[sl]
                    tev = None
                    for grp in range(NFF // 4):
                        pt = p_t3[grp % 2]
                        tt = None
                        for b4 in range(4):
                            bd = grp * 4 + b4
                            w = []
                            if b4 == 0:
                                w = [tq3, p.cur(sm_a)]
                                if grp == 0:
                                    w.append(hqT_used[sl])
                            tt = comp(p, "tensor",
                                      lambda t, bd=bd, b4=b4, pt=pt: t.transpose(
                                          pt[:, b4 * 128:(b4 + 1) * 128],
                                          s_hq[:, bd * 128:(bd + 1) * 128],
                                          s_id[:, :]),
                                      sm_t if b4 == 3 else None, waits=w)
                        tev = comp(p, "scalar",
                                   lambda s, grp=grp, hqT=hqT, pt=pt:
                                   s.activation(
                                       out=hqT[:, grp * 512:(grp + 1) * 512],
                                       in_=pt[:, :], func=AF.Identity), sm_a,
                                   waits=[tt])
                    hq_free = p.cur(sm_t)
                    tmm = None
                    for ch in range(NFF):
                        for os2 in range(2):
                            w = []
                            if ch == 0 and os2 == 0:
                                w = [tev, wd_tok, pd_used]
                            tmm = comp(p, "tensor",
                                       lambda t, ch=ch, os2=os2, hqT=hqT:
                                       t.matmul(
                                           p_d[:, os2 * 512:(os2 + 1) * 512],
                                           hqT[:, ch * 128:(ch + 1) * 128],
                                           s_wdT[:, ch * H + os2 * 512:
                                                 ch * H + os2 * 512 + 512],
                                           start=(ch == 0),
                                           stop=(ch == NFF - 1)),
                                       sm_t if (ch == NFF - 1 and os2 == 1)
                                       else None, waits=w)
                    hqT_used[sl] = tmm
                    otb = s_ot[sl]
                    tx2 = comp(p, "vector", lambda v, otb=otb, xb=xb, c0=c0:
                               v.scalar_tensor_tensor(
                                   otb[:, :], p_d[:, :],
                                   s_stc[:, c0 + 5:c0 + 6], xb[:, :],
                                   ALU.mult, ALU.add), sm_v,
                               waits=[tmm, tokx, ot_used[sl]])
                    pd_used = tx2
                    x1_used3[sl] = tx2
                    tvs = comp(p, "vector", lambda v, otb=otb, c0=c0:
                               v.reduce_sum(s_stc[:, c0 + 6:c0 + 7],
                                            otb[:, :], axis=AX.X), sm_v,
                               waits=[tx2])
                    tas = comp(p, "scalar", lambda s, otb=otb, c0=c0:
                               s.activation(
                                   out=s_sq3[:, :], in_=otb[:, :],
                                   func=AF.Square,
                                   accum_out=s_stc[:, c0 + 7:c0 + 8]), sm_a,
                               waits=[tvs, x1_used3[1 - sl] if False else None])
                    tv1 = comp(p, "vector", lambda v, c0=c0: v.tensor_scalar(
                        s_stc[:, c0 + 6:c0 + 7], s_stc[:, c0 + 6:c0 + 7],
                        1.0 / H, None, ALU.mult), sm_v, waits=[tas])
                    tv2 = comp(p, "vector", lambda v, c0=c0: v.tensor_tensor(
                        s_stc[:, c0:c0 + 1], s_stc[:, c0 + 6:c0 + 7],
                        s_stc[:, c0 + 6:c0 + 7], ALU.mult), sm_v, waits=[tv1])
                    tv3 = comp(p, "vector", lambda v, c0=c0:
                               v.scalar_tensor_tensor(
                                   s_stc[:, c0 + 1:c0 + 2],
                                   s_stc[:, c0 + 7:c0 + 8], 1.0 / H,
                                   s_stc[:, c0:c0 + 1],
                                   ALU.mult, ALU.subtract), sm_v, waits=[tv2])
                    ta1 = comp(p, "scalar", lambda s, c0=c0: s.activation(
                        out=s_stc[:, c0 + 2:c0 + 3],
                        in_=s_stc[:, c0 + 1:c0 + 2], func=AF.Ln, bias=s_epscol[:, :]),
                        sm_a, waits=[tv3])
                    ta2 = comp(p, "scalar", lambda s, c0=c0: s.activation(
                        out=s_stc[:, c0 + 3:c0 + 4],
                        in_=s_stc[:, c0 + 2:c0 + 3], func=AF.Exp, scale=s_mhalf[:, :]),
                        sm_a, waits=[ta1])
                    tv4 = comp(p, "vector", lambda v, c0=c0: v.tensor_tensor(
                        s_stc[:, c0 + 4:c0 + 5], s_stc[:, c0 + 6:c0 + 7],
                        s_stc[:, c0 + 3:c0 + 4], ALU.mult), sm_v, waits=[ta2])
                    txo = comp(p, "vector", lambda v, otb=otb, c0=c0:
                               v.tensor_scalar(
                                   otb[:, :], otb[:, :],
                                   s_stc[:, c0 + 3:c0 + 4],
                                   s_stc[:, c0 + 4:c0 + 5],
                                   ALU.mult, ALU.subtract), sm_v, waits=[tv4])
                    ot_used[sl] = dma(
                        p, "scalar",
                        bass.AP(out, (sbase + tb * 128) * H, [[H, 128], [1, H]]),
                        otb[:, :], sm_o, waits=[txo])
                xq_free[spar] = p.cur(sm_t)
            fwait(p, "gpsimd", [p.cur(sm_o)])
            p.run_block()

    return nc


_NC_CACHE = {}


def kernel(**inputs):
    if 8192 not in _NC_CACHE:
        _NC_CACHE[8192] = build_program(8192)
    nc = _NC_CACHE[8192]
    hidden = np.ascontiguousarray(np.asarray(inputs["L0_attn_hidden"], np.float32))
    names = {
        "ln_w": "L0_attn_ln_w", "ln_b": "L0_attn_ln_b", "in_w": "L0_attn_in_w",
        "in_b": "L0_attn_in_b", "conv_w": "L0_attn_conv_w",
        "conv_b": "L0_attn_conv_b", "xp_w": "L0_attn_xp_w",
        "dtp_w": "L0_attn_dtp_w", "dtp_b": "L0_attn_dtp_b",
        "A_log": "L0_attn_A_log", "D": "L0_attn_D", "out_w": "L0_attn_out_w",
        "out_b": "L0_attn_out_b", "w_gate": "L0_ffn_w_gate",
        "w_up": "L0_ffn_w_up", "w_down": "L0_ffn_w_down",
    }
    common = {k: np.ascontiguousarray(np.asarray(inputs[v], np.float32))
              for k, v in names.items()}
    in_maps = []
    for b in range(hidden.shape[0]):
        m = dict(common)
        m["hidden"] = np.ascontiguousarray(hidden[b])
        in_maps.append(m)
    res = run_bass_kernel_spmd(nc, in_maps, core_ids=list(range(len(in_maps))))
    return np.stack([np.asarray(r["out"], np.float32) for r in res.results])


# version 06
# speedup vs baseline: 1.0264x; 1.0264x over previous
import numpy as np
from contextlib import ExitStack

import concourse.bass as bass
import concourse.mybir as mybir
from concourse.bass_utils import run_bass_kernel_spmd

F32 = mybir.dt.float32
BF16 = mybir.dt.bfloat16
AF = mybir.ActivationFunctionType
ALU = mybir.AluOpType
AX = mybir.AxisListType

H = 1024
IN = 2048
FF = 4096
NST = 16
RK = 64
KC = 4
NCH = IN // 128
NKH = H // 128
NFF = FF // 128
C_RND = 12582912.0
EPS = 1e-5

TA = 512
TB = 1024
TC = 1024
NGRP = 4


class P:
    def __init__(self, nc):
        self.nc = nc
        self.order = ("sync", "gpsimd", "vector", "scalar", "tensor")
        self.ops = {k: [] for k in self.order}
        self.cnt = {}
        self.eng_of = {}

    def sem(self, ctx, name, engine=None):
        s = ctx.enter_context(self.nc.semaphore(name))
        self.cnt[id(s)] = 0
        if engine:
            self.eng_of[id(s)] = engine
        return s

    def emit(self, engine, fn):
        self.ops[engine].append(fn)

    def tok(self, sem, amt):
        self.cnt[id(sem)] += amt
        return (sem, self.cnt[id(sem)])

    def cur(self, sem):
        return (sem, self.cnt[id(sem)])

    def run_block(self):
        with self.nc.Block() as block:
            for eng_name in self.order:
                ops = self.ops[eng_name]
                if not ops:
                    continue

                def body(eng, ops=list(ops)):
                    for fn in ops:
                        fn(eng)

                getattr(block, eng_name)(body)
        self.ops = {k: [] for k in self.order}


def dma(p, queue, out_ap, in_ap, sem, waits=(), slow=False):
    waits = [w for w in waits if w is not None]
    tok = p.tok(sem, 16)

    def fn(eng, out_ap=out_ap, in_ap=in_ap, waits=tuple(waits)):
        for (ws, wv) in waits:
            eng.wait_ge(ws, wv)
        kw = {"allow_slow_non_contiguous": True} if slow else {}
        eng.dma_start(out=out_ap, in_=in_ap, **kw).then_inc(tok[0], 16)

    p.emit(queue, fn)
    return tok


FILTER_ENGINES = ()


def comp(p, engine, build, sem=None, waits=(), fe=False):
    waits = [w for w in waits
             if w is not None and not (fe and
                                       p.eng_of.get(id(w[0])) == engine)]
    tok = p.tok(sem, 1) if sem is not None else None

    def fn(eng, waits=tuple(waits)):
        for (ws, wv) in waits:
            eng.wait_ge(ws, wv)
        inst = build(eng)
        if tok is not None and inst is not None:
            inst.then_inc(tok[0], 1)

    p.emit(engine, fn)
    return tok


def fwait(p, engine, waits):
    def fn(eng, waits=tuple(waits)):
        for (ws, wv) in waits:
            eng.wait_ge(ws, wv)

    p.emit(engine, fn)


def vgather(t, ncols):
    return bass.AP(t, 0, [[1, 128], [128, ncols]])


def build_program(L=8192, debug=False):
    nc = bass.Bass("TRN2", target_bir_lowering=False, debug=False)
    NT_A = L // TA
    NT_B = L // TB
    NT_C = L // TC

    hid = nc.dram_tensor("hidden", [L, H], F32, kind="ExternalInput")
    ln_w = nc.dram_tensor("ln_w", [H], F32, kind="ExternalInput")
    ln_b = nc.dram_tensor("ln_b", [H], F32, kind="ExternalInput")
    in_w = nc.dram_tensor("in_w", [2 * IN, H], F32, kind="ExternalInput")
    in_b = nc.dram_tensor("in_b", [2 * IN], F32, kind="ExternalInput")
    conv_w = nc.dram_tensor("conv_w", [IN, 1, KC], F32, kind="ExternalInput")
    conv_b = nc.dram_tensor("conv_b", [IN], F32, kind="ExternalInput")
    xp_w = nc.dram_tensor("xp_w", [RK + 2 * NST, IN], F32, kind="ExternalInput")
    dtp_w = nc.dram_tensor("dtp_w", [IN, RK], F32, kind="ExternalInput")
    dtp_b = nc.dram_tensor("dtp_b", [IN], F32, kind="ExternalInput")
    A_log = nc.dram_tensor("A_log", [IN, NST], F32, kind="ExternalInput")
    D_vec = nc.dram_tensor("D", [IN], F32, kind="ExternalInput")
    out_w = nc.dram_tensor("out_w", [H, IN], F32, kind="ExternalInput")
    out_b = nc.dram_tensor("out_b", [H], F32, kind="ExternalInput")
    w_gate = nc.dram_tensor("w_gate", [FF, H], F32, kind="ExternalInput")
    w_up = nc.dram_tensor("w_up", [FF, H], F32, kind="ExternalInput")
    w_down = nc.dram_tensor("w_down", [H, FF], F32, kind="ExternalInput")
    out = nc.dram_tensor("out", [L, H], F32, kind="ExternalOutput")

    dk = "ExternalOutput" if debug else "Internal"
    dt_s = nc.dram_tensor("dt_s", [NCH, 128, L], F32, kind=dk)
    dtx_s = nc.dram_tensor("dtx_s", [NCH, 128, L], F32, kind=dk)
    ys_s = nc.dram_tensor("ys_s", [NCH, 128, L], BF16, kind=dk)
    g_s = nc.dram_tensor("g_s", [NCH, 128, L], BF16, kind=dk)
    x1_s = nc.dram_tensor("x1_s", [L, H], F32, kind=dk)
    hp_s = nc.dram_tensor("hp_s", [2, TC, FF], F32, kind=dk)
    bcT_s = nc.dram_tensor("bcT_s", [32, L], F32, kind=dk)
    wgT_s = nc.dram_tensor("wgT_s", [NKH, 128, FF], BF16, kind=dk)
    if debug:
        inwT_d = nc.dram_tensor("inwT_d", [128, NKH * 2 * IN], BF16,
                                kind="ExternalOutput")
        btld_d = nc.dram_tensor("btld_d", [128, 32], F32, kind="ExternalOutput")
        mw_d = nc.dram_tensor("mw_d", [128, 3], F32, kind="ExternalOutput")
    wuT_s = nc.dram_tensor("wuT_s", [NKH, 128, FF], BF16, kind=dk)
    wdT_s = nc.dram_tensor("wdT_s", [NFF, 128, H], BF16, kind=dk)

    with ExitStack() as per:
        def sbp(name, shape, dt=F32):
            return per.enter_context(nc.sbuf_tensor(name, shape, dt))

        s_id = sbp("s_id", [128, 128], BF16)
        s_lnw = sbp("s_lnw", [128, NKH])
        s_lnb_bf = sbp("s_lnb_bf", [128, NKH], BF16)
        s_inb = sbp("s_inb", [128, 32])
        s_btld = sbp("s_btld", [128, 32])
        s_cw = sbp("s_cw", [128, NCH * KC])
        s_cb = sbp("s_cb", [128, NCH])
        s_dtpb = sbp("s_dtpb", [128, NCH])
        s_dcol = sbp("s_dcol", [128, NCH])
        s_acol = sbp("s_acol", [128, NCH * NST])
        s_mw = sbp("s_mw", [128, 3])
        s_outb_row = sbp("s_outb_row", [1, H], BF16)
        s_ones_bf = sbp("s_ones_bf", [1, 128], BF16)
        s_ones_f32 = sbp("s_ones_f32", [1, 128])
        s_onecol = sbp("s_onecol", [128, 1])
        s_epscol = sbp("s_epscol", [128, 1])
        s_mhalf = sbp("s_mhalf", [128, 1])
        s_carry = sbp("s_carry", [128, NCH * NST])
        s_halo = sbp("s_halo", [128, NCH * (KC - 1)])

        p = P(nc)
        sm_ld = p.sem(per, "ld")
        sm_st = p.sem(per, "st")
        sm_v = p.sem(per, "v", engine="vector")
        sm_a = p.sem(per, "a", engine="scalar")
        sm_t = p.sem(per, "t", engine="tensor")
        sm_q = p.sem(per, "q", engine="gpsimd")
        sm_o = p.sem(per, "o")

        with ExitStack() as ph:
            t_iop = ph.enter_context(nc.sbuf_tensor("t_iop", [128, 128], F32))
            t_iof = ph.enter_context(nc.sbuf_tensor("t_iof", [128, 128], F32))
            t_al = ph.enter_context(nc.sbuf_tensor("t_al", [128, NST], F32))
            t_lnb = ph.enter_context(nc.sbuf_tensor("t_lnb", [128, NKH], F32))
            t_ob = ph.enter_context(nc.sbuf_tensor("t_ob", [1, H], F32))

            def g_init(g):
                g.iota(t_iop[:, :], [[0, 128]], channel_multiplier=1,
                       allow_small_or_imprecise_dtypes=True)
                g.iota(t_iof[:, :], [[1, 128]], channel_multiplier=0,
                       allow_small_or_imprecise_dtypes=True)
                g.memset(s_ones_bf[:, :], 1.0)
                g.memset(s_ones_f32[:, :], 1.0)
                g.memset(s_onecol[:, :], 1.0)
                g.memset(s_epscol[:, :], 1e-5)
                g.memset(s_mhalf[:, :], -0.5)
                g.memset(s_carry[:, :], 0.0)
                return g.memset(s_halo[:, :], 0.0)

            tg = comp(p, "gpsimd", g_init, sm_q)
            tl2 = dma(p, "sync", t_lnb[:, :], vgather(ln_b, NKH), sm_ld, slow=True)
            dma(p, "sync", s_lnw[:, :], vgather(ln_w, NKH), sm_ld, slow=True)
            dma(p, "sync", s_inb[:, :], vgather(in_b, 32), sm_ld, slow=True)
            dma(p, "sync", s_cb[:, :], vgather(conv_b, NCH), sm_ld, slow=True)
            dma(p, "sync", s_dtpb[:, :], vgather(dtp_b, NCH), sm_ld, slow=True)
            dma(p, "sync", s_dcol[:, :], vgather(D_vec, NCH), sm_ld, slow=True)
            dma(p, "sync", s_cw[:, :],
                bass.AP(conv_w, 0, [[KC, 128], [KC * 128, NCH], [1, KC]]),
                sm_ld, slow=True)
            tob = dma(p, "sync", t_ob[:, :],
                      bass.AP(out_b, 0, [[H, 1], [1, H]]), sm_ld)
            comp(p, "vector", lambda v: v.tensor_tensor(
                s_id[:, :], t_iop[:, :], t_iof[:, :], ALU.is_equal), sm_v,
                waits=[tg])
            comp(p, "vector", lambda v: v.tensor_copy(
                s_lnb_bf[:, :], t_lnb[:, :]), sm_v, waits=[tl2])
            comp(p, "vector", lambda v: v.tensor_copy(
                s_outb_row[:, :], t_ob[:, :]), sm_v, waits=[tob])
            for ch in range(NCH):
                tok = dma(p, "sync", t_al[:, :],
                          bass.AP(A_log, ch * 128 * NST, [[NST, 128], [1, NST]]),
                          sm_ld, waits=[p.cur(sm_a)] if ch else [])
                ta = comp(p, "scalar", lambda s, ch=ch: s.activation(
                    out=s_acol[:, ch * NST:(ch + 1) * NST], in_=t_al[:, :],
                    func=AF.Exp), sm_a, waits=[tok])
                comp(p, "vector", lambda v, ch=ch: v.tensor_scalar(
                    s_acol[:, ch * NST:(ch + 1) * NST],
                    s_acol[:, ch * NST:(ch + 1) * NST], -1.0, None, ALU.mult),
                    sm_v, waits=[ta])
            p.run_block()

        with ExitStack() as phA:
            s_inwT = phA.enter_context(
                nc.sbuf_tensor("s_inwT", [128, NKH * 2 * IN], BF16))

            with ExitStack() as ph:
                t_w = [ph.enter_context(nc.sbuf_tensor(f"t_w{i}", [128, FF], F32))
                       for i in range(2)]
                t_wq = ph.enter_context(nc.sbuf_tensor("t_wq", [128, FF], BF16))
                t_red = [ph.enter_context(nc.sbuf_tensor(f"t_red{i}", [128, 1], F32))
                         for i in range(3)]
                t_sc = ph.enter_context(nc.sbuf_tensor("t_sc", [1, 4], F32))
                t_scb = ph.enter_context(nc.sbuf_tensor("t_scb", [128, 1], F32))
                t_ev = [ph.enter_context(nc.sbuf_tensor(f"t_ev{i}", [128, 512], BF16))
                        for i in range(2)]
                p_tr = [ph.enter_context(nc.psum_tensor(f"p_tr{i}", [128, 512], BF16))
                        for i in range(2)]
                p_sc = ph.enter_context(nc.psum_tensor("p_sc", [128, 4], F32))

                def quant_matrix(w_t, rows, cols, wT_s, mw_idx):
                    nchr = rows // 128
                    for cch in range(nchr):
                        buf = t_w[cch % 2]
                        tok = dma(p, "sync", buf[:, :cols],
                                  bass.AP(w_t, cch * 128 * cols,
                                          [[cols, 128], [1, cols]]), sm_ld,
                                  waits=[p.cur(sm_v), p.cur(sm_t)]
                                  if cch < 2 else [p.cur(sm_v)])
                        red = t_red[cch % 2]
                        tr = comp(p, "vector", lambda v, buf=buf, red=red, cols=cols:
                                  v.reduce_sum(red[:, :], buf[:, :cols], axis=AX.X,
                                               apply_absolute_value=True),
                                  sm_v, waits=[tok])
                        if cch == 0:
                            comp(p, "vector", lambda v, red=red: v.tensor_copy(
                                t_red[2][:, :], red[:, :]), sm_v,
                                waits=[tr, p.cur(sm_t)])
                        else:
                            comp(p, "vector", lambda v, red=red: v.tensor_add(
                                t_red[2][:, :], t_red[2][:, :], red[:, :]), sm_v,
                                waits=[tr])
                    tm = comp(p, "tensor", lambda t: t.matmul(
                        p_sc[:1, 0:1], t_red[2][:, :], s_onecol[:, :],
                        start=True, stop=True), sm_t, waits=[p.cur(sm_v)])
                    tv1 = comp(p, "vector", lambda v, rows=rows, cols=cols:
                               v.tensor_scalar(
                                   t_sc[:1, 0:1], p_sc[:1, 0:1],
                                   1.0 / (rows * cols), 1e-5, ALU.mult, ALU.max),
                               sm_v, waits=[tm])
                    tv2 = comp(p, "vector", lambda v: v.reciprocal(
                        t_sc[:1, 1:2], t_sc[:1, 0:1]), sm_v, waits=[tv1])
                    tb = comp(p, "tensor", lambda t: t.matmul(
                        p_sc[:, 0:2], s_ones_f32[:1, :], t_sc[:1, 0:2],
                        start=True, stop=True), sm_t, waits=[tv2])
                    tvb = comp(p, "vector", lambda v: v.tensor_copy(
                        t_scb[:, :], p_sc[:, 1:2]), sm_v, waits=[tb])
                    comp(p, "vector", lambda v, mw_idx=mw_idx: v.tensor_copy(
                        s_mw[:, mw_idx:mw_idx + 1], p_sc[:, 0:1]), sm_v)
                    for cch in range(nchr):
                        buf = t_w[cch % 2]
                        tok = dma(p, "sync", buf[:, :cols],
                                  bass.AP(w_t, cch * 128 * cols,
                                          [[cols, 128], [1, cols]]), sm_ld,
                                  waits=[p.cur(sm_v)])
                        tq1 = comp(p, "vector", lambda v, buf=buf, cols=cols:
                                   v.tensor_scalar(
                                       buf[:, :cols], buf[:, :cols], t_scb[:, :],
                                       C_RND, ALU.mult, ALU.add), sm_v,
                                   waits=[tok, tvb, p.cur(sm_t)])
                        tq2 = comp(p, "vector", lambda v, buf=buf, cols=cols:
                                   v.tensor_scalar(
                                       buf[:, :cols], buf[:, :cols], C_RND, -1.0,
                                       ALU.subtract, ALU.max), sm_v, waits=[tq1])
                        tq3 = comp(p, "vector", lambda v, buf=buf, cols=cols:
                                   v.tensor_scalar(
                                       t_wq[:, :cols], buf[:, :cols], 1.0, None,
                                       ALU.min), sm_v, waits=[tq2])
                        for grp in range(cols // 512):
                            pt = p_tr[grp % 2]
                            tt = None
                            for b4 in range(4):
                                bd = grp * 4 + b4
                                tt = comp(p, "tensor",
                                          lambda t, bd=bd, b4=b4, pt=pt: t.transpose(
                                              pt[:, b4 * 128:(b4 + 1) * 128],
                                              t_wq[:, bd * 128:(bd + 1) * 128],
                                              s_id[:, :]),
                                          sm_t if b4 == 3 else None,
                                          waits=[tq3, p.cur(sm_st)]
                                          if b4 == 0 else [])
                            ev = t_ev[grp % 2]
                            ta = comp(p, "scalar", lambda s, ev=ev, pt=pt:
                                      s.activation(out=ev[:, :], in_=pt[:, :],
                                                   func=AF.Identity),
                                      sm_a, waits=[tt])
                            dma(p, "scalar",
                                bass.AP(wT_s, grp * 4 * 128 * rows + cch * 128,
                                        [[rows, 128], [128 * rows, 4], [1, 128]]),
                                ev[:, :], sm_st, waits=[ta])

                quant_matrix(w_gate, FF, H, wgT_s, 0)
                quant_matrix(w_up, FF, H, wuT_s, 1)
                quant_matrix(w_down, H, FF, wdT_s, 2)

                for cch in range(2 * IN // 128):
                    buf = t_w[cch % 2]
                    tok = dma(p, "sync", buf[:, :H],
                              bass.AP(in_w, cch * 128 * H, [[H, 128], [1, H]]),
                              sm_ld, waits=[p.cur(sm_v)])
                    tc = comp(p, "vector", lambda v, buf=buf: v.tensor_copy(
                        t_wq[:, :H], buf[:, :H]), sm_v,
                        waits=[tok, p.cur(sm_t)])
                    for grp in range(NKH // 4):
                        pt = p_tr[grp % 2]
                        tt = None
                        for b4 in range(4):
                            bd = grp * 4 + b4
                            tt = comp(p, "tensor",
                                      lambda t, bd=bd, b4=b4, pt=pt: t.transpose(
                                          pt[:, b4 * 128:(b4 + 1) * 128],
                                          t_wq[:, bd * 128:(bd + 1) * 128],
                                          s_id[:, :]),
                                      sm_t if b4 == 3 else None,
                                      waits=[tc, p.cur(sm_a)] if b4 == 0 else [])
                        for b4 in range(4):
                            bd = grp * 4 + b4
                            comp(p, "scalar",
                                 lambda s, bd=bd, b4=b4, pt=pt, cch=cch:
                                 s.activation(
                                     out=s_inwT[:, bd * 2 * IN + cch * 128:
                                                bd * 2 * IN + (cch + 1) * 128],
                                     in_=pt[:, b4 * 128:(b4 + 1) * 128],
                                     func=AF.Identity),
                                 sm_a, waits=[tt] if b4 == 0 else [])
                for ob in range(32):
                    tprev = None
                    for bd in range(NKH):
                        tprev = comp(p, "tensor", lambda t, ob=ob, bd=bd: t.matmul(
                            p_sc[:, 2:3],
                            s_inwT[:, bd * 2 * IN + ob * 128:
                                   bd * 2 * IN + ob * 128 + 128],
                            s_lnb_bf[:, bd:bd + 1],
                            start=(bd == 0), stop=(bd == NKH - 1)),
                            sm_t if bd == NKH - 1 else None,
                            waits=[p.cur(sm_a), p.cur(sm_v)] if bd == 0 else [])
                    tvb2 = comp(p, "vector", lambda v, ob=ob: v.tensor_tensor(
                        s_btld[:, ob:ob + 1], p_sc[:, 2:3], s_inb[:, ob:ob + 1],
                        ALU.add), sm_v, waits=[tprev])
                    fwait(p, "tensor", [tvb2])
                for bd in range(NKH):
                    comp(p, "vector", lambda v, bd=bd: v.tensor_scalar(
                        s_inwT[:, bd * 2 * IN:(bd + 1) * 2 * IN],
                        s_inwT[:, bd * 2 * IN:(bd + 1) * 2 * IN],
                        s_lnw[:, bd:bd + 1], None, ALU.mult), sm_v,
                        waits=[p.cur(sm_t)] if bd == 0 else [])
                if debug:
                    dma(p, "scalar", inwT_d.ap(), s_inwT[:, :], sm_st,
                        waits=[p.cur(sm_v)])
                    dma(p, "scalar", btld_d.ap(), s_btld[:, :], sm_st,
                        waits=[p.cur(sm_v)])
                    dma(p, "scalar", mw_d.ap(), s_mw[:, :], sm_st,
                        waits=[p.cur(sm_v)])
                    fwait(p, "scalar", [p.cur(sm_st)])
                p.run_block()

            with ExitStack() as ph:
                s_xpT = ph.enter_context(
                    nc.sbuf_tensor("s_xpT", [128, NCH * 96], BF16))
                s_dtpT = ph.enter_context(
                    nc.sbuf_tensor("s_dtpT", [64, NCH * 128], BF16))
                NB = TA // 128
                s_h = [ph.enter_context(
                    nc.sbuf_tensor(f"s_h{i}", [128, NB * H], F32))
                    for i in range(2)]
                s_hn2 = [ph.enter_context(
                    nc.sbuf_tensor(f"s_hn{i}", [128, NB * H], BF16))
                    for i in range(2)]
                s_hn = s_hn2[0]
                s_hnT2 = [ph.enter_context(
                    nc.sbuf_tensor(f"s_hnT{i}", [128, NKH * TA], BF16))
                    for i in range(2)]
                s_hnT = s_hnT2[0]
                s_xc = ph.enter_context(
                    nc.sbuf_tensor("s_xc", [128, NCH * (TA + 3)], F32))
                s_par = ph.enter_context(nc.sbuf_tensor("s_par", [96, TA], F32))
                s_drb = ph.enter_context(nc.sbuf_tensor("s_drb", [64, TA], BF16))
                s_stat = ph.enter_context(
                    nc.sbuf_tensor("s_stat", [128, 8 * NB], F32))
                s_sq = ph.enter_context(nc.sbuf_tensor("s_sq", [128, H], F32))
                s_ca = [ph.enter_context(
                    nc.sbuf_tensor(f"s_ca{i}", [128, TA], F32)) for i in range(2)]
                s_xcb = [ph.enter_context(
                    nc.sbuf_tensor(f"s_xcb{i}", [128, TA], BF16)) for i in range(2)]
                s_dt = [ph.enter_context(
                    nc.sbuf_tensor(f"s_dt{i}", [128, TA], F32)) for i in range(2)]
                s_z = [ph.enter_context(
                    nc.sbuf_tensor(f"s_z{i}", [128, TA], F32)) for i in range(2)]
                s_sp = [ph.enter_context(
                    nc.sbuf_tensor(f"s_sp{i}", [128, TA], F32)) for i in range(2)]
                s_gt = [ph.enter_context(
                    nc.sbuf_tensor(f"s_gt{i}", [128, TA], BF16)) for i in range(2)]
                s_dx = [ph.enter_context(
                    nc.sbuf_tensor(f"s_dx{i}", [128, TA], BF16)) for i in range(2)]
                p_mm = [ph.enter_context(
                    nc.psum_tensor(f"p_mm{i}", [128, TA], F32)) for i in range(2)]
                p_xp = ph.enter_context(nc.psum_tensor("p_xp", [96, TA], F32))
                p_dt = [ph.enter_context(
                    nc.psum_tensor(f"p_dt{i}", [128, TA], F32)) for i in range(2)]
                p_t2 = ph.enter_context(nc.psum_tensor("p_t2", [128, 512], BF16))

                t_xw = s_h[0]
                t_xq2 = s_h[1]
                tok = dma(p, "sync", t_xw[:96, :IN],
                          bass.AP(xp_w, 0, [[IN, 96], [1, IN]]), sm_ld)
                tcx = comp(p, "vector", lambda v: v.tensor_copy(
                    s_hn[:96, :IN], t_xw[:96, :IN]), sm_v, waits=[tok])
                for ch in range(NCH):
                    sl4 = (ch % 2) * 128
                    tt = comp(p, "tensor", lambda t, ch=ch, sl4=sl4: t.transpose(
                        p_t2[:, sl4:sl4 + 96],
                        s_hn[:96, ch * 128:(ch + 1) * 128], s_id[:96, :96]),
                        sm_t, waits=[tcx, p.cur(sm_a)])
                    comp(p, "scalar", lambda s, ch=ch, sl4=sl4: s.activation(
                        out=s_xpT[:, ch * 96:(ch + 1) * 96],
                        in_=p_t2[:, sl4:sl4 + 96], func=AF.Identity), sm_a,
                        waits=[tt])
                tdw = dma(p, "sync", t_xq2[:, :RK * NCH],
                          bass.AP(dtp_w, 0, [[RK, 128], [128 * RK, NCH], [1, RK]]),
                          sm_ld)
                tcd = comp(p, "vector", lambda v: v.tensor_copy(
                    s_hn[:, :RK * NCH].bitcast(BF16)
                    if False else s_hn[:, :RK * NCH], t_xq2[:, :RK * NCH]),
                    sm_v, waits=[tdw, p.cur(sm_t)])
                for ch in range(NCH):
                    sl4 = (ch % 2) * 128
                    tt = comp(p, "tensor", lambda t, ch=ch, sl4=sl4: t.transpose(
                        p_t2[:RK, sl4:sl4 + 128],
                        s_hn[:, ch * RK:(ch + 1) * RK], s_id[:, :]),
                        sm_t, waits=[tcd, p.cur(sm_a)])
                    comp(p, "scalar", lambda s, ch=ch, sl4=sl4: s.activation(
                        out=s_dtpT[:, ch * 128:(ch + 1) * 128],
                        in_=p_t2[:RK, sl4:sl4 + 128], func=AF.Identity), sm_a,
                        waits=[tt])
                p.run_block()

                h_used = [None, None]
                cons_x = {}
                g_used = [None, None]
                cons_hn = [None, None]
                cons_hnT = [None, None]
                xp_evac = None
                dtp_last = None
                evac_tok = {}
                for it in range(NT_A):
                    base = it * TA
                    s_hn = s_hn2[it % 2]
                    s_hnT = s_hnT2[it % 2]
                    hb = s_h[it % 2]
                    htok = dma(
                        p, "sync", hb[:, :],
                        bass.AP(hid, base * H, [[H, 128], [128 * H, NB], [1, H]]),
                        sm_ld, waits=[h_used[it % 2]])
                    hn_toks = []
                    red_toks = []
                    sq_toks = []
                    for j in range(NB):
                        hbj = hb[:, j * H:(j + 1) * H]
                        c0 = 8 * j
                        red_toks.append(comp(
                            p, "vector", lambda v, hbj=hbj, c0=c0:
                            v.reduce_sum(s_stat[:, c0:c0 + 1], hbj, axis=AX.X),
                            sm_v, waits=[htok] + ([cons_hn[it % 2]] if j == 0 else [])))
                    for j in range(NB):
                        hbj = hb[:, j * H:(j + 1) * H]
                        c0 = 8 * j
                        sq_toks.append(comp(
                            p, "scalar", lambda s, hbj=hbj, c0=c0:
                            s.activation(out=s_sq[:, :], in_=hbj,
                                         func=AF.Square,
                                         accum_out=s_stat[:, c0 + 1:c0 + 2]),
                            sm_a, waits=[htok] if j else [htok, p.cur(sm_v)]))
                    def sv(col):
                        return bass.AP(s_stat, col, [[8 * NB, 128], [8, NB]])
                    tv1 = comp(p, "vector", lambda v: v.tensor_scalar(
                        sv(0), sv(0), 1.0 / H, None, ALU.mult), sm_v,
                        waits=[sq_toks[-1], red_toks[-1]])
                    tv2 = comp(p, "vector", lambda v: v.tensor_tensor(
                        sv(2), sv(0), sv(0), ALU.mult), sm_v, waits=[tv1])
                    tv3 = comp(p, "vector", lambda v: v.scalar_tensor_tensor(
                        sv(3), sv(1), 1.0 / H, sv(2), ALU.mult, ALU.subtract),
                        sm_v, waits=[tv2])
                    ta1 = comp(p, "scalar", lambda s: s.activation(
                        out=sv(4), in_=sv(3), func=AF.Ln,
                        bias=s_epscol[:, :]), sm_a, waits=[tv3])
                    ta2 = comp(p, "scalar", lambda s: s.activation(
                        out=sv(5), in_=sv(4), func=AF.Exp,
                        scale=s_mhalf[:, :]), sm_a, waits=[ta1])
                    tv4 = comp(p, "vector", lambda v: v.tensor_tensor(
                        sv(6), sv(0), sv(5), ALU.mult), sm_v, waits=[ta2])
                    for j in range(NB):
                        hbj = hb[:, j * H:(j + 1) * H]
                        c0 = 8 * j
                        hn_toks.append(comp(
                            p, "vector", lambda v, hbj=hbj, j=j, c0=c0, s_hn=s_hn:
                            v.tensor_scalar(
                                s_hn[:, j * H:(j + 1) * H], hbj,
                                s_stat[:, c0 + 5:c0 + 6],
                                s_stat[:, c0 + 6:c0 + 7],
                                ALU.mult, ALU.subtract), sm_v,
                            waits=[tv4, cons_hn[it % 2]] if j == 0 else []))
                    h_used[it % 2] = hn_toks[-1]
                    for j in range(NB):
                        for g2 in range(2):
                            tt = None
                            for b4 in range(4):
                                bd = g2 * 4 + b4
                                k = j * NKH + bd
                                w = []
                                if b4 == 0:
                                    w = [hn_toks[j], p.cur(sm_a),
                                         cons_hnT[it % 2] if k == 0 else None]
                                tt = comp(
                                    p, "tensor",
                                    lambda t, j=j, bd=bd, b4=b4, s_hn=s_hn:
                                    t.transpose(
                                        p_t2[:, b4 * 128:(b4 + 1) * 128],
                                        s_hn[:, j * H + bd * 128:
                                             j * H + (bd + 1) * 128],
                                        s_id[:, :]),
                                    sm_t if b4 == 3 else None, waits=w)
                            evac_tok[("hnT", it, j, g2)] = comp(
                                p, "scalar", lambda s, j=j, g2=g2, s_hnT=s_hnT:
                                s.activation(
                                    out=bass.AP(
                                        s_hnT, (g2 * 4) * TA + j * 128,
                                        [[NKH * TA, 128], [TA, 4], [1, 128]]),
                                    in_=p_t2[:, :], func=AF.Identity),
                                sm_a, waits=[tt])
                    cons_hn[it % 2] = p.cur(sm_t)
                    hnT_ready = evac_tok[("hnT", it, NB - 1, 1)]
                    for ob in range(32):
                        pm = p_mm[ob % 2]
                        tmm = None
                        for bd in range(NKH):
                            w = []
                            if bd == 0:
                                w = [hnT_ready, evac_tok.get(("ev", it, ob - 2))]
                            tmm = comp(p, "tensor",
                                       lambda t, ob=ob, bd=bd, pm=pm,
                                       s_hnT=s_hnT: t.matmul(
                                           pm[:, :],
                                           s_inwT[:, bd * 2 * IN + ob * 128:
                                                  bd * 2 * IN + ob * 128 + 128],
                                           s_hnT[:, bd * TA:(bd + 1) * TA],
                                           start=(bd == 0), stop=(bd == NKH - 1)),
                                       sm_t if bd == NKH - 1 else None, waits=w)
                        if ob < 16:
                            ch = ob
                            evac_tok[("ev", it, ob)] = comp(
                                p, "scalar", lambda s, ch=ch, pm=pm, ob=ob:
                                s.activation(
                                    out=s_xc[:, ch * (TA + 3) + 3:
                                             (ch + 1) * (TA + 3)],
                                    in_=pm[:, :], func=AF.Identity,
                                    bias=s_btld[:, ob:ob + 1]), sm_a,
                                waits=[tmm, cons_x.get(ch)])
                        else:
                            ch = ob - 16
                            gb = s_gt[ch % 2]
                            tev = comp(
                                p, "scalar", lambda s, gb=gb, pm=pm, ob=ob:
                                s.activation(out=gb[:, :], in_=pm[:, :],
                                             func=AF.Silu,
                                             bias=s_btld[:, ob:ob + 1]), sm_a,
                                waits=[tmm, g_used[ch % 2]])
                            evac_tok[("ev", it, ob)] = tev
                            g_used[ch % 2] = dma(
                                p, "scalar",
                                bass.AP(g_s, ch * 128 * L + base,
                                        [[L, 128], [1, TA]]),
                                gb[:, :], sm_st, waits=[tev])
                    for ch in range(NCH):
                        xs = ch * (TA + 3)
                        ev = evac_tok[("ev", it, ch)]
                        th0 = comp(p, "vector", lambda v, ch=ch, xs=xs:
                                   v.tensor_copy(s_xc[:, xs:xs + 3],
                                                 s_halo[:, ch * 3:(ch + 1) * 3]),
                                   sm_v, waits=[ev])
                        th1 = comp(p, "vector", lambda v, ch=ch, xs=xs:
                                   v.tensor_copy(s_halo[:, ch * 3:(ch + 1) * 3],
                                                 s_xc[:, xs + TA:xs + TA + 3]),
                                   sm_v, waits=[th0])
                        tcv = comp(p, "vector", lambda v, ch=ch, xs=xs:
                                   v.tensor_scalar(
                                       s_ca[0][:, :], s_xc[:, xs:xs + TA],
                                       s_cw[:, ch * KC:ch * KC + 1], None,
                                       ALU.mult), sm_v, waits=[th1])
                        for k in range(1, KC):
                            src = s_ca[(k - 1) % 2]
                            dst = s_ca[k % 2]
                            tcv = comp(p, "vector",
                                       lambda v, ch=ch, xs=xs, k=k, src=src,
                                       dst=dst:
                                       v.scalar_tensor_tensor(
                                           dst[:, :], s_xc[:, xs + k:xs + k + TA],
                                           s_cw[:, ch * KC + k:ch * KC + k + 1],
                                           src[:, :], ALU.mult, ALU.add),
                                       sm_v, waits=[tcv])
                        tsl = comp(p, "scalar", lambda s, ch=ch, xs=xs:
                                   s.activation(
                                       out=s_xc[:, xs + 3:xs + 3 + TA],
                                       in_=s_ca[(KC - 1) % 2][:, :],
                                       func=AF.Silu, bias=s_cb[:, ch:ch + 1]),
                                   sm_a, waits=[tcv])
                        xb = s_xcb[ch % 2]
                        tcb = comp(p, "vector", lambda v, ch=ch, xs=xs, xb=xb:
                                   v.tensor_copy(xb[:, :],
                                                 s_xc[:, xs + 3:xs + 3 + TA]),
                                   sm_v, waits=[tsl])
                        comp(p, "tensor", lambda t, ch=ch, xb=xb: t.matmul(
                            p_xp[:, :], s_xpT[:, ch * 96:(ch + 1) * 96],
                            xb[:, :], start=(ch == 0), stop=(ch == NCH - 1)),
                            sm_t if ch == NCH - 1 else None,
                            waits=[tcb] + ([xp_evac] if ch == 0 else []))
                        dxb = s_dx[ch % 2]
                        tdx = comp(p, "vector", lambda v, ch=ch, xs=xs, dxb=dxb:
                                   v.tensor_scalar(
                                       dxb[:, :], s_xc[:, xs + 3:xs + 3 + TA],
                                       s_dcol[:, ch:ch + 1], None, ALU.mult),
                                   sm_v, waits=[tsl])
                        dma(p, "scalar",
                            bass.AP(ys_s, ch * 128 * L + base,
                                    [[L, 128], [1, TA]]),
                            dxb[:, :], sm_st, waits=[tdx])
                    xp_mm_tok = p.cur(sm_t)
                    xp_evac = comp(p, "scalar", lambda s: s.activation(
                        out=s_par[:, :], in_=p_xp[:, :], func=AF.Identity),
                        sm_a, waits=[xp_mm_tok, dtp_last])
                    dma(p, "sync",
                        bass.AP(bcT_s, base, [[L, 32], [1, TA]]),
                        bass.AP(s_par, 64 * TA, [[TA, 32], [1, TA]]),
                        sm_ld, waits=[xp_evac])
                    tdr = comp(p, "vector", lambda v: v.tensor_copy(
                        s_drb[:, :], s_par[:64, :]), sm_v, waits=[xp_evac])
                    for ch in range(NCH):
                        pd = p_dt[ch % 2]
                        zb = s_z[ch % 2]
                        spb = s_sp[ch % 2]
                        db = s_dt[ch % 2]
                        tdm = comp(p, "tensor", lambda t, ch=ch, pd=pd: t.matmul(
                            pd[:, :], s_dtpT[:, ch * 128:(ch + 1) * 128],
                            s_drb[:, :], start=True, stop=True), sm_t,
                            waits=[tdr] + ([evac_tok.get(("z", it, ch - 2))]
                                           if ch >= 2 else []))
                        tz = comp(p, "vector", lambda v, pd=pd, ch=ch, zb=zb:
                                  v.tensor_scalar(zb[:, :], pd[:, :],
                                                  s_dtpb[:, ch:ch + 1], None,
                                                  ALU.add), sm_v, waits=[tdm])
                        evac_tok[("z", it, ch)] = tz
                        nzb = s_ca[ch % 2]
                        tsp0 = comp(p, "vector", lambda v, zb=zb, nzb=nzb:
                                    v.tensor_scalar(nzb[:, :], zb[:, :], -1.0,
                                                    None, ALU.mult),
                                    sm_v, waits=[tz])
                        tsp1 = comp(p, "vector", lambda v, zb=zb, spb=spb,
                                    nzb=nzb:
                                    v.tensor_tensor(spb[:, :], zb[:, :],
                                                    nzb[:, :], ALU.min),
                                    sm_v, waits=[tsp0])
                        tsp2 = comp(p, "scalar", lambda s, spb=spb: s.activation(
                            out=spb[:, :], in_=spb[:, :], func=AF.Exp), sm_a,
                            waits=[tsp1])
                        tsp3 = comp(p, "scalar", lambda s, spb=spb: s.activation(
                            out=spb[:, :], in_=spb[:, :], func=AF.Ln, bias=1.0),
                            sm_a, waits=[tsp2])
                        tsp4 = comp(p, "vector",
                                    lambda v, zb=zb, spb=spb, db=db:
                                    v.scalar_tensor_tensor(
                                        db[:, :], zb[:, :], 0.0, spb[:, :],
                                        ALU.max, ALU.add), sm_v, waits=[tsp3])
                        dma(p, "scalar",
                            bass.AP(dt_s, ch * 128 * L + base,
                                    [[L, 128], [1, TA]]),
                            db[:, :], sm_st, waits=[tsp4])
                        xs = ch * (TA + 3)
                        tdtx = comp(p, "vector",
                                    lambda v, db=db, ch=ch, xs=xs, zb=zb:
                                    v.tensor_tensor(
                                        zb[:, :], db[:, :],
                                        s_xc[:, xs + 3:xs + 3 + TA], ALU.mult),
                                    sm_v, waits=[tsp4])
                        cons_x[ch] = tdtx
                        dma(p, "scalar",
                            bass.AP(dtx_s, ch * 128 * L + base,
                                    [[L, 128], [1, TA]]),
                            zb[:, :], sm_st, waits=[tdtx])
                    dtp_last = p.cur(sm_v)
                    cons_hnT[it % 2] = p.cur(sm_t)
                fwait(p, "scalar", [p.cur(sm_st)])
                fwait(p, "sync", [p.cur(sm_ld)])
                p.run_block()

        with ExitStack() as phB:
            s_owT = phB.enter_context(
                nc.sbuf_tensor("s_owT", [128, NCH * H], BF16))
            with ExitStack() as ph:
                t_w = [ph.enter_context(nc.sbuf_tensor(f"u_w{i}", [128, IN], F32))
                       for i in range(2)]
                t_wq = ph.enter_context(nc.sbuf_tensor("u_wq", [128, IN], BF16))
                p_tr = [ph.enter_context(nc.psum_tensor(f"q_tr{i}", [128, 512], BF16))
                        for i in range(2)]
                for oc in range(NKH):
                    buf = t_w[oc % 2]
                    tok = dma(p, "sync", buf[:, :],
                              bass.AP(out_w, oc * 128 * IN, [[IN, 128], [1, IN]]),
                              sm_ld, waits=[p.cur(sm_v)] if oc >= 2 else [])
                    tc = comp(p, "vector", lambda v, buf=buf: v.tensor_copy(
                        t_wq[:, :], buf[:, :]), sm_v, waits=[tok, p.cur(sm_t)])
                    for grp in range(NCH // 4):
                        pt = p_tr[grp % 2]
                        tt = None
                        for b4 in range(4):
                            cc = grp * 4 + b4
                            tt = comp(p, "tensor",
                                      lambda t, cc=cc, b4=b4, pt=pt: t.transpose(
                                          pt[:, b4 * 128:(b4 + 1) * 128],
                                          t_wq[:, cc * 128:(cc + 1) * 128],
                                          s_id[:, :]),
                                      sm_t if b4 == 3 else None,
                                      waits=[tc, p.cur(sm_a)] if b4 == 0 else [])
                        comp(p, "scalar", lambda s, grp=grp, oc=oc, pt=pt:
                             s.activation(
                                 out=bass.AP(s_owT, (4 * grp) * H + oc * 128,
                                             [[NCH * H, 128], [H, 4], [1, 128]]),
                                 in_=pt[:, :], func=AF.Identity), sm_a,
                             waits=[tt])
                p.run_block()

            with ExitStack() as ph:
                s_br = [ph.enter_context(
                    nc.sbuf_tensor(f"s_br{i}", [128, TB], BF16))
                    for i in range(NGRP)]
                s_cr = [ph.enter_context(
                    nc.sbuf_tensor(f"s_cr{i}", [128, TB], BF16))
                    for i in range(NGRP)]
                s_dtc = [ph.enter_context(
                    nc.sbuf_tensor(f"s_dtc{i}", [128, TB], F32)) for i in range(3)]
                s_dxc = [ph.enter_context(
                    nc.sbuf_tensor(f"s_dxc{i}", [128, TB], F32)) for i in range(3)]
                s_al = [ph.enter_context(
                    nc.sbuf_tensor(f"s_al{i}", [128, TB], F32)) for i in range(4)]
                s_u = [ph.enter_context(
                    nc.sbuf_tensor(f"s_u{i}", [128, TB], F32)) for i in range(4)]
                s_sc = [ph.enter_context(
                    nc.sbuf_tensor(f"s_sc{i}", [128, TB], BF16)) for i in range(4)]
                s_tmp = [ph.enter_context(
                    nc.sbuf_tensor(f"s_tmp{i}", [128, TB], BF16)) for i in range(4)]
                s_y = [ph.enter_context(
                    nc.sbuf_tensor(f"s_y{i}", [128, TB], BF16)) for i in range(4)]
                s_gch = [ph.enter_context(
                    nc.sbuf_tensor(f"s_gch{i}", [128, TB], BF16)) for i in range(2)]
                s_ygT = ph.enter_context(
                    nc.sbuf_tensor("s_ygT", [128, NCH * TB], BF16))
                s_hres = [ph.enter_context(
                    nc.sbuf_tensor(f"s_hres{i}", [128, H], F32)) for i in range(2)]
                s_t2 = [ph.enter_context(
                    nc.sbuf_tensor(f"s_t2{i}", [128, H], F32)) for i in range(2)]
                s_x1 = [ph.enter_context(
                    nc.sbuf_tensor(f"s_x1{i}", [128, H], F32)) for i in range(2)]
                s_st2 = ph.enter_context(nc.sbuf_tensor("s_st2", [128, 16], F32))
                s_sq2 = None
                s_bcrow = ph.enter_context(
                    nc.sbuf_tensor("s_bcrow", [1, 2 * TB], F32))
                p_bcb = ph.enter_context(nc.psum_tensor("p_bcb", [128, TB], F32))
                p_bcc = ph.enter_context(nc.psum_tensor("p_bcc", [128, TB], F32))
                p_op = [ph.enter_context(
                    nc.psum_tensor(f"p_op{i}", [128, H], F32)) for i in range(2)]

                br_evac = {}
                bcrow_used = {}
                br_used = {}
                cr_used = {}
                dt_used = [None, None, None]
                dtx_used = [None, None, None]
                al_used = [None] * 4
                u_used = [None] * 4
                sc_used = [None] * 4
                tmp_used = [None] * 4
                y_used = [None, None]
                g_used2 = [None, None]
                ygT_cons = None
                hres_used = [None, None]
                t2_used = [None, None]
                x1_used = [None, None]
                pop_used = [None, None]
                for bt in range(NT_B):
                    base = bt * TB
                    for ng in range(NGRP):
                        nlist = list(range(ng * NGRP, (ng + 1) * NGRP))
                        prev_evac = {0: br_evac.get((NGRP - 1, 0)),
                                     1: br_evac.get((NGRP - 1, 1))}
                        for i, n in enumerate(nlist):
                            for half, (s_dst, rowoff, pbc) in enumerate(
                                    [(s_br[i], n, p_bcb), (s_cr[i], 16 + n, p_bcc)]):
                                roff = ((2 * i + half) % 2) * TB
                                tstg = dma(
                                    p, "sync",
                                    bass.AP(s_bcrow, roff, [[2 * TB, 1], [1, TB]]),
                                    bass.AP(bcT_s, rowoff * L + base,
                                            [[L, 1], [1, TB]]),
                                    sm_ld, waits=[bcrow_used.get(roff)])
                                tmmb = None
                                for q in range(TB // 512):
                                    w = [tstg]
                                    if q == 0:
                                        w += [prev_evac[half],
                                              (br_used if half == 0 else
                                               cr_used).get(i)]
                                    tmmb = comp(
                                        p, "tensor",
                                        lambda t, roff=roff, q=q, pbc=pbc:
                                        t.matmul(
                                            pbc[:, q * 512:(q + 1) * 512],
                                            s_ones_f32[:1, :],
                                            bass.AP(s_bcrow, roff + q * 512,
                                                    [[2 * TB, 1], [1, 512]]),
                                            start=True, stop=True),
                                        sm_t if q == TB // 512 - 1 else None,
                                        waits=w)
                                bcrow_used[roff] = tmmb
                                ev = comp(
                                    p, "scalar",
                                    lambda s, s_dst=s_dst, pbc=pbc:
                                    s.activation(out=s_dst[:, :], in_=pbc[:, :],
                                                 func=AF.Identity), sm_a,
                                    waits=[tmmb])
                                br_evac[(i, half)] = ev
                                prev_evac[half] = ev
                        for ch in range(NCH):
                            sl = ch % 3
                            sly = ch % 2
                            dtb = s_dtc[sl]
                            tok_dt = dma(p, "sync", dtb[:, :],
                                         bass.AP(dt_s, ch * 128 * L + base,
                                                 [[L, 128], [1, TB]]), sm_ld,
                                         waits=[dt_used[sl]])
                            dxb = s_dxc[sl]
                            tok_dx = dma(p, "sync", dxb[:, :],
                                         bass.AP(dtx_s, ch * 128 * L + base,
                                                 [[L, 128], [1, TB]]), sm_ld,
                                         waits=[dtx_used[sl]])
                            yb, yb2 = s_y[2 * sly], s_y[2 * sly + 1]
                            tok_y = dma(p, "sync", yb[:, :],
                                        bass.AP(ys_s, ch * 128 * L + base,
                                                [[L, 128], [1, TB]]), sm_ld,
                                        waits=[y_used[sly]])
                            ylast = tok_y
                            ycur, yoth = yb, yb2
                            pend = []
                            for i, n in enumerate(nlist):
                                isl = i
                                ab = s_al[isl]
                                ta = comp(p, "scalar",
                                          lambda s, ab=ab, ch=ch, n=n, dtb=dtb:
                                          s.activation(
                                              out=ab[:, :], in_=dtb[:, :],
                                              func=AF.Exp,
                                              scale=s_acol[:, ch * NST + n:
                                                           ch * NST + n + 1]),
                                          sm_a, waits=[tok_dt, al_used[isl]])
                                last_ta = ta
                                ub = s_u[isl]
                                tu = comp(p, "gpsimd",
                                          lambda g, ub=ub, dxb=dxb, i=i:
                                          g.tensor_tensor(ub[:, :], dxb[:, :],
                                                          s_br[i][:, :],
                                                          ALU.mult), sm_q,
                                          waits=[tok_dx, br_evac[(i, 0)],
                                                 u_used[isl]])
                                last_tu = tu
                                scb = s_sc[isl]
                                cidx = ch * NST + n
                                tsc = comp(p, "vector",
                                           lambda v, scb=scb, ab=ab, ub=ub,
                                           cidx=cidx:
                                           v.tensor_tensor_scan(
                                               scb[:, :], ab[:, :], ub[:, :],
                                               s_carry[:, cidx:cidx + 1],
                                               ALU.mult, ALU.add), sm_v,
                                           waits=[ta, tu, sc_used[isl]],
                                           fe=True)
                                al_used[isl] = tsc
                                u_used[isl] = tsc
                                comp(p, "vector",
                                     lambda v, scb=scb, cidx=cidx:
                                     v.tensor_copy(s_carry[:, cidx:cidx + 1],
                                                   scb[:, TB - 1:TB]), sm_v)
                                tmb = s_tmp[isl]
                                tcm = comp(p, "vector",
                                           lambda v, tmb=tmb, scb=scb, i=i:
                                           v.tensor_tensor(tmb[:, :], scb[:, :],
                                                           s_cr[i][:, :],
                                                           ALU.mult), sm_v,
                                           waits=[br_evac[(i, 1)],
                                                  tmp_used[isl]], fe=True)
                                sc_used[isl] = tcm
                                br_used[i] = tcm
                                cr_used[i] = tcm
                                pend.append((i, tmb))
                                if len(pend) >= 2:
                                    i0, tmb0 = pend.pop(0)
                                    ylast = comp(
                                        p, "vector",
                                        lambda v, yoth=yoth, ycur=ycur,
                                        tmb0=tmb0:
                                        v.tensor_add(yoth[:, :], ycur[:, :],
                                                     tmb0[:, :]), sm_v,
                                        waits=[ylast, tok_y], fe=True)
                                    tmp_used[i0] = ylast
                                    ycur, yoth = yoth, ycur
                            while pend:
                                i0, tmb0 = pend.pop(0)
                                ylast = comp(
                                    p, "vector",
                                    lambda v, yoth=yoth, ycur=ycur, tmb0=tmb0:
                                    v.tensor_add(yoth[:, :], ycur[:, :],
                                                 tmb0[:, :]), sm_v,
                                    waits=[ylast, tok_y], fe=True)
                                ycur, yoth = yoth, ycur
                                tmp_used[i0] = ylast
                            dt_used[sl] = last_ta
                            dtx_used[sl] = last_tu
                            if ng < NGRP - 1:
                                y_used[sly] = dma(
                                    p, "scalar",
                                    bass.AP(ys_s, ch * 128 * L + base,
                                            [[L, 128], [1, TB]]),
                                    ycur[:, :], sm_st, waits=[ylast])
                            else:
                                gb = s_gch[sly]
                                tok_g = dma(p, "sync", gb[:, :],
                                            bass.AP(g_s, ch * 128 * L + base,
                                                    [[L, 128], [1, TB]]),
                                            sm_ld, waits=[g_used2[sly]])
                                tyg = comp(p, "vector",
                                           lambda v, ycur=ycur, gb=gb, ch=ch:
                                           v.tensor_tensor(
                                               s_ygT[:, ch * TB:(ch + 1) * TB],
                                               ycur[:, :], gb[:, :], ALU.mult),
                                           sm_v,
                                           waits=[ylast, tok_g,
                                                  ygT_cons if ch == 0 else None])
                                g_used2[sly] = tyg
                                y_used[sly] = tyg
                    yg_ready = p.cur(sm_v)
                    for tb in range(TB // 128):
                        sl = tb % 2
                        po = p_op[sl]
                        hr = s_hres[sl]
                        tok_h = dma(p, "sync", hr[:, :],
                                    bass.AP(hid, (base + tb * 128) * H,
                                            [[H, 128], [1, H]]), sm_ld,
                                    waits=[hres_used[sl]])
                        tmm = None
                        for os_ in range(2):
                            for ch in range(NCH):
                                w = []
                                if ch == 0 and os_ == 0:
                                    w = [yg_ready, pop_used[sl]]
                                tmm = comp(
                                    p, "tensor",
                                    lambda t, tb=tb, os_=os_, ch=ch, po=po:
                                    t.matmul(
                                        po[:, os_ * 512:(os_ + 1) * 512],
                                        s_ygT[:, ch * TB + tb * 128:
                                              ch * TB + tb * 128 + 128],
                                        s_owT[:, ch * H + os_ * 512:
                                              ch * H + os_ * 512 + 512],
                                        start=(ch == 0), stop=False),
                                    None, waits=w)
                            tmm = comp(p, "tensor",
                                       lambda t, os_=os_, po=po: t.matmul(
                                           po[:, os_ * 512:(os_ + 1) * 512],
                                           s_ones_bf[:1, :],
                                           s_outb_row[:1, os_ * 512:
                                                      (os_ + 1) * 512],
                                           start=False, stop=True),
                                       sm_t if os_ == 1 else None)
                        t2b = s_t2[sl]
                        tt2 = comp(p, "vector", lambda v, t2b=t2b, hr=hr, po=po:
                                   v.scalar_tensor_tensor(
                                       t2b[:, :], hr[:, :], 2.0, po[:, :],
                                       ALU.mult, ALU.add), sm_v,
                                   waits=[tmm, tok_h, t2_used[sl]])
                        pop_used[sl] = tt2
                        c0 = 8 * sl
                        tvs = comp(p, "vector", lambda v, t2b=t2b, c0=c0:
                                   v.reduce_sum(s_st2[:, c0:c0 + 1], t2b[:, :],
                                                axis=AX.X), sm_v, waits=[tt2])
                        tas = comp(p, "scalar", lambda s, t2b=t2b, c0=c0,
                                   hr=hr:
                                   s.activation(
                                       out=hr[:, :], in_=t2b[:, :],
                                       func=AF.Square,
                                       accum_out=s_st2[:, c0 + 1:c0 + 2]),
                                   sm_a, waits=[tvs])
                        hres_used[sl] = tas
                        tv1 = comp(p, "vector", lambda v, c0=c0: v.tensor_scalar(
                            s_st2[:, c0:c0 + 1], s_st2[:, c0:c0 + 1],
                            1.0 / H, None, ALU.mult), sm_v, waits=[tas])
                        tv2 = comp(p, "vector", lambda v, c0=c0: v.tensor_tensor(
                            s_st2[:, c0 + 2:c0 + 3], s_st2[:, c0:c0 + 1],
                            s_st2[:, c0:c0 + 1], ALU.mult), sm_v, waits=[tv1])
                        tv3 = comp(p, "vector", lambda v, c0=c0:
                                   v.scalar_tensor_tensor(
                                       s_st2[:, c0 + 3:c0 + 4],
                                       s_st2[:, c0 + 1:c0 + 2], 1.0 / H,
                                       s_st2[:, c0 + 2:c0 + 3],
                                       ALU.mult, ALU.subtract), sm_v,
                                   waits=[tv2])
                        ta1 = comp(p, "scalar", lambda s, c0=c0: s.activation(
                            out=s_st2[:, c0 + 4:c0 + 5],
                            in_=s_st2[:, c0 + 3:c0 + 4], func=AF.Ln, bias=s_epscol[:, :]),
                            sm_a, waits=[tv3])
                        ta2 = comp(p, "scalar", lambda s, c0=c0: s.activation(
                            out=s_st2[:, c0 + 5:c0 + 6],
                            in_=s_st2[:, c0 + 4:c0 + 5], func=AF.Exp,
                            scale=s_mhalf[:, :]), sm_a, waits=[ta1])
                        tv4 = comp(p, "vector", lambda v, c0=c0: v.tensor_tensor(
                            s_st2[:, c0 + 6:c0 + 7], s_st2[:, c0:c0 + 1],
                            s_st2[:, c0 + 5:c0 + 6], ALU.mult), sm_v,
                            waits=[ta2])
                        x1b = s_x1[sl]
                        tx1 = comp(p, "vector", lambda v, x1b=x1b, t2b=t2b, c0=c0:
                                   v.tensor_scalar(
                                       x1b[:, :], t2b[:, :],
                                       s_st2[:, c0 + 5:c0 + 6],
                                       s_st2[:, c0 + 6:c0 + 7],
                                       ALU.mult, ALU.subtract), sm_v,
                                   waits=[tv4, x1_used[sl]])
                        t2_used[sl] = tx1
                        x1_used[sl] = dma(
                            p, "scalar",
                            bass.AP(x1_s, (base + tb * 128) * H,
                                    [[H, 128], [1, H]]),
                            x1b[:, :], sm_st, waits=[tx1])
                    ygT_cons = p.cur(sm_t)
                fwait(p, "scalar", [p.cur(sm_st)])
                fwait(p, "sync", [p.cur(sm_ld)])
                p.run_block()

        with ExitStack() as ph:
            NTB = TC // 128
            s_wdT = ph.enter_context(
                nc.sbuf_tensor("c_wdT", [128, NFF * H], BF16))
            s_xqT2 = [ph.enter_context(
                nc.sbuf_tensor(f"c_xqT{i}", [128, NKH * TC], BF16))
                for i in range(2)]
            s_xqT = s_xqT2[0]
            s_wg = [ph.enter_context(
                nc.sbuf_tensor("c_wg0", [128, NKH * 512], BF16))]
            s_wu = [ph.enter_context(
                nc.sbuf_tensor("c_wu0", [128, NKH * 512], BF16))]
            s_x1c = [ph.enter_context(
                nc.sbuf_tensor(f"c_x1{i}", [128, H], F32)) for i in range(2)]
            s_xq = ph.enter_context(nc.sbuf_tensor("c_xq", [128, H], BF16))
            s_tg = [ph.enter_context(
                nc.sbuf_tensor(f"c_tg{i}", [128, 512], F32)) for i in range(2)]
            s_hp = [ph.enter_context(
                nc.sbuf_tensor(f"c_hp{i}", [128, 512], F32)) for i in range(2)]
            s_hpb = [ph.enter_context(
                nc.sbuf_tensor(f"c_hpb{i}", [128, FF], F32)) for i in range(2)]
            s_hq = ph.enter_context(nc.sbuf_tensor("c_hq", [128, FF], BF16))
            s_hqT = [ph.enter_context(
                nc.sbuf_tensor(f"c_hqT{i}", [128, NFF * 128], BF16))
                for i in range(2)]
            s_rmx2 = [ph.enter_context(
                nc.sbuf_tensor(f"c_rmx{i}", [128, NTB * 8], F32))
                for i in range(2)]
            s_rmx = s_rmx2[0]
            s_fcols2 = [ph.enter_context(
                nc.sbuf_tensor(f"c_fc{i}", [128, NTB * 2], F32))
                for i in range(2)]
            s_fcols = s_fcols2[0]
            s_stc = ph.enter_context(nc.sbuf_tensor("c_st", [128, 32], F32))
            s_sq3 = ph.enter_context(nc.sbuf_tensor("c_sq3", [128, H], F32))
            s_ot = [ph.enter_context(
                nc.sbuf_tensor(f"c_ot{i}", [128, H], F32)) for i in range(2)]
            p_g = [ph.enter_context(
                nc.psum_tensor(f"c_pg{i}", [128, 512], F32)) for i in range(2)]
            p_u2 = [ph.enter_context(
                nc.psum_tensor(f"c_pu{i}", [128, 512], F32)) for i in range(2)]
            p_t3 = [ph.enter_context(
                nc.psum_tensor(f"c_pt{i}", [128, 512], BF16)) for i in range(2)]
            p_d = ph.enter_context(nc.psum_tensor("c_pd", [128, H], F32))

            wd_tok = dma(p, "sync",
                         bass.AP(s_wdT, 0, [[NFF * H, 128], [H, NFF], [1, H]]),
                         bass.AP(wdT_s, 0, [[H, 128], [128 * H, NFF], [1, H]]),
                         sm_ld)

            x1_used3 = [None, None]
            xq_free = [None, None]
            tg_used = [None, None]
            hp_ps = [None, None]
            hp_st = [None, None]
            hpb_used = [None, None]
            hq_free = None
            hqT_used = [None, None]
            ot_used = [None, None]
            pd_used = None
            wg_used = [None, None]
            wu_used = [None, None]
            for sc_i in range(NT_C):
                sbase = sc_i * TC
                spar = sc_i % 2
                s_xqT = s_xqT2[spar]
                s_rmx = s_rmx2[spar]
                s_fcols = s_fcols2[spar]
                hp_off = spar * TC * FF
                for tb in range(NTB):
                    sl = tb % 2
                    c0 = 16 + 8 * sl
                    xb = s_x1c[sl]
                    tokx = dma(p, "sync", xb[:, :],
                               bass.AP(x1_s, (sbase + tb * 128) * H,
                                       [[H, 128], [1, H]]), sm_ld,
                               waits=[x1_used3[sl]])
                    trm = comp(p, "vector", lambda v, xb=xb, c0=c0: v.reduce_max(
                        s_stc[:, c0:c0 + 1], xb[:, :], axis=AX.X,
                        apply_absolute_value=True), sm_v, waits=[tokx])
                    tm2 = comp(p, "vector", lambda v, c0=c0: v.tensor_scalar(
                        s_stc[:, c0 + 1:c0 + 2], s_stc[:, c0:c0 + 1], 1e-5,
                        None, ALU.max), sm_v, waits=[trm])
                    trc = comp(p, "vector", lambda v, c0=c0: v.reciprocal(
                        s_stc[:, c0 + 2:c0 + 3], s_stc[:, c0 + 1:c0 + 2]),
                        sm_v, waits=[tm2])
                    tsc2 = comp(p, "vector", lambda v, c0=c0: v.tensor_scalar(
                        s_stc[:, c0 + 3:c0 + 4], s_stc[:, c0 + 2:c0 + 3],
                        127.0, None, ALU.mult), sm_v, waits=[trc])
                    tf1 = comp(p, "vector", lambda v, tb=tb, c0=c0,
                               s_fcols=s_fcols:
                               v.scalar_tensor_tensor(
                                   s_fcols[:, 2 * tb:2 * tb + 1],
                                   s_mw[:, 0:1], 1.0 / 127.0,
                                   s_stc[:, c0 + 1:c0 + 2],
                                   ALU.mult, ALU.mult), sm_v, waits=[tsc2])
                    tf2 = comp(p, "vector", lambda v, tb=tb, c0=c0,
                               s_fcols=s_fcols:
                               v.scalar_tensor_tensor(
                                   s_fcols[:, 2 * tb + 1:2 * tb + 2],
                                   s_mw[:, 1:2], 1.0 / 127.0,
                                   s_stc[:, c0 + 1:c0 + 2],
                                   ALU.mult, ALU.mult), sm_v, waits=[tf1])
                    tq1 = comp(p, "vector", lambda v, xb=xb, c0=c0:
                               v.tensor_scalar(
                                   xb[:, :], xb[:, :],
                                   s_stc[:, c0 + 3:c0 + 4], C_RND,
                                   ALU.mult, ALU.add), sm_v,
                               waits=[tf2] +
                               ([hq_free, p.cur(sm_a)] if tb == 0 else []))
                    tq2 = comp(p, "vector", lambda v, xb=xb: v.tensor_scalar(
                        xb[:, :], xb[:, :], C_RND, -128.0,
                        ALU.subtract, ALU.max), sm_v, waits=[tq1])
                    tq3 = comp(p, "vector", lambda v, xb=xb: v.tensor_scalar(
                        s_xq[:, :], xb[:, :], 127.0, None, ALU.min),
                        sm_v, waits=[tq2])
                    x1_used3[sl] = tq3
                    for grp in range(2):
                        pt = p_t3[grp]
                        tt = None
                        for b4 in range(4):
                            bd = grp * 4 + b4
                            w = []
                            if b4 == 0:
                                w = [tq3, p.cur(sm_a)]
                                if tb == 0 and grp == 0:
                                    w.append(xq_free[spar])
                            tt = comp(p, "tensor",
                                      lambda t, bd=bd, b4=b4, pt=pt: t.transpose(
                                          pt[:, b4 * 128:(b4 + 1) * 128],
                                          s_xq[:, bd * 128:(bd + 1) * 128],
                                          s_id[:, :]),
                                      sm_t if b4 == 3 else None, waits=w)
                        comp(p, "scalar", lambda s, grp=grp, tb=tb, pt=pt,
                             s_xqT=s_xqT:
                             s.activation(
                                 out=bass.AP(s_xqT, (4 * grp) * TC + tb * 128,
                                             [[NKH * TC, 128], [TC, 4],
                                              [1, 128]]),
                                 in_=pt[:, :], func=AF.Identity), sm_a,
                             waits=[tt])
                xq_ready = p.cur(sm_a)
                for os_ in range(8):
                    wsl = 0
                    wgb = s_wg[wsl]
                    wub = s_wu[wsl]
                    tokwg = dma(p, "sync", wgb[:, :],
                                bass.AP(wgT_s, os_ * 512,
                                        [[FF, 128], [128 * FF, NKH], [1, 512]]),
                                sm_ld, waits=[wg_used[wsl]])
                    tokwu = dma(p, "sync", wub[:, :],
                                bass.AP(wuT_s, os_ * 512,
                                        [[FF, 128], [128 * FF, NKH], [1, 512]]),
                                sm_ld, waits=[wu_used[wsl]])
                    for tb in range(NTB):
                        sl = tb % 2
                        pg = p_g[sl]
                        pu = p_u2[sl]
                        tmm = None
                        for ch in range(NKH):
                            w = []
                            if ch == 0:
                                w = [xq_ready, tokwg, tokwu, hp_ps[sl]]
                            tmm = comp(p, "tensor",
                                       lambda t, ch=ch, tb=tb, pg=pg, wgb=wgb,
                                       s_xqT=s_xqT:
                                       t.matmul(
                                           pg[:, :],
                                           s_xqT[:, ch * TC + tb * 128:
                                                 ch * TC + tb * 128 + 128],
                                           wgb[:, ch * 512:(ch + 1) * 512],
                                           start=(ch == 0),
                                           stop=(ch == NKH - 1)),
                                       None, waits=w)
                            tmm = comp(p, "tensor",
                                       lambda t, ch=ch, tb=tb, pu=pu, wub=wub,
                                       s_xqT=s_xqT:
                                       t.matmul(
                                           pu[:, :],
                                           s_xqT[:, ch * TC + tb * 128:
                                                 ch * TC + tb * 128 + 128],
                                           wub[:, ch * 512:(ch + 1) * 512],
                                           start=(ch == 0),
                                           stop=(ch == NKH - 1)),
                                       sm_t if ch == NKH - 1 else None)
                        wg_used[wsl] = tmm
                        wu_used[wsl] = tmm
                        tgb = s_tg[sl]
                        ttg = comp(p, "scalar", lambda s, tgb=tgb, pg=pg, tb=tb,
                                   s_fcols=s_fcols:
                                   s.activation(
                                       out=tgb[:, :], in_=pg[:, :],
                                       func=AF.Silu,
                                       scale=s_fcols[:, 2 * tb:2 * tb + 1]),
                                   sm_a, waits=[tmm, tg_used[sl]])
                        hpb = s_hp[sl]
                        thp = comp(p, "vector",
                                   lambda v, hpb=hpb, tgb=tgb, pu=pu:
                                   v.tensor_tensor(hpb[:, :], tgb[:, :],
                                                   pu[:, :], ALU.mult), sm_v,
                                   waits=[ttg, hp_st[sl]])
                        tg_used[sl] = thp
                        hp_ps[sl] = thp
                        trm2 = comp(p, "vector",
                                    lambda v, hpb=hpb, tb=tb, os_=os_,
                                    s_rmx=s_rmx:
                                    v.reduce_max(
                                        s_rmx[:, tb * 8 + os_:
                                              tb * 8 + os_ + 1],
                                        hpb[:, :], axis=AX.X,
                                        apply_absolute_value=True), sm_v,
                                    waits=[thp])
                        hp_st[sl] = dma(
                            p, "scalar",
                            bass.AP(hp_s, hp_off + (tb * 128) * FF + os_ * 512,
                                    [[FF, 128], [1, 512]]),
                            hpb[:, :], sm_st, waits=[trm2])
                hp_done = p.cur(sm_st)
                for tb in range(NTB):
                    sl = tb % 2
                    c0 = 8 * sl
                    hb = s_hpb[sl]
                    tokh = dma(p, "sync", hb[:, :],
                               bass.AP(hp_s, hp_off + (tb * 128) * FF,
                                       [[FF, 128], [1, FF]]), sm_ld,
                               waits=[hp_done if tb == 0 else None,
                                      hpb_used[sl]])
                    xb = s_x1c[sl]
                    tokx = dma(p, "sync", xb[:, :],
                               bass.AP(x1_s, (sbase + tb * 128) * H,
                                       [[H, 128], [1, H]]), sm_ld,
                               waits=[x1_used3[sl]])
                    trm = comp(p, "vector", lambda v, tb=tb, c0=c0,
                               s_rmx=s_rmx: v.reduce_max(
                        s_stc[:, c0:c0 + 1], s_rmx[:, tb * 8:(tb + 1) * 8],
                        axis=AX.X), sm_v)
                    tt1 = comp(p, "vector", lambda v, tb=tb, c0=c0,
                               s_fcols=s_fcols:
                               v.tensor_tensor(
                                   s_stc[:, c0 + 1:c0 + 2],
                                   s_fcols[:, 2 * tb + 1:2 * tb + 2],
                                   s_stc[:, c0:c0 + 1], ALU.mult), sm_v,
                               waits=[trm])
                    tt2 = comp(p, "vector", lambda v, c0=c0: v.tensor_scalar(
                        s_stc[:, c0 + 2:c0 + 3], s_stc[:, c0 + 1:c0 + 2],
                        1e-5, None, ALU.max), sm_v, waits=[tt1])
                    tt3 = comp(p, "vector", lambda v, c0=c0: v.reciprocal(
                        s_stc[:, c0 + 3:c0 + 4], s_stc[:, c0 + 2:c0 + 3]),
                        sm_v, waits=[tt2])
                    tt4 = comp(p, "vector", lambda v, tb=tb, c0=c0,
                               s_fcols=s_fcols:
                               v.scalar_tensor_tensor(
                                   s_stc[:, c0 + 4:c0 + 5],
                                   s_fcols[:, 2 * tb + 1:2 * tb + 2], 127.0,
                                   s_stc[:, c0 + 3:c0 + 4],
                                   ALU.mult, ALU.mult), sm_v, waits=[tt3])
                    tt5 = comp(p, "vector", lambda v, c0=c0:
                               v.scalar_tensor_tensor(
                                   s_stc[:, c0 + 5:c0 + 6],
                                   s_stc[:, c0 + 2:c0 + 3], 1.0 / 127.0,
                                   s_mw[:, 2:3], ALU.mult, ALU.mult), sm_v,
                               waits=[tt4])
                    tq1 = comp(p, "vector", lambda v, hb=hb, c0=c0:
                               v.tensor_scalar(
                                   hb[:, :], hb[:, :],
                                   s_stc[:, c0 + 4:c0 + 5], C_RND,
                                   ALU.mult, ALU.add), sm_v,
                               waits=[tokh, tt5, hq_free])
                    tq2 = comp(p, "vector", lambda v, hb=hb: v.tensor_scalar(
                        hb[:, :], hb[:, :], C_RND, -128.0,
                        ALU.subtract, ALU.max), sm_v, waits=[tq1])
                    tq3 = comp(p, "vector", lambda v, hb=hb: v.tensor_scalar(
                        s_hq[:, :], hb[:, :], 127.0, None, ALU.min), sm_v,
                        waits=[tq2])
                    hpb_used[sl] = tq3
                    hqT = s_hqT[sl]
                    tev = None
                    for grp in range(NFF // 4):
                        pt = p_t3[grp % 2]
                        tt = None
                        for b4 in range(4):
                            bd = grp * 4 + b4
                            w = []
                            if b4 == 0:
                                w = [tq3, p.cur(sm_a)]
                                if grp == 0:
                                    w.append(hqT_used[sl])
                            tt = comp(p, "tensor",
                                      lambda t, bd=bd, b4=b4, pt=pt: t.transpose(
                                          pt[:, b4 * 128:(b4 + 1) * 128],
                                          s_hq[:, bd * 128:(bd + 1) * 128],
                                          s_id[:, :]),
                                      sm_t if b4 == 3 else None, waits=w)
                        tev = comp(p, "scalar",
                                   lambda s, grp=grp, hqT=hqT, pt=pt:
                                   s.activation(
                                       out=hqT[:, grp * 512:(grp + 1) * 512],
                                       in_=pt[:, :], func=AF.Identity), sm_a,
                                   waits=[tt])
                    hq_free = p.cur(sm_t)
                    tmm = None
                    for ch in range(NFF):
                        for os2 in range(2):
                            w = []
                            if ch == 0 and os2 == 0:
                                w = [tev, wd_tok, pd_used]
                            tmm = comp(p, "tensor",
                                       lambda t, ch=ch, os2=os2, hqT=hqT:
                                       t.matmul(
                                           p_d[:, os2 * 512:(os2 + 1) * 512],
                                           hqT[:, ch * 128:(ch + 1) * 128],
                                           s_wdT[:, ch * H + os2 * 512:
                                                 ch * H + os2 * 512 + 512],
                                           start=(ch == 0),
                                           stop=(ch == NFF - 1)),
                                       sm_t if (ch == NFF - 1 and os2 == 1)
                                       else None, waits=w)
                    hqT_used[sl] = tmm
                    otb = s_ot[sl]
                    tx2 = comp(p, "vector", lambda v, otb=otb, xb=xb, c0=c0:
                               v.scalar_tensor_tensor(
                                   otb[:, :], p_d[:, :],
                                   s_stc[:, c0 + 5:c0 + 6], xb[:, :],
                                   ALU.mult, ALU.add), sm_v,
                               waits=[tmm, tokx, ot_used[sl]])
                    pd_used = tx2
                    x1_used3[sl] = tx2
                    tvs = comp(p, "vector", lambda v, otb=otb, c0=c0:
                               v.reduce_sum(s_stc[:, c0 + 6:c0 + 7],
                                            otb[:, :], axis=AX.X), sm_v,
                               waits=[tx2])
                    tas = comp(p, "scalar", lambda s, otb=otb, c0=c0:
                               s.activation(
                                   out=s_sq3[:, :], in_=otb[:, :],
                                   func=AF.Square,
                                   accum_out=s_stc[:, c0 + 7:c0 + 8]), sm_a,
                               waits=[tvs, x1_used3[1 - sl] if False else None])
                    tv1 = comp(p, "vector", lambda v, c0=c0: v.tensor_scalar(
                        s_stc[:, c0 + 6:c0 + 7], s_stc[:, c0 + 6:c0 + 7],
                        1.0 / H, None, ALU.mult), sm_v, waits=[tas])
                    tv2 = comp(p, "vector", lambda v, c0=c0: v.tensor_tensor(
                        s_stc[:, c0:c0 + 1], s_stc[:, c0 + 6:c0 + 7],
                        s_stc[:, c0 + 6:c0 + 7], ALU.mult), sm_v, waits=[tv1])
                    tv3 = comp(p, "vector", lambda v, c0=c0:
                               v.scalar_tensor_tensor(
                                   s_stc[:, c0 + 1:c0 + 2],
                                   s_stc[:, c0 + 7:c0 + 8], 1.0 / H,
                                   s_stc[:, c0:c0 + 1],
                                   ALU.mult, ALU.subtract), sm_v, waits=[tv2])
                    ta1 = comp(p, "scalar", lambda s, c0=c0: s.activation(
                        out=s_stc[:, c0 + 2:c0 + 3],
                        in_=s_stc[:, c0 + 1:c0 + 2], func=AF.Ln, bias=s_epscol[:, :]),
                        sm_a, waits=[tv3])
                    ta2 = comp(p, "scalar", lambda s, c0=c0: s.activation(
                        out=s_stc[:, c0 + 3:c0 + 4],
                        in_=s_stc[:, c0 + 2:c0 + 3], func=AF.Exp, scale=s_mhalf[:, :]),
                        sm_a, waits=[ta1])
                    tv4 = comp(p, "vector", lambda v, c0=c0: v.tensor_tensor(
                        s_stc[:, c0 + 4:c0 + 5], s_stc[:, c0 + 6:c0 + 7],
                        s_stc[:, c0 + 3:c0 + 4], ALU.mult), sm_v, waits=[ta2])
                    txo = comp(p, "vector", lambda v, otb=otb, c0=c0:
                               v.tensor_scalar(
                                   otb[:, :], otb[:, :],
                                   s_stc[:, c0 + 3:c0 + 4],
                                   s_stc[:, c0 + 4:c0 + 5],
                                   ALU.mult, ALU.subtract), sm_v, waits=[tv4])
                    ot_used[sl] = dma(
                        p, "scalar",
                        bass.AP(out, (sbase + tb * 128) * H, [[H, 128], [1, H]]),
                        otb[:, :], sm_o, waits=[txo])
                xq_free[spar] = p.cur(sm_t)
            fwait(p, "gpsimd", [p.cur(sm_o)])
            p.run_block()

    return nc


_NC_CACHE = {}


def kernel(**inputs):
    if 8192 not in _NC_CACHE:
        _NC_CACHE[8192] = build_program(8192)
    nc = _NC_CACHE[8192]
    hidden = np.ascontiguousarray(np.asarray(inputs["L0_attn_hidden"], np.float32))
    names = {
        "ln_w": "L0_attn_ln_w", "ln_b": "L0_attn_ln_b", "in_w": "L0_attn_in_w",
        "in_b": "L0_attn_in_b", "conv_w": "L0_attn_conv_w",
        "conv_b": "L0_attn_conv_b", "xp_w": "L0_attn_xp_w",
        "dtp_w": "L0_attn_dtp_w", "dtp_b": "L0_attn_dtp_b",
        "A_log": "L0_attn_A_log", "D": "L0_attn_D", "out_w": "L0_attn_out_w",
        "out_b": "L0_attn_out_b", "w_gate": "L0_ffn_w_gate",
        "w_up": "L0_ffn_w_up", "w_down": "L0_ffn_w_down",
    }
    common = {k: np.ascontiguousarray(np.asarray(inputs[v], np.float32))
              for k, v in names.items()}
    in_maps = []
    for b in range(hidden.shape[0]):
        m = dict(common)
        m["hidden"] = np.ascontiguousarray(hidden[b])
        in_maps.append(m)
    res = run_bass_kernel_spmd(nc, in_maps, core_ids=list(range(len(in_maps))))
    return np.stack([np.asarray(r["out"], np.float32) for r in res.results])


# version 08
# speedup vs baseline: 1.0563x; 1.0291x over previous
import numpy as np
from contextlib import ExitStack

import concourse.bass as bass
import concourse.mybir as mybir
from concourse.bass_utils import run_bass_kernel_spmd

F32 = mybir.dt.float32
BF16 = mybir.dt.bfloat16
AF = mybir.ActivationFunctionType
ALU = mybir.AluOpType
AX = mybir.AxisListType

H = 1024
IN = 2048
FF = 4096
NST = 16
RK = 64
KC = 4
NCH = IN // 128
NKH = H // 128
NFF = FF // 128
C_RND = 12582912.0
EPS = 1e-5

TA = 512
TB = 1024
TC = 1024
NGRP = 4


class P:
    def __init__(self, nc):
        self.nc = nc
        self.order = ("sync", "gpsimd", "vector", "scalar", "tensor")
        self.ops = {k: [] for k in self.order}
        self.cnt = {}
        self.eng_of = {}

    def sem(self, ctx, name, engine=None):
        s = ctx.enter_context(self.nc.semaphore(name))
        self.cnt[id(s)] = 0
        if engine:
            self.eng_of[id(s)] = engine
        return s

    def emit(self, engine, fn):
        self.ops[engine].append(fn)

    def tok(self, sem, amt):
        self.cnt[id(sem)] += amt
        return (sem, self.cnt[id(sem)])

    def cur(self, sem):
        return (sem, self.cnt[id(sem)])

    def run_block(self):
        with self.nc.Block() as block:
            for eng_name in self.order:
                ops = self.ops[eng_name]
                if not ops:
                    continue

                def body(eng, ops=list(ops)):
                    for fn in ops:
                        fn(eng)

                getattr(block, eng_name)(body)
        self.ops = {k: [] for k in self.order}


def dma(p, queue, out_ap, in_ap, sem, waits=(), slow=False):
    waits = [w for w in waits if w is not None]
    tok = p.tok(sem, 16)

    def fn(eng, out_ap=out_ap, in_ap=in_ap, waits=tuple(waits)):
        for (ws, wv) in waits:
            eng.wait_ge(ws, wv)
        kw = {"allow_slow_non_contiguous": True} if slow else {}
        eng.dma_start(out=out_ap, in_=in_ap, **kw).then_inc(tok[0], 16)

    p.emit(queue, fn)
    return tok


FILTER_ENGINES = ()


def comp(p, engine, build, sem=None, waits=(), fe=False):
    waits = [w for w in waits
             if w is not None and not (fe and
                                       p.eng_of.get(id(w[0])) == engine)]
    tok = p.tok(sem, 1) if sem is not None else None

    def fn(eng, waits=tuple(waits)):
        for (ws, wv) in waits:
            eng.wait_ge(ws, wv)
        inst = build(eng)
        if tok is not None and inst is not None:
            inst.then_inc(tok[0], 1)

    p.emit(engine, fn)
    return tok


def fwait(p, engine, waits):
    def fn(eng, waits=tuple(waits)):
        for (ws, wv) in waits:
            eng.wait_ge(ws, wv)

    p.emit(engine, fn)


def vgather(t, ncols):
    return bass.AP(t, 0, [[1, 128], [128, ncols]])


def build_program(L=8192, debug=False):
    nc = bass.Bass("TRN2", target_bir_lowering=False, debug=False)
    NT_A = L // TA
    NT_B = L // TB
    NT_C = L // TC

    hid = nc.dram_tensor("hidden", [L, H], F32, kind="ExternalInput")
    ln_w = nc.dram_tensor("ln_w", [H], F32, kind="ExternalInput")
    ln_b = nc.dram_tensor("ln_b", [H], F32, kind="ExternalInput")
    in_w = nc.dram_tensor("in_w", [2 * IN, H], F32, kind="ExternalInput")
    in_b = nc.dram_tensor("in_b", [2 * IN], F32, kind="ExternalInput")
    conv_w = nc.dram_tensor("conv_w", [IN, 1, KC], F32, kind="ExternalInput")
    conv_b = nc.dram_tensor("conv_b", [IN], F32, kind="ExternalInput")
    xp_w = nc.dram_tensor("xp_w", [RK + 2 * NST, IN], F32, kind="ExternalInput")
    dtp_w = nc.dram_tensor("dtp_w", [IN, RK], F32, kind="ExternalInput")
    dtp_b = nc.dram_tensor("dtp_b", [IN], F32, kind="ExternalInput")
    A_log = nc.dram_tensor("A_log", [IN, NST], F32, kind="ExternalInput")
    D_vec = nc.dram_tensor("D", [IN], F32, kind="ExternalInput")
    out_w = nc.dram_tensor("out_w", [H, IN], F32, kind="ExternalInput")
    out_b = nc.dram_tensor("out_b", [H], F32, kind="ExternalInput")
    w_gate = nc.dram_tensor("w_gate", [FF, H], F32, kind="ExternalInput")
    w_up = nc.dram_tensor("w_up", [FF, H], F32, kind="ExternalInput")
    w_down = nc.dram_tensor("w_down", [H, FF], F32, kind="ExternalInput")
    out = nc.dram_tensor("out", [L, H], F32, kind="ExternalOutput")

    dk = "ExternalOutput" if debug else "Internal"
    dt_s = nc.dram_tensor("dt_s", [NCH, 128, L], F32, kind=dk)
    dtx_s = nc.dram_tensor("dtx_s", [NCH, 128, L], F32, kind=dk)
    ys_s = nc.dram_tensor("ys_s", [NCH, 128, L], BF16, kind=dk)
    g_s = nc.dram_tensor("g_s", [NCH, 128, L], BF16, kind=dk)
    x1_s = nc.dram_tensor("x1_s", [L, H], F32, kind=dk)
    hp_s = nc.dram_tensor("hp_s", [2, TC, FF], F32, kind=dk)
    bcT_s = nc.dram_tensor("bcT_s", [32, L], F32, kind=dk)
    wgT_s = nc.dram_tensor("wgT_s", [NKH, 128, FF], BF16, kind=dk)
    if debug:
        inwT_d = nc.dram_tensor("inwT_d", [128, NKH * 2 * IN], BF16,
                                kind="ExternalOutput")
        btld_d = nc.dram_tensor("btld_d", [128, 32], F32, kind="ExternalOutput")
        mw_d = nc.dram_tensor("mw_d", [128, 3], F32, kind="ExternalOutput")
    wuT_s = nc.dram_tensor("wuT_s", [NKH, 128, FF], BF16, kind=dk)
    wdT_s = nc.dram_tensor("wdT_s", [NFF, 128, H], BF16, kind=dk)

    with ExitStack() as per:
        def sbp(name, shape, dt=F32):
            return per.enter_context(nc.sbuf_tensor(name, shape, dt))

        s_id = sbp("s_id", [128, 128], BF16)
        s_lnw = sbp("s_lnw", [128, NKH])
        s_lnb_bf = sbp("s_lnb_bf", [128, NKH], BF16)
        s_inb = sbp("s_inb", [128, 32])
        s_btld = sbp("s_btld", [128, 32])
        s_cw = sbp("s_cw", [128, NCH * KC])
        s_cb = sbp("s_cb", [128, NCH])
        s_dtpb = sbp("s_dtpb", [128, NCH])
        s_dcol = sbp("s_dcol", [128, NCH])
        s_acol = sbp("s_acol", [128, NCH * NST])
        s_mw = sbp("s_mw", [128, 3])
        s_outb_row = sbp("s_outb_row", [1, H], BF16)
        s_ones_bf = sbp("s_ones_bf", [1, 128], BF16)
        s_ones_f32 = sbp("s_ones_f32", [1, 128])
        s_onecol = sbp("s_onecol", [128, 1])
        s_epscol = sbp("s_epscol", [128, 1])
        s_mhalf = sbp("s_mhalf", [128, 1])
        s_carry = sbp("s_carry", [128, NCH * NST])
        s_halo = sbp("s_halo", [128, NCH * (KC - 1)])

        p = P(nc)
        sm_ld = p.sem(per, "ld")
        sm_st = p.sem(per, "st")
        sm_v = p.sem(per, "v", engine="vector")
        sm_a = p.sem(per, "a", engine="scalar")
        sm_t = p.sem(per, "t", engine="tensor")
        sm_q = p.sem(per, "q", engine="gpsimd")
        sm_o = p.sem(per, "o")

        with ExitStack() as ph:
            t_iop = ph.enter_context(nc.sbuf_tensor("t_iop", [128, 128], F32))
            t_iof = ph.enter_context(nc.sbuf_tensor("t_iof", [128, 128], F32))
            t_al = ph.enter_context(nc.sbuf_tensor("t_al", [128, NST], F32))
            t_lnb = ph.enter_context(nc.sbuf_tensor("t_lnb", [128, NKH], F32))
            t_ob = ph.enter_context(nc.sbuf_tensor("t_ob", [1, H], F32))

            def g_init(g):
                g.iota(t_iop[:, :], [[0, 128]], channel_multiplier=1,
                       allow_small_or_imprecise_dtypes=True)
                g.iota(t_iof[:, :], [[1, 128]], channel_multiplier=0,
                       allow_small_or_imprecise_dtypes=True)
                g.memset(s_ones_bf[:, :], 1.0)
                g.memset(s_ones_f32[:, :], 1.0)
                g.memset(s_onecol[:, :], 1.0)
                g.memset(s_epscol[:, :], 1e-5)
                g.memset(s_mhalf[:, :], -0.5)
                g.memset(s_carry[:, :], 0.0)
                return g.memset(s_halo[:, :], 0.0)

            tg = comp(p, "gpsimd", g_init, sm_q)
            tl2 = dma(p, "sync", t_lnb[:, :], vgather(ln_b, NKH), sm_ld, slow=True)
            dma(p, "sync", s_lnw[:, :], vgather(ln_w, NKH), sm_ld, slow=True)
            dma(p, "sync", s_inb[:, :], vgather(in_b, 32), sm_ld, slow=True)
            dma(p, "sync", s_cb[:, :], vgather(conv_b, NCH), sm_ld, slow=True)
            dma(p, "sync", s_dtpb[:, :], vgather(dtp_b, NCH), sm_ld, slow=True)
            dma(p, "sync", s_dcol[:, :], vgather(D_vec, NCH), sm_ld, slow=True)
            dma(p, "sync", s_cw[:, :],
                bass.AP(conv_w, 0, [[KC, 128], [KC * 128, NCH], [1, KC]]),
                sm_ld, slow=True)
            tob = dma(p, "sync", t_ob[:, :],
                      bass.AP(out_b, 0, [[H, 1], [1, H]]), sm_ld)
            comp(p, "vector", lambda v: v.tensor_tensor(
                s_id[:, :], t_iop[:, :], t_iof[:, :], ALU.is_equal), sm_v,
                waits=[tg])
            comp(p, "vector", lambda v: v.tensor_copy(
                s_lnb_bf[:, :], t_lnb[:, :]), sm_v, waits=[tl2])
            comp(p, "vector", lambda v: v.tensor_copy(
                s_outb_row[:, :], t_ob[:, :]), sm_v, waits=[tob])
            for ch in range(NCH):
                tok = dma(p, "sync", t_al[:, :],
                          bass.AP(A_log, ch * 128 * NST, [[NST, 128], [1, NST]]),
                          sm_ld, waits=[p.cur(sm_a)] if ch else [])
                ta = comp(p, "scalar", lambda s, ch=ch: s.activation(
                    out=s_acol[:, ch * NST:(ch + 1) * NST], in_=t_al[:, :],
                    func=AF.Exp), sm_a, waits=[tok])
                comp(p, "vector", lambda v, ch=ch: v.tensor_scalar(
                    s_acol[:, ch * NST:(ch + 1) * NST],
                    s_acol[:, ch * NST:(ch + 1) * NST], -1.0, None, ALU.mult),
                    sm_v, waits=[ta])
            p.run_block()

        with ExitStack() as phA:
            s_inwT = phA.enter_context(
                nc.sbuf_tensor("s_inwT", [128, NKH * 2 * IN], BF16))

            with ExitStack() as ph:
                t_w = [ph.enter_context(nc.sbuf_tensor(f"t_w{i}", [128, FF], F32))
                       for i in range(2)]
                t_wq = ph.enter_context(nc.sbuf_tensor("t_wq", [128, FF], BF16))
                t_red = [ph.enter_context(nc.sbuf_tensor(f"t_red{i}", [128, 1], F32))
                         for i in range(3)]
                t_sc = ph.enter_context(nc.sbuf_tensor("t_sc", [1, 4], F32))
                t_scb = ph.enter_context(nc.sbuf_tensor("t_scb", [128, 1], F32))
                t_ev = [ph.enter_context(nc.sbuf_tensor(f"t_ev{i}", [128, 512], BF16))
                        for i in range(2)]
                p_tr = [ph.enter_context(nc.psum_tensor(f"p_tr{i}", [128, 512], BF16))
                        for i in range(2)]
                p_sc = ph.enter_context(nc.psum_tensor("p_sc", [128, 4], F32))

                def quant_matrix(w_t, rows, cols, wT_s, mw_idx):
                    nchr = rows // 128
                    for cch in range(nchr):
                        buf = t_w[cch % 2]
                        tok = dma(p, "sync", buf[:, :cols],
                                  bass.AP(w_t, cch * 128 * cols,
                                          [[cols, 128], [1, cols]]), sm_ld,
                                  waits=[p.cur(sm_v), p.cur(sm_t)]
                                  if cch < 2 else [p.cur(sm_v)])
                        red = t_red[cch % 2]
                        tr = comp(p, "vector", lambda v, buf=buf, red=red, cols=cols:
                                  v.reduce_sum(red[:, :], buf[:, :cols], axis=AX.X,
                                               apply_absolute_value=True),
                                  sm_v, waits=[tok])
                        if cch == 0:
                            comp(p, "vector", lambda v, red=red: v.tensor_copy(
                                t_red[2][:, :], red[:, :]), sm_v,
                                waits=[tr, p.cur(sm_t)])
                        else:
                            comp(p, "vector", lambda v, red=red: v.tensor_add(
                                t_red[2][:, :], t_red[2][:, :], red[:, :]), sm_v,
                                waits=[tr])
                    tm = comp(p, "tensor", lambda t: t.matmul(
                        p_sc[:1, 0:1], t_red[2][:, :], s_onecol[:, :],
                        start=True, stop=True), sm_t, waits=[p.cur(sm_v)])
                    tv1 = comp(p, "vector", lambda v, rows=rows, cols=cols:
                               v.tensor_scalar(
                                   t_sc[:1, 0:1], p_sc[:1, 0:1],
                                   1.0 / (rows * cols), 1e-5, ALU.mult, ALU.max),
                               sm_v, waits=[tm])
                    tv2 = comp(p, "vector", lambda v: v.reciprocal(
                        t_sc[:1, 1:2], t_sc[:1, 0:1]), sm_v, waits=[tv1])
                    tb = comp(p, "tensor", lambda t: t.matmul(
                        p_sc[:, 0:2], s_ones_f32[:1, :], t_sc[:1, 0:2],
                        start=True, stop=True), sm_t, waits=[tv2])
                    tvb = comp(p, "vector", lambda v: v.tensor_copy(
                        t_scb[:, :], p_sc[:, 1:2]), sm_v, waits=[tb])
                    comp(p, "vector", lambda v, mw_idx=mw_idx: v.tensor_copy(
                        s_mw[:, mw_idx:mw_idx + 1], p_sc[:, 0:1]), sm_v)
                    for cch in range(nchr):
                        buf = t_w[cch % 2]
                        tok = dma(p, "sync", buf[:, :cols],
                                  bass.AP(w_t, cch * 128 * cols,
                                          [[cols, 128], [1, cols]]), sm_ld,
                                  waits=[p.cur(sm_v)])
                        tq1 = comp(p, "vector", lambda v, buf=buf, cols=cols:
                                   v.tensor_scalar(
                                       buf[:, :cols], buf[:, :cols], t_scb[:, :],
                                       C_RND, ALU.mult, ALU.add), sm_v,
                                   waits=[tok, tvb, p.cur(sm_t)])
                        tq2 = comp(p, "vector", lambda v, buf=buf, cols=cols:
                                   v.tensor_scalar(
                                       buf[:, :cols], buf[:, :cols], C_RND, -1.0,
                                       ALU.subtract, ALU.max), sm_v, waits=[tq1])
                        tq3 = comp(p, "vector", lambda v, buf=buf, cols=cols:
                                   v.tensor_scalar(
                                       t_wq[:, :cols], buf[:, :cols], 1.0, None,
                                       ALU.min), sm_v, waits=[tq2])
                        for grp in range(cols // 512):
                            pt = p_tr[grp % 2]
                            tt = None
                            for b4 in range(4):
                                bd = grp * 4 + b4
                                tt = comp(p, "tensor",
                                          lambda t, bd=bd, b4=b4, pt=pt: t.transpose(
                                              pt[:, b4 * 128:(b4 + 1) * 128],
                                              t_wq[:, bd * 128:(bd + 1) * 128],
                                              s_id[:, :]),
                                          sm_t if b4 == 3 else None,
                                          waits=[tq3, p.cur(sm_st)]
                                          if b4 == 0 else [])
                            ev = t_ev[grp % 2]
                            ta = comp(p, "scalar", lambda s, ev=ev, pt=pt:
                                      s.activation(out=ev[:, :], in_=pt[:, :],
                                                   func=AF.Identity),
                                      sm_a, waits=[tt])
                            dma(p, "scalar",
                                bass.AP(wT_s, grp * 4 * 128 * rows + cch * 128,
                                        [[rows, 128], [128 * rows, 4], [1, 128]]),
                                ev[:, :], sm_st, waits=[ta])

                quant_matrix(w_gate, FF, H, wgT_s, 0)
                quant_matrix(w_up, FF, H, wuT_s, 1)
                quant_matrix(w_down, H, FF, wdT_s, 2)

                for cch in range(2 * IN // 128):
                    buf = t_w[cch % 2]
                    tok = dma(p, "sync", buf[:, :H],
                              bass.AP(in_w, cch * 128 * H, [[H, 128], [1, H]]),
                              sm_ld, waits=[p.cur(sm_v)])
                    tc = comp(p, "vector", lambda v, buf=buf: v.tensor_copy(
                        t_wq[:, :H], buf[:, :H]), sm_v,
                        waits=[tok, p.cur(sm_t)])
                    for grp in range(NKH // 4):
                        pt = p_tr[grp % 2]
                        tt = None
                        for b4 in range(4):
                            bd = grp * 4 + b4
                            tt = comp(p, "tensor",
                                      lambda t, bd=bd, b4=b4, pt=pt: t.transpose(
                                          pt[:, b4 * 128:(b4 + 1) * 128],
                                          t_wq[:, bd * 128:(bd + 1) * 128],
                                          s_id[:, :]),
                                      sm_t if b4 == 3 else None,
                                      waits=[tc, p.cur(sm_a)] if b4 == 0 else [])
                        for b4 in range(4):
                            bd = grp * 4 + b4
                            comp(p, "scalar",
                                 lambda s, bd=bd, b4=b4, pt=pt, cch=cch:
                                 s.activation(
                                     out=s_inwT[:, bd * 2 * IN + cch * 128:
                                                bd * 2 * IN + (cch + 1) * 128],
                                     in_=pt[:, b4 * 128:(b4 + 1) * 128],
                                     func=AF.Identity),
                                 sm_a, waits=[tt] if b4 == 0 else [])
                for ob in range(32):
                    tprev = None
                    for bd in range(NKH):
                        tprev = comp(p, "tensor", lambda t, ob=ob, bd=bd: t.matmul(
                            p_sc[:, 2:3],
                            s_inwT[:, bd * 2 * IN + ob * 128:
                                   bd * 2 * IN + ob * 128 + 128],
                            s_lnb_bf[:, bd:bd + 1],
                            start=(bd == 0), stop=(bd == NKH - 1)),
                            sm_t if bd == NKH - 1 else None,
                            waits=[p.cur(sm_a), p.cur(sm_v)] if bd == 0 else [])
                    tvb2 = comp(p, "vector", lambda v, ob=ob: v.tensor_tensor(
                        s_btld[:, ob:ob + 1], p_sc[:, 2:3], s_inb[:, ob:ob + 1],
                        ALU.add), sm_v, waits=[tprev])
                    fwait(p, "tensor", [tvb2])
                for bd in range(NKH):
                    comp(p, "vector", lambda v, bd=bd: v.tensor_scalar(
                        s_inwT[:, bd * 2 * IN:(bd + 1) * 2 * IN],
                        s_inwT[:, bd * 2 * IN:(bd + 1) * 2 * IN],
                        s_lnw[:, bd:bd + 1], None, ALU.mult), sm_v,
                        waits=[p.cur(sm_t)] if bd == 0 else [])
                if debug:
                    dma(p, "scalar", inwT_d.ap(), s_inwT[:, :], sm_st,
                        waits=[p.cur(sm_v)])
                    dma(p, "scalar", btld_d.ap(), s_btld[:, :], sm_st,
                        waits=[p.cur(sm_v)])
                    dma(p, "scalar", mw_d.ap(), s_mw[:, :], sm_st,
                        waits=[p.cur(sm_v)])
                    fwait(p, "scalar", [p.cur(sm_st)])
                p.run_block()

            with ExitStack() as ph:
                s_xpT = ph.enter_context(
                    nc.sbuf_tensor("s_xpT", [128, NCH * 96], BF16))
                s_dtpT = ph.enter_context(
                    nc.sbuf_tensor("s_dtpT", [64, NCH * 128], BF16))
                NB = TA // 128
                s_h = [ph.enter_context(
                    nc.sbuf_tensor(f"s_h{i}", [128, NB * H], F32))
                    for i in range(2)]
                s_hn2 = [ph.enter_context(
                    nc.sbuf_tensor(f"s_hn{i}", [128, NB * H], BF16))
                    for i in range(2)]
                s_hn = s_hn2[0]
                s_hnT2 = [ph.enter_context(
                    nc.sbuf_tensor(f"s_hnT{i}", [128, NKH * TA], BF16))
                    for i in range(2)]
                s_hnT = s_hnT2[0]
                s_xc = ph.enter_context(
                    nc.sbuf_tensor("s_xc", [128, NCH * (TA + 3)], F32))
                s_par = ph.enter_context(nc.sbuf_tensor("s_par", [96, TA], F32))
                s_drb = ph.enter_context(nc.sbuf_tensor("s_drb", [64, TA], BF16))
                s_stat = ph.enter_context(
                    nc.sbuf_tensor("s_stat", [128, 8 * NB], F32))
                s_sq = ph.enter_context(nc.sbuf_tensor("s_sq", [128, H], F32))
                s_ca = [ph.enter_context(
                    nc.sbuf_tensor(f"s_ca{i}", [128, TA], F32)) for i in range(2)]
                s_xcb = [ph.enter_context(
                    nc.sbuf_tensor(f"s_xcb{i}", [128, TA], BF16)) for i in range(2)]
                s_dt = [ph.enter_context(
                    nc.sbuf_tensor(f"s_dt{i}", [128, TA], F32)) for i in range(2)]
                s_z = [ph.enter_context(
                    nc.sbuf_tensor(f"s_z{i}", [128, TA], F32)) for i in range(2)]
                s_sp = [ph.enter_context(
                    nc.sbuf_tensor(f"s_sp{i}", [128, TA], F32)) for i in range(2)]
                s_gt = [ph.enter_context(
                    nc.sbuf_tensor(f"s_gt{i}", [128, TA], BF16)) for i in range(2)]
                s_dx = [ph.enter_context(
                    nc.sbuf_tensor(f"s_dx{i}", [128, TA], BF16)) for i in range(2)]
                p_mm = [ph.enter_context(
                    nc.psum_tensor(f"p_mm{i}", [128, TA], F32)) for i in range(2)]
                p_xp = ph.enter_context(nc.psum_tensor("p_xp", [96, TA], F32))
                p_dt = [ph.enter_context(
                    nc.psum_tensor(f"p_dt{i}", [128, TA], F32)) for i in range(2)]
                p_t2 = ph.enter_context(nc.psum_tensor("p_t2", [128, 512], BF16))

                t_xw = s_h[0]
                t_xq2 = s_h[1]
                tok = dma(p, "sync", t_xw[:96, :IN],
                          bass.AP(xp_w, 0, [[IN, 96], [1, IN]]), sm_ld)
                tcx = comp(p, "vector", lambda v: v.tensor_copy(
                    s_hn[:96, :IN], t_xw[:96, :IN]), sm_v, waits=[tok])
                for ch in range(NCH):
                    sl4 = (ch % 2) * 128
                    tt = comp(p, "tensor", lambda t, ch=ch, sl4=sl4: t.transpose(
                        p_t2[:, sl4:sl4 + 96],
                        s_hn[:96, ch * 128:(ch + 1) * 128], s_id[:96, :96]),
                        sm_t, waits=[tcx, p.cur(sm_a)])
                    comp(p, "scalar", lambda s, ch=ch, sl4=sl4: s.activation(
                        out=s_xpT[:, ch * 96:(ch + 1) * 96],
                        in_=p_t2[:, sl4:sl4 + 96], func=AF.Identity), sm_a,
                        waits=[tt])
                tdw = dma(p, "sync", t_xq2[:, :RK * NCH],
                          bass.AP(dtp_w, 0, [[RK, 128], [128 * RK, NCH], [1, RK]]),
                          sm_ld)
                tcd = comp(p, "vector", lambda v: v.tensor_copy(
                    s_hn[:, :RK * NCH].bitcast(BF16)
                    if False else s_hn[:, :RK * NCH], t_xq2[:, :RK * NCH]),
                    sm_v, waits=[tdw, p.cur(sm_t)])
                for ch in range(NCH):
                    sl4 = (ch % 2) * 128
                    tt = comp(p, "tensor", lambda t, ch=ch, sl4=sl4: t.transpose(
                        p_t2[:RK, sl4:sl4 + 128],
                        s_hn[:, ch * RK:(ch + 1) * RK], s_id[:, :]),
                        sm_t, waits=[tcd, p.cur(sm_a)])
                    comp(p, "scalar", lambda s, ch=ch, sl4=sl4: s.activation(
                        out=s_dtpT[:, ch * 128:(ch + 1) * 128],
                        in_=p_t2[:RK, sl4:sl4 + 128], func=AF.Identity), sm_a,
                        waits=[tt])
                p.run_block()

                h_used = [None, None]
                cons_x = {}
                g_used = [None, None]
                cons_hn = [None, None]
                cons_hnT = [None, None]
                xp_evac = None
                dtp_last = None
                evac_tok = {}
                for it in range(NT_A):
                    base = it * TA
                    s_hn = s_hn2[it % 2]
                    s_hnT = s_hnT2[it % 2]
                    hb = s_h[it % 2]
                    htok = dma(
                        p, "sync", hb[:, :],
                        bass.AP(hid, base * H, [[H, 128], [128 * H, NB], [1, H]]),
                        sm_ld, waits=[h_used[it % 2]])
                    hn_toks = []
                    red_toks = []
                    sq_toks = []
                    for j in range(NB):
                        hbj = hb[:, j * H:(j + 1) * H]
                        c0 = 8 * j
                        red_toks.append(comp(
                            p, "vector", lambda v, hbj=hbj, c0=c0:
                            v.reduce_sum(s_stat[:, c0:c0 + 1], hbj, axis=AX.X),
                            sm_v, waits=[htok] + ([cons_hn[it % 2]] if j == 0 else [])))
                    for j in range(NB):
                        hbj = hb[:, j * H:(j + 1) * H]
                        c0 = 8 * j
                        sq_toks.append(comp(
                            p, "scalar", lambda s, hbj=hbj, c0=c0:
                            s.activation(out=s_sq[:, :], in_=hbj,
                                         func=AF.Square,
                                         accum_out=s_stat[:, c0 + 1:c0 + 2]),
                            sm_a, waits=[htok] if j else [htok, p.cur(sm_v)]))
                    def sv(col):
                        return bass.AP(s_stat, col, [[8 * NB, 128], [8, NB]])
                    tv1 = comp(p, "vector", lambda v: v.tensor_scalar(
                        sv(0), sv(0), 1.0 / H, None, ALU.mult), sm_v,
                        waits=[sq_toks[-1], red_toks[-1]])
                    tv2 = comp(p, "vector", lambda v: v.tensor_tensor(
                        sv(2), sv(0), sv(0), ALU.mult), sm_v, waits=[tv1])
                    tv3 = comp(p, "vector", lambda v: v.scalar_tensor_tensor(
                        sv(3), sv(1), 1.0 / H, sv(2), ALU.mult, ALU.subtract),
                        sm_v, waits=[tv2])
                    ta1 = comp(p, "scalar", lambda s: s.activation(
                        out=sv(4), in_=sv(3), func=AF.Ln,
                        bias=s_epscol[:, :]), sm_a, waits=[tv3])
                    ta2 = comp(p, "scalar", lambda s: s.activation(
                        out=sv(5), in_=sv(4), func=AF.Exp,
                        scale=s_mhalf[:, :]), sm_a, waits=[ta1])
                    tv4 = comp(p, "vector", lambda v: v.tensor_tensor(
                        sv(6), sv(0), sv(5), ALU.mult), sm_v, waits=[ta2])
                    for j in range(NB):
                        hbj = hb[:, j * H:(j + 1) * H]
                        c0 = 8 * j
                        hn_toks.append(comp(
                            p, "vector", lambda v, hbj=hbj, j=j, c0=c0, s_hn=s_hn:
                            v.tensor_scalar(
                                s_hn[:, j * H:(j + 1) * H], hbj,
                                s_stat[:, c0 + 5:c0 + 6],
                                s_stat[:, c0 + 6:c0 + 7],
                                ALU.mult, ALU.subtract), sm_v,
                            waits=[tv4, cons_hn[it % 2]] if j == 0 else []))
                    h_used[it % 2] = hn_toks[-1]
                    for j in range(NB):
                        for g2 in range(2):
                            tt = None
                            for b4 in range(4):
                                bd = g2 * 4 + b4
                                k = j * NKH + bd
                                w = []
                                if b4 == 0:
                                    w = [hn_toks[j], p.cur(sm_a),
                                         cons_hnT[it % 2] if k == 0 else None]
                                tt = comp(
                                    p, "tensor",
                                    lambda t, j=j, bd=bd, b4=b4, s_hn=s_hn:
                                    t.transpose(
                                        p_t2[:, b4 * 128:(b4 + 1) * 128],
                                        s_hn[:, j * H + bd * 128:
                                             j * H + (bd + 1) * 128],
                                        s_id[:, :]),
                                    sm_t if b4 == 3 else None, waits=w)
                            evac_tok[("hnT", it, j, g2)] = comp(
                                p, "scalar", lambda s, j=j, g2=g2, s_hnT=s_hnT:
                                s.activation(
                                    out=bass.AP(
                                        s_hnT, (g2 * 4) * TA + j * 128,
                                        [[NKH * TA, 128], [TA, 4], [1, 128]]),
                                    in_=p_t2[:, :], func=AF.Identity),
                                sm_a, waits=[tt])
                    cons_hn[it % 2] = p.cur(sm_t)
                    hnT_ready = evac_tok[("hnT", it, NB - 1, 1)]
                    for ob in range(32):
                        pm = p_mm[ob % 2]
                        tmm = None
                        for bd in range(NKH):
                            w = []
                            if bd == 0:
                                w = [hnT_ready, evac_tok.get(("ev", it, ob - 2))]
                            tmm = comp(p, "tensor",
                                       lambda t, ob=ob, bd=bd, pm=pm,
                                       s_hnT=s_hnT: t.matmul(
                                           pm[:, :],
                                           s_inwT[:, bd * 2 * IN + ob * 128:
                                                  bd * 2 * IN + ob * 128 + 128],
                                           s_hnT[:, bd * TA:(bd + 1) * TA],
                                           start=(bd == 0), stop=(bd == NKH - 1)),
                                       sm_t if bd == NKH - 1 else None, waits=w)
                        if ob < 16:
                            ch = ob
                            evac_tok[("ev", it, ob)] = comp(
                                p, "scalar", lambda s, ch=ch, pm=pm, ob=ob:
                                s.activation(
                                    out=s_xc[:, ch * (TA + 3) + 3:
                                             (ch + 1) * (TA + 3)],
                                    in_=pm[:, :], func=AF.Identity,
                                    bias=s_btld[:, ob:ob + 1]), sm_a,
                                waits=[tmm, cons_x.get(ch)])
                        else:
                            ch = ob - 16
                            gb = s_gt[ch % 2]
                            tev = comp(
                                p, "scalar", lambda s, gb=gb, pm=pm, ob=ob:
                                s.activation(out=gb[:, :], in_=pm[:, :],
                                             func=AF.Silu,
                                             bias=s_btld[:, ob:ob + 1]), sm_a,
                                waits=[tmm, g_used[ch % 2]])
                            evac_tok[("ev", it, ob)] = tev
                            g_used[ch % 2] = dma(
                                p, "scalar",
                                bass.AP(g_s, ch * 128 * L + base,
                                        [[L, 128], [1, TA]]),
                                gb[:, :], sm_st, waits=[tev])
                    for ch in range(NCH):
                        xs = ch * (TA + 3)
                        ev = evac_tok[("ev", it, ch)]
                        th0 = comp(p, "vector", lambda v, ch=ch, xs=xs:
                                   v.tensor_copy(s_xc[:, xs:xs + 3],
                                                 s_halo[:, ch * 3:(ch + 1) * 3]),
                                   sm_v, waits=[ev])
                        th1 = comp(p, "vector", lambda v, ch=ch, xs=xs:
                                   v.tensor_copy(s_halo[:, ch * 3:(ch + 1) * 3],
                                                 s_xc[:, xs + TA:xs + TA + 3]),
                                   sm_v, waits=[th0])
                        tcv = comp(p, "vector", lambda v, ch=ch, xs=xs:
                                   v.tensor_scalar(
                                       s_ca[0][:, :], s_xc[:, xs:xs + TA],
                                       s_cw[:, ch * KC:ch * KC + 1], None,
                                       ALU.mult), sm_v, waits=[th1])
                        for k in range(1, KC):
                            src = s_ca[(k - 1) % 2]
                            dst = s_ca[k % 2]
                            tcv = comp(p, "vector",
                                       lambda v, ch=ch, xs=xs, k=k, src=src,
                                       dst=dst:
                                       v.scalar_tensor_tensor(
                                           dst[:, :], s_xc[:, xs + k:xs + k + TA],
                                           s_cw[:, ch * KC + k:ch * KC + k + 1],
                                           src[:, :], ALU.mult, ALU.add),
                                       sm_v, waits=[tcv])
                        tsl = comp(p, "scalar", lambda s, ch=ch, xs=xs:
                                   s.activation(
                                       out=s_xc[:, xs + 3:xs + 3 + TA],
                                       in_=s_ca[(KC - 1) % 2][:, :],
                                       func=AF.Silu, bias=s_cb[:, ch:ch + 1]),
                                   sm_a, waits=[tcv])
                        xb = s_xcb[ch % 2]
                        tcb = comp(p, "vector", lambda v, ch=ch, xs=xs, xb=xb:
                                   v.tensor_copy(xb[:, :],
                                                 s_xc[:, xs + 3:xs + 3 + TA]),
                                   sm_v, waits=[tsl])
                        comp(p, "tensor", lambda t, ch=ch, xb=xb: t.matmul(
                            p_xp[:, :], s_xpT[:, ch * 96:(ch + 1) * 96],
                            xb[:, :], start=(ch == 0), stop=(ch == NCH - 1)),
                            sm_t if ch == NCH - 1 else None,
                            waits=[tcb] + ([xp_evac] if ch == 0 else []))
                        dxb = s_dx[ch % 2]
                        tdx = comp(p, "vector", lambda v, ch=ch, xs=xs, dxb=dxb:
                                   v.tensor_scalar(
                                       dxb[:, :], s_xc[:, xs + 3:xs + 3 + TA],
                                       s_dcol[:, ch:ch + 1], None, ALU.mult),
                                   sm_v, waits=[tsl])
                        dma(p, "scalar",
                            bass.AP(ys_s, ch * 128 * L + base,
                                    [[L, 128], [1, TA]]),
                            dxb[:, :], sm_st, waits=[tdx])
                    xp_mm_tok = p.cur(sm_t)
                    xp_evac = comp(p, "scalar", lambda s: s.activation(
                        out=s_par[:, :], in_=p_xp[:, :], func=AF.Identity),
                        sm_a, waits=[xp_mm_tok, dtp_last])
                    dma(p, "sync",
                        bass.AP(bcT_s, base, [[L, 32], [1, TA]]),
                        bass.AP(s_par, 64 * TA, [[TA, 32], [1, TA]]),
                        sm_ld, waits=[xp_evac])
                    tdr = comp(p, "vector", lambda v: v.tensor_copy(
                        s_drb[:, :], s_par[:64, :]), sm_v, waits=[xp_evac])
                    for ch in range(NCH):
                        pd = p_dt[ch % 2]
                        zb = s_z[ch % 2]
                        spb = s_sp[ch % 2]
                        db = s_dt[ch % 2]
                        tdm = comp(p, "tensor", lambda t, ch=ch, pd=pd: t.matmul(
                            pd[:, :], s_dtpT[:, ch * 128:(ch + 1) * 128],
                            s_drb[:, :], start=True, stop=True), sm_t,
                            waits=[tdr] + ([evac_tok.get(("z", it, ch - 2))]
                                           if ch >= 2 else []))
                        tz = comp(p, "vector", lambda v, pd=pd, ch=ch, zb=zb:
                                  v.tensor_scalar(zb[:, :], pd[:, :],
                                                  s_dtpb[:, ch:ch + 1], None,
                                                  ALU.add), sm_v, waits=[tdm])
                        evac_tok[("z", it, ch)] = tz
                        nzb = s_ca[ch % 2]
                        tsp0 = comp(p, "vector", lambda v, zb=zb, nzb=nzb:
                                    v.tensor_scalar(nzb[:, :], zb[:, :], -1.0,
                                                    None, ALU.mult),
                                    sm_v, waits=[tz])
                        tsp1 = comp(p, "vector", lambda v, zb=zb, spb=spb,
                                    nzb=nzb:
                                    v.tensor_tensor(spb[:, :], zb[:, :],
                                                    nzb[:, :], ALU.min),
                                    sm_v, waits=[tsp0])
                        tsp2 = comp(p, "scalar", lambda s, spb=spb: s.activation(
                            out=spb[:, :], in_=spb[:, :], func=AF.Exp), sm_a,
                            waits=[tsp1])
                        tsp3 = comp(p, "scalar", lambda s, spb=spb: s.activation(
                            out=spb[:, :], in_=spb[:, :], func=AF.Ln, bias=1.0),
                            sm_a, waits=[tsp2])
                        tsp4 = comp(p, "vector",
                                    lambda v, zb=zb, spb=spb, db=db:
                                    v.scalar_tensor_tensor(
                                        db[:, :], zb[:, :], 0.0, spb[:, :],
                                        ALU.max, ALU.add), sm_v, waits=[tsp3])
                        dma(p, "scalar",
                            bass.AP(dt_s, ch * 128 * L + base,
                                    [[L, 128], [1, TA]]),
                            db[:, :], sm_st, waits=[tsp4])
                        xs = ch * (TA + 3)
                        tdtx = comp(p, "vector",
                                    lambda v, db=db, ch=ch, xs=xs, zb=zb:
                                    v.tensor_tensor(
                                        zb[:, :], db[:, :],
                                        s_xc[:, xs + 3:xs + 3 + TA], ALU.mult),
                                    sm_v, waits=[tsp4])
                        cons_x[ch] = tdtx
                        dma(p, "scalar",
                            bass.AP(dtx_s, ch * 128 * L + base,
                                    [[L, 128], [1, TA]]),
                            zb[:, :], sm_st, waits=[tdtx])
                    dtp_last = p.cur(sm_v)
                    cons_hnT[it % 2] = p.cur(sm_t)
                fwait(p, "scalar", [p.cur(sm_st)])
                fwait(p, "sync", [p.cur(sm_ld)])
                p.run_block()

        with ExitStack() as phB:
            s_owT = phB.enter_context(
                nc.sbuf_tensor("s_owT", [128, NCH * H], BF16))
            with ExitStack() as ph:
                t_w = [ph.enter_context(nc.sbuf_tensor(f"u_w{i}", [128, IN], F32))
                       for i in range(2)]
                t_wq = ph.enter_context(nc.sbuf_tensor("u_wq", [128, IN], BF16))
                p_tr = [ph.enter_context(nc.psum_tensor(f"q_tr{i}", [128, 512], BF16))
                        for i in range(2)]
                for oc in range(NKH):
                    buf = t_w[oc % 2]
                    tok = dma(p, "sync", buf[:, :],
                              bass.AP(out_w, oc * 128 * IN, [[IN, 128], [1, IN]]),
                              sm_ld, waits=[p.cur(sm_v)] if oc >= 2 else [])
                    tc = comp(p, "vector", lambda v, buf=buf: v.tensor_copy(
                        t_wq[:, :], buf[:, :]), sm_v, waits=[tok, p.cur(sm_t)])
                    for grp in range(NCH // 4):
                        pt = p_tr[grp % 2]
                        tt = None
                        for b4 in range(4):
                            cc = grp * 4 + b4
                            tt = comp(p, "tensor",
                                      lambda t, cc=cc, b4=b4, pt=pt: t.transpose(
                                          pt[:, b4 * 128:(b4 + 1) * 128],
                                          t_wq[:, cc * 128:(cc + 1) * 128],
                                          s_id[:, :]),
                                      sm_t if b4 == 3 else None,
                                      waits=[tc, p.cur(sm_a)] if b4 == 0 else [])
                        comp(p, "scalar", lambda s, grp=grp, oc=oc, pt=pt:
                             s.activation(
                                 out=bass.AP(s_owT, (4 * grp) * H + oc * 128,
                                             [[NCH * H, 128], [H, 4], [1, 128]]),
                                 in_=pt[:, :], func=AF.Identity), sm_a,
                             waits=[tt])
                p.run_block()

            with ExitStack() as ph:
                s_br = [ph.enter_context(
                    nc.sbuf_tensor(f"s_br{i}", [128, TB], BF16))
                    for i in range(NGRP)]
                s_cr = [ph.enter_context(
                    nc.sbuf_tensor(f"s_cr{i}", [128, TB], BF16))
                    for i in range(NGRP)]
                s_dtc = [ph.enter_context(
                    nc.sbuf_tensor(f"s_dtc{i}", [128, TB], F32)) for i in range(3)]
                s_dxc = [ph.enter_context(
                    nc.sbuf_tensor(f"s_dxc{i}", [128, TB], F32)) for i in range(3)]
                s_al = [ph.enter_context(
                    nc.sbuf_tensor(f"s_al{i}", [128, TB], F32)) for i in range(4)]
                s_u = [ph.enter_context(
                    nc.sbuf_tensor(f"s_u{i}", [128, TB], F32)) for i in range(4)]
                s_sc = [ph.enter_context(
                    nc.sbuf_tensor(f"s_sc{i}", [128, TB], BF16)) for i in range(4)]
                s_tmp = [ph.enter_context(
                    nc.sbuf_tensor(f"s_tmp{i}", [128, TB], BF16)) for i in range(4)]
                s_y = [ph.enter_context(
                    nc.sbuf_tensor(f"s_y{i}", [128, TB], BF16)) for i in range(4)]
                s_gch = [ph.enter_context(
                    nc.sbuf_tensor(f"s_gch{i}", [128, TB], BF16)) for i in range(2)]
                s_ygT = ph.enter_context(
                    nc.sbuf_tensor("s_ygT", [128, NCH * TB], BF16))
                s_hres = [ph.enter_context(
                    nc.sbuf_tensor(f"s_hres{i}", [128, H], F32)) for i in range(2)]
                s_t2 = [ph.enter_context(
                    nc.sbuf_tensor(f"s_t2{i}", [128, H], F32)) for i in range(2)]
                s_x1 = [ph.enter_context(
                    nc.sbuf_tensor(f"s_x1{i}", [128, H], F32)) for i in range(2)]
                s_st2 = ph.enter_context(nc.sbuf_tensor("s_st2", [128, 16], F32))
                s_sq2 = None
                s_bcrow = ph.enter_context(
                    nc.sbuf_tensor("s_bcrow", [1, 2 * TB], F32))
                p_bcb = ph.enter_context(nc.psum_tensor("p_bcb", [128, TB], F32))
                p_bcc = ph.enter_context(nc.psum_tensor("p_bcc", [128, TB], F32))
                p_op = [ph.enter_context(
                    nc.psum_tensor(f"p_op{i}", [128, H], F32)) for i in range(2)]

                br_evac = {}
                bcrow_used = {}
                br_used = {}
                cr_used = {}
                dt_used = [None, None, None]
                dtx_used = [None, None, None]
                al_used = [None] * 4
                u_used = [None] * 4
                sc_used = [None] * 4
                tmp_used = [None] * 4
                y_used = [None, None]
                g_used2 = [None, None]
                ygT_cons = None
                hres_used = [None, None]
                t2_used = [None, None]
                x1_used = [None, None]
                pop_used = [None, None]
                for bt in range(NT_B):
                    base = bt * TB
                    for ng in range(NGRP):
                        nlist = list(range(ng * NGRP, (ng + 1) * NGRP))
                        prev_evac = {0: br_evac.get((NGRP - 1, 0)),
                                     1: br_evac.get((NGRP - 1, 1))}
                        for i, n in enumerate(nlist):
                            for half, (s_dst, rowoff, pbc) in enumerate(
                                    [(s_br[i], n, p_bcb), (s_cr[i], 16 + n, p_bcc)]):
                                roff = ((2 * i + half) % 2) * TB
                                tstg = dma(
                                    p, "sync",
                                    bass.AP(s_bcrow, roff, [[2 * TB, 1], [1, TB]]),
                                    bass.AP(bcT_s, rowoff * L + base,
                                            [[L, 1], [1, TB]]),
                                    sm_ld, waits=[bcrow_used.get(roff)])
                                tmmb = None
                                for q in range(TB // 512):
                                    w = [tstg]
                                    if q == 0:
                                        w += [prev_evac[half],
                                              (br_used if half == 0 else
                                               cr_used).get(i)]
                                    tmmb = comp(
                                        p, "tensor",
                                        lambda t, roff=roff, q=q, pbc=pbc:
                                        t.matmul(
                                            pbc[:, q * 512:(q + 1) * 512],
                                            s_ones_f32[:1, :],
                                            bass.AP(s_bcrow, roff + q * 512,
                                                    [[2 * TB, 1], [1, 512]]),
                                            start=True, stop=True),
                                        sm_t if q == TB // 512 - 1 else None,
                                        waits=w)
                                bcrow_used[roff] = tmmb
                                ev = comp(
                                    p, "scalar",
                                    lambda s, s_dst=s_dst, pbc=pbc:
                                    s.activation(out=s_dst[:, :], in_=pbc[:, :],
                                                 func=AF.Identity), sm_a,
                                    waits=[tmmb])
                                br_evac[(i, half)] = ev
                                prev_evac[half] = ev
                        load_toks = {}

                        def emit_loads(ch, ng=ng, base=base, load_toks=load_toks):
                            sl = ch % 3
                            sly = ch % 2
                            t1 = dma(p, "sync", s_dtc[sl][:, :],
                                     bass.AP(dt_s, ch * 128 * L + base,
                                             [[L, 128], [1, TB]]), sm_ld,
                                     waits=[dt_used[sl]])
                            t2 = dma(p, "sync", s_dxc[sl][:, :],
                                     bass.AP(dtx_s, ch * 128 * L + base,
                                             [[L, 128], [1, TB]]), sm_ld,
                                     waits=[dtx_used[sl]])
                            t3 = dma(p, "sync", s_y[2 * sly][:, :],
                                     bass.AP(ys_s, ch * 128 * L + base,
                                             [[L, 128], [1, TB]]), sm_ld,
                                     waits=[y_used[sly]])
                            t4 = None
                            if ng == NGRP - 1:
                                t4 = dma(p, "sync", s_gch[sly][:, :],
                                         bass.AP(g_s, ch * 128 * L + base,
                                                 [[L, 128], [1, TB]]),
                                         sm_ld, waits=[g_used2[sly]])
                            load_toks[ch] = (t1, t2, t3, t4)

                        emit_loads(0)
                        for ch in range(NCH):
                            sl = ch % 3
                            sly = ch % 2
                            if ch + 1 < NCH:
                                emit_loads(ch + 1)
                            dtb = s_dtc[sl]
                            dxb = s_dxc[sl]
                            tok_dt, tok_dx, tok_y, tok_g = load_toks[ch]
                            yb, yb2 = s_y[2 * sly], s_y[2 * sly + 1]
                            ylast = tok_y
                            ycur, yoth = yb, yb2
                            pend = []
                            for i, n in enumerate(nlist):
                                isl = i
                                ab = s_al[isl]
                                ta = comp(p, "scalar",
                                          lambda s, ab=ab, ch=ch, n=n, dtb=dtb:
                                          s.activation(
                                              out=ab[:, :], in_=dtb[:, :],
                                              func=AF.Exp,
                                              scale=s_acol[:, ch * NST + n:
                                                           ch * NST + n + 1]),
                                          sm_a, waits=[tok_dt, al_used[isl]])
                                last_ta = ta
                                ub = s_u[isl]
                                tu = comp(p, "gpsimd",
                                          lambda g, ub=ub, dxb=dxb, i=i:
                                          g.tensor_tensor(ub[:, :], dxb[:, :],
                                                          s_br[i][:, :],
                                                          ALU.mult), sm_q,
                                          waits=[tok_dx, br_evac[(i, 0)],
                                                 u_used[isl]])
                                last_tu = tu
                                scb = s_sc[isl]
                                cidx = ch * NST + n
                                tsc = comp(p, "vector",
                                           lambda v, scb=scb, ab=ab, ub=ub,
                                           cidx=cidx:
                                           v.tensor_tensor_scan(
                                               scb[:, :], ab[:, :], ub[:, :],
                                               s_carry[:, cidx:cidx + 1],
                                               ALU.mult, ALU.add), sm_v,
                                           waits=[ta, tu, sc_used[isl]],
                                           fe=True)
                                al_used[isl] = tsc
                                u_used[isl] = tsc
                                comp(p, "vector",
                                     lambda v, scb=scb, cidx=cidx:
                                     v.tensor_copy(s_carry[:, cidx:cidx + 1],
                                                   scb[:, TB - 1:TB]), sm_v)
                                tmb = s_tmp[isl]
                                tcm = comp(p, "vector",
                                           lambda v, tmb=tmb, scb=scb, i=i:
                                           v.tensor_tensor(tmb[:, :], scb[:, :],
                                                           s_cr[i][:, :],
                                                           ALU.mult), sm_v,
                                           waits=[br_evac[(i, 1)],
                                                  tmp_used[isl]], fe=True)
                                sc_used[isl] = tcm
                                br_used[i] = tcm
                                cr_used[i] = tcm
                                pend.append((i, tmb))
                                if len(pend) >= 2:
                                    i0, tmb0 = pend.pop(0)
                                    ylast = comp(
                                        p, "vector",
                                        lambda v, yoth=yoth, ycur=ycur,
                                        tmb0=tmb0:
                                        v.tensor_add(yoth[:, :], ycur[:, :],
                                                     tmb0[:, :]), sm_v,
                                        waits=[ylast, tok_y], fe=True)
                                    tmp_used[i0] = ylast
                                    ycur, yoth = yoth, ycur
                            while pend:
                                i0, tmb0 = pend.pop(0)
                                ylast = comp(
                                    p, "vector",
                                    lambda v, yoth=yoth, ycur=ycur, tmb0=tmb0:
                                    v.tensor_add(yoth[:, :], ycur[:, :],
                                                 tmb0[:, :]), sm_v,
                                    waits=[ylast, tok_y], fe=True)
                                ycur, yoth = yoth, ycur
                                tmp_used[i0] = ylast
                            dt_used[sl] = last_ta
                            dtx_used[sl] = last_tu
                            if ng < NGRP - 1:
                                y_used[sly] = dma(
                                    p, "sync",
                                    bass.AP(ys_s, ch * 128 * L + base,
                                            [[L, 128], [1, TB]]),
                                    ycur[:, :], sm_st, waits=[ylast])
                            else:
                                gb = s_gch[sly]
                                tyg = comp(p, "vector",
                                           lambda v, ycur=ycur, gb=gb, ch=ch:
                                           v.tensor_tensor(
                                               s_ygT[:, ch * TB:(ch + 1) * TB],
                                               ycur[:, :], gb[:, :], ALU.mult),
                                           sm_v,
                                           waits=[ylast, tok_g,
                                                  ygT_cons if ch == 0 else None])
                                g_used2[sly] = tyg
                                y_used[sly] = tyg
                    yg_ready = p.cur(sm_v)
                    for tb in range(TB // 128):
                        sl = tb % 2
                        po = p_op[sl]
                        hr = s_hres[sl]
                        tok_h = dma(p, "sync", hr[:, :],
                                    bass.AP(hid, (base + tb * 128) * H,
                                            [[H, 128], [1, H]]), sm_ld,
                                    waits=[hres_used[sl]])
                        tmm = None
                        for os_ in range(2):
                            for ch in range(NCH):
                                w = []
                                if ch == 0 and os_ == 0:
                                    w = [yg_ready, pop_used[sl]]
                                tmm = comp(
                                    p, "tensor",
                                    lambda t, tb=tb, os_=os_, ch=ch, po=po:
                                    t.matmul(
                                        po[:, os_ * 512:(os_ + 1) * 512],
                                        s_ygT[:, ch * TB + tb * 128:
                                              ch * TB + tb * 128 + 128],
                                        s_owT[:, ch * H + os_ * 512:
                                              ch * H + os_ * 512 + 512],
                                        start=(ch == 0), stop=False),
                                    None, waits=w)
                            tmm = comp(p, "tensor",
                                       lambda t, os_=os_, po=po: t.matmul(
                                           po[:, os_ * 512:(os_ + 1) * 512],
                                           s_ones_bf[:1, :],
                                           s_outb_row[:1, os_ * 512:
                                                      (os_ + 1) * 512],
                                           start=False, stop=True),
                                       sm_t if os_ == 1 else None)
                        t2b = s_t2[sl]
                        tt2 = comp(p, "vector", lambda v, t2b=t2b, hr=hr, po=po:
                                   v.scalar_tensor_tensor(
                                       t2b[:, :], hr[:, :], 2.0, po[:, :],
                                       ALU.mult, ALU.add), sm_v,
                                   waits=[tmm, tok_h, t2_used[sl]])
                        pop_used[sl] = tt2
                        c0 = 8 * sl
                        tvs = comp(p, "vector", lambda v, t2b=t2b, c0=c0:
                                   v.reduce_sum(s_st2[:, c0:c0 + 1], t2b[:, :],
                                                axis=AX.X), sm_v, waits=[tt2])
                        tas = comp(p, "scalar", lambda s, t2b=t2b, c0=c0,
                                   hr=hr:
                                   s.activation(
                                       out=hr[:, :], in_=t2b[:, :],
                                       func=AF.Square,
                                       accum_out=s_st2[:, c0 + 1:c0 + 2]),
                                   sm_a, waits=[tvs])
                        hres_used[sl] = tas
                        tv1 = comp(p, "vector", lambda v, c0=c0: v.tensor_scalar(
                            s_st2[:, c0:c0 + 1], s_st2[:, c0:c0 + 1],
                            1.0 / H, None, ALU.mult), sm_v, waits=[tas])
                        tv2 = comp(p, "vector", lambda v, c0=c0: v.tensor_tensor(
                            s_st2[:, c0 + 2:c0 + 3], s_st2[:, c0:c0 + 1],
                            s_st2[:, c0:c0 + 1], ALU.mult), sm_v, waits=[tv1])
                        tv3 = comp(p, "vector", lambda v, c0=c0:
                                   v.scalar_tensor_tensor(
                                       s_st2[:, c0 + 3:c0 + 4],
                                       s_st2[:, c0 + 1:c0 + 2], 1.0 / H,
                                       s_st2[:, c0 + 2:c0 + 3],
                                       ALU.mult, ALU.subtract), sm_v,
                                   waits=[tv2])
                        ta1 = comp(p, "scalar", lambda s, c0=c0: s.activation(
                            out=s_st2[:, c0 + 4:c0 + 5],
                            in_=s_st2[:, c0 + 3:c0 + 4], func=AF.Ln, bias=s_epscol[:, :]),
                            sm_a, waits=[tv3])
                        ta2 = comp(p, "scalar", lambda s, c0=c0: s.activation(
                            out=s_st2[:, c0 + 5:c0 + 6],
                            in_=s_st2[:, c0 + 4:c0 + 5], func=AF.Exp,
                            scale=s_mhalf[:, :]), sm_a, waits=[ta1])
                        tv4 = comp(p, "vector", lambda v, c0=c0: v.tensor_tensor(
                            s_st2[:, c0 + 6:c0 + 7], s_st2[:, c0:c0 + 1],
                            s_st2[:, c0 + 5:c0 + 6], ALU.mult), sm_v,
                            waits=[ta2])
                        x1b = s_x1[sl]
                        tx1 = comp(p, "vector", lambda v, x1b=x1b, t2b=t2b, c0=c0:
                                   v.tensor_scalar(
                                       x1b[:, :], t2b[:, :],
                                       s_st2[:, c0 + 5:c0 + 6],
                                       s_st2[:, c0 + 6:c0 + 7],
                                       ALU.mult, ALU.subtract), sm_v,
                                   waits=[tv4, x1_used[sl]])
                        t2_used[sl] = tx1
                        x1_used[sl] = dma(
                            p, "scalar",
                            bass.AP(x1_s, (base + tb * 128) * H,
                                    [[H, 128], [1, H]]),
                            x1b[:, :], sm_st, waits=[tx1])
                    ygT_cons = p.cur(sm_t)
                fwait(p, "scalar", [p.cur(sm_st)])
                fwait(p, "sync", [p.cur(sm_ld)])
                p.run_block()

        with ExitStack() as ph:
            NTB = TC // 128
            s_wdT = ph.enter_context(
                nc.sbuf_tensor("c_wdT", [128, NFF * H], BF16))
            s_xqT2 = [ph.enter_context(
                nc.sbuf_tensor(f"c_xqT{i}", [128, NKH * TC], BF16))
                for i in range(2)]
            s_xqT = s_xqT2[0]
            s_wg = [ph.enter_context(
                nc.sbuf_tensor("c_wg0", [128, NKH * 512], BF16))]
            s_wu = [ph.enter_context(
                nc.sbuf_tensor("c_wu0", [128, NKH * 512], BF16))]
            s_x1c = [ph.enter_context(
                nc.sbuf_tensor(f"c_x1{i}", [128, H], F32)) for i in range(2)]
            s_xq = ph.enter_context(nc.sbuf_tensor("c_xq", [128, H], BF16))
            s_tg = [ph.enter_context(
                nc.sbuf_tensor(f"c_tg{i}", [128, 512], F32)) for i in range(2)]
            s_hp = [ph.enter_context(
                nc.sbuf_tensor(f"c_hp{i}", [128, 512], F32)) for i in range(2)]
            s_hpb = [ph.enter_context(
                nc.sbuf_tensor(f"c_hpb{i}", [128, FF], F32)) for i in range(2)]
            s_hq = ph.enter_context(nc.sbuf_tensor("c_hq", [128, FF], BF16))
            s_hqT = [ph.enter_context(
                nc.sbuf_tensor(f"c_hqT{i}", [128, NFF * 128], BF16))
                for i in range(2)]
            s_rmx2 = [ph.enter_context(
                nc.sbuf_tensor(f"c_rmx{i}", [128, NTB * 8], F32))
                for i in range(2)]
            s_rmx = s_rmx2[0]
            s_fcols2 = [ph.enter_context(
                nc.sbuf_tensor(f"c_fc{i}", [128, NTB * 2], F32))
                for i in range(2)]
            s_fcols = s_fcols2[0]
            s_stc = ph.enter_context(nc.sbuf_tensor("c_st", [128, 32], F32))
            s_sq3 = ph.enter_context(nc.sbuf_tensor("c_sq3", [128, H], F32))
            s_ot = [ph.enter_context(
                nc.sbuf_tensor(f"c_ot{i}", [128, H], F32)) for i in range(2)]
            p_g = [ph.enter_context(
                nc.psum_tensor(f"c_pg{i}", [128, 512], F32)) for i in range(2)]
            p_u2 = [ph.enter_context(
                nc.psum_tensor(f"c_pu{i}", [128, 512], F32)) for i in range(2)]
            p_t3 = [ph.enter_context(
                nc.psum_tensor(f"c_pt{i}", [128, 512], BF16)) for i in range(2)]
            p_d = ph.enter_context(nc.psum_tensor("c_pd", [128, H], F32))

            wd_tok = dma(p, "sync",
                         bass.AP(s_wdT, 0, [[NFF * H, 128], [H, NFF], [1, H]]),
                         bass.AP(wdT_s, 0, [[H, 128], [128 * H, NFF], [1, H]]),
                         sm_ld)

            x1_used3 = [None, None]
            xq_free = [None, None]
            tg_used = [None, None]
            hp_ps = [None, None]
            hp_st = [None, None]
            hpb_used = [None, None]
            hq_free = None
            hqT_used = [None, None]
            ot_used = [None, None]
            pd_used = None
            wg_used = [None, None]
            wu_used = [None, None]
            for sc_i in range(NT_C):
                sbase = sc_i * TC
                spar = sc_i % 2
                s_xqT = s_xqT2[spar]
                s_rmx = s_rmx2[spar]
                s_fcols = s_fcols2[spar]
                hp_off = spar * TC * FF
                for tb in range(NTB):
                    sl = tb % 2
                    c0 = 16 + 8 * sl
                    xb = s_x1c[sl]
                    tokx = dma(p, "sync", xb[:, :],
                               bass.AP(x1_s, (sbase + tb * 128) * H,
                                       [[H, 128], [1, H]]), sm_ld,
                               waits=[x1_used3[sl]])
                    trm = comp(p, "vector", lambda v, xb=xb, c0=c0: v.reduce_max(
                        s_stc[:, c0:c0 + 1], xb[:, :], axis=AX.X,
                        apply_absolute_value=True), sm_v, waits=[tokx])
                    tm2 = comp(p, "vector", lambda v, c0=c0: v.tensor_scalar(
                        s_stc[:, c0 + 1:c0 + 2], s_stc[:, c0:c0 + 1], 1e-5,
                        None, ALU.max), sm_v, waits=[trm])
                    trc = comp(p, "vector", lambda v, c0=c0: v.reciprocal(
                        s_stc[:, c0 + 2:c0 + 3], s_stc[:, c0 + 1:c0 + 2]),
                        sm_v, waits=[tm2])
                    tsc2 = comp(p, "vector", lambda v, c0=c0: v.tensor_scalar(
                        s_stc[:, c0 + 3:c0 + 4], s_stc[:, c0 + 2:c0 + 3],
                        127.0, None, ALU.mult), sm_v, waits=[trc])
                    tf1 = comp(p, "vector", lambda v, tb=tb, c0=c0,
                               s_fcols=s_fcols:
                               v.scalar_tensor_tensor(
                                   s_fcols[:, 2 * tb:2 * tb + 1],
                                   s_mw[:, 0:1], 1.0 / 127.0,
                                   s_stc[:, c0 + 1:c0 + 2],
                                   ALU.mult, ALU.mult), sm_v, waits=[tsc2])
                    tf2 = comp(p, "vector", lambda v, tb=tb, c0=c0,
                               s_fcols=s_fcols:
                               v.scalar_tensor_tensor(
                                   s_fcols[:, 2 * tb + 1:2 * tb + 2],
                                   s_mw[:, 1:2], 1.0 / 127.0,
                                   s_stc[:, c0 + 1:c0 + 2],
                                   ALU.mult, ALU.mult), sm_v, waits=[tf1])
                    tq1 = comp(p, "vector", lambda v, xb=xb, c0=c0:
                               v.tensor_scalar(
                                   xb[:, :], xb[:, :],
                                   s_stc[:, c0 + 3:c0 + 4], C_RND,
                                   ALU.mult, ALU.add), sm_v,
                               waits=[tf2] +
                               ([hq_free, p.cur(sm_a)] if tb == 0 else []))
                    tq2 = comp(p, "vector", lambda v, xb=xb: v.tensor_scalar(
                        xb[:, :], xb[:, :], C_RND, -128.0,
                        ALU.subtract, ALU.max), sm_v, waits=[tq1])
                    tq3 = comp(p, "vector", lambda v, xb=xb: v.tensor_scalar(
                        s_xq[:, :], xb[:, :], 127.0, None, ALU.min),
                        sm_v, waits=[tq2])
                    x1_used3[sl] = tq3
                    for grp in range(2):
                        pt = p_t3[grp]
                        tt = None
                        for b4 in range(4):
                            bd = grp * 4 + b4
                            w = []
                            if b4 == 0:
                                w = [tq3, p.cur(sm_a)]
                                if tb == 0 and grp == 0:
                                    w.append(xq_free[spar])
                            tt = comp(p, "tensor",
                                      lambda t, bd=bd, b4=b4, pt=pt: t.transpose(
                                          pt[:, b4 * 128:(b4 + 1) * 128],
                                          s_xq[:, bd * 128:(bd + 1) * 128],
                                          s_id[:, :]),
                                      sm_t if b4 == 3 else None, waits=w)
                        comp(p, "scalar", lambda s, grp=grp, tb=tb, pt=pt,
                             s_xqT=s_xqT:
                             s.activation(
                                 out=bass.AP(s_xqT, (4 * grp) * TC + tb * 128,
                                             [[NKH * TC, 128], [TC, 4],
                                              [1, 128]]),
                                 in_=pt[:, :], func=AF.Identity), sm_a,
                             waits=[tt])
                xq_ready = p.cur(sm_a)
                for os_ in range(8):
                    wsl = 0
                    wgb = s_wg[wsl]
                    wub = s_wu[wsl]
                    tokwg = dma(p, "sync", wgb[:, :],
                                bass.AP(wgT_s, os_ * 512,
                                        [[FF, 128], [128 * FF, NKH], [1, 512]]),
                                sm_ld, waits=[wg_used[wsl]])
                    tokwu = dma(p, "sync", wub[:, :],
                                bass.AP(wuT_s, os_ * 512,
                                        [[FF, 128], [128 * FF, NKH], [1, 512]]),
                                sm_ld, waits=[wu_used[wsl]])
                    for tb in range(NTB):
                        sl = tb % 2
                        pg = p_g[sl]
                        pu = p_u2[sl]
                        tmm = None
                        for ch in range(NKH):
                            w = []
                            if ch == 0:
                                w = [xq_ready, tokwg, tokwu, hp_ps[sl]]
                            tmm = comp(p, "tensor",
                                       lambda t, ch=ch, tb=tb, pg=pg, wgb=wgb,
                                       s_xqT=s_xqT:
                                       t.matmul(
                                           pg[:, :],
                                           s_xqT[:, ch * TC + tb * 128:
                                                 ch * TC + tb * 128 + 128],
                                           wgb[:, ch * 512:(ch + 1) * 512],
                                           start=(ch == 0),
                                           stop=(ch == NKH - 1)),
                                       None, waits=w)
                            tmm = comp(p, "tensor",
                                       lambda t, ch=ch, tb=tb, pu=pu, wub=wub,
                                       s_xqT=s_xqT:
                                       t.matmul(
                                           pu[:, :],
                                           s_xqT[:, ch * TC + tb * 128:
                                                 ch * TC + tb * 128 + 128],
                                           wub[:, ch * 512:(ch + 1) * 512],
                                           start=(ch == 0),
                                           stop=(ch == NKH - 1)),
                                       sm_t if ch == NKH - 1 else None)
                        wg_used[wsl] = tmm
                        wu_used[wsl] = tmm
                        tgb = s_tg[sl]
                        ttg = comp(p, "scalar", lambda s, tgb=tgb, pg=pg, tb=tb,
                                   s_fcols=s_fcols:
                                   s.activation(
                                       out=tgb[:, :], in_=pg[:, :],
                                       func=AF.Silu,
                                       scale=s_fcols[:, 2 * tb:2 * tb + 1]),
                                   sm_a, waits=[tmm, tg_used[sl]])
                        hpb = s_hp[sl]
                        thp = comp(p, "vector",
                                   lambda v, hpb=hpb, tgb=tgb, pu=pu:
                                   v.tensor_tensor(hpb[:, :], tgb[:, :],
                                                   pu[:, :], ALU.mult), sm_v,
                                   waits=[ttg, hp_st[sl]])
                        tg_used[sl] = thp
                        hp_ps[sl] = thp
                        trm2 = comp(p, "vector",
                                    lambda v, hpb=hpb, tb=tb, os_=os_,
                                    s_rmx=s_rmx:
                                    v.reduce_max(
                                        s_rmx[:, tb * 8 + os_:
                                              tb * 8 + os_ + 1],
                                        hpb[:, :], axis=AX.X,
                                        apply_absolute_value=True), sm_v,
                                    waits=[thp])
                        hp_st[sl] = dma(
                            p, "scalar",
                            bass.AP(hp_s, hp_off + (tb * 128) * FF + os_ * 512,
                                    [[FF, 128], [1, 512]]),
                            hpb[:, :], sm_st, waits=[trm2])
                hp_done = p.cur(sm_st)
                for tb in range(NTB):
                    sl = tb % 2
                    c0 = 8 * sl
                    hb = s_hpb[sl]
                    tokh = dma(p, "sync", hb[:, :],
                               bass.AP(hp_s, hp_off + (tb * 128) * FF,
                                       [[FF, 128], [1, FF]]), sm_ld,
                               waits=[hp_done if tb == 0 else None,
                                      hpb_used[sl]])
                    xb = s_x1c[sl]
                    tokx = dma(p, "sync", xb[:, :],
                               bass.AP(x1_s, (sbase + tb * 128) * H,
                                       [[H, 128], [1, H]]), sm_ld,
                               waits=[x1_used3[sl]])
                    trm = comp(p, "vector", lambda v, tb=tb, c0=c0,
                               s_rmx=s_rmx: v.reduce_max(
                        s_stc[:, c0:c0 + 1], s_rmx[:, tb * 8:(tb + 1) * 8],
                        axis=AX.X), sm_v)
                    tt1 = comp(p, "vector", lambda v, tb=tb, c0=c0,
                               s_fcols=s_fcols:
                               v.tensor_tensor(
                                   s_stc[:, c0 + 1:c0 + 2],
                                   s_fcols[:, 2 * tb + 1:2 * tb + 2],
                                   s_stc[:, c0:c0 + 1], ALU.mult), sm_v,
                               waits=[trm])
                    tt2 = comp(p, "vector", lambda v, c0=c0: v.tensor_scalar(
                        s_stc[:, c0 + 2:c0 + 3], s_stc[:, c0 + 1:c0 + 2],
                        1e-5, None, ALU.max), sm_v, waits=[tt1])
                    tt3 = comp(p, "vector", lambda v, c0=c0: v.reciprocal(
                        s_stc[:, c0 + 3:c0 + 4], s_stc[:, c0 + 2:c0 + 3]),
                        sm_v, waits=[tt2])
                    tt4 = comp(p, "vector", lambda v, tb=tb, c0=c0,
                               s_fcols=s_fcols:
                               v.scalar_tensor_tensor(
                                   s_stc[:, c0 + 4:c0 + 5],
                                   s_fcols[:, 2 * tb + 1:2 * tb + 2], 127.0,
                                   s_stc[:, c0 + 3:c0 + 4],
                                   ALU.mult, ALU.mult), sm_v, waits=[tt3])
                    tt5 = comp(p, "vector", lambda v, c0=c0:
                               v.scalar_tensor_tensor(
                                   s_stc[:, c0 + 5:c0 + 6],
                                   s_stc[:, c0 + 2:c0 + 3], 1.0 / 127.0,
                                   s_mw[:, 2:3], ALU.mult, ALU.mult), sm_v,
                               waits=[tt4])
                    tq1 = comp(p, "vector", lambda v, hb=hb, c0=c0:
                               v.tensor_scalar(
                                   hb[:, :], hb[:, :],
                                   s_stc[:, c0 + 4:c0 + 5], C_RND,
                                   ALU.mult, ALU.add), sm_v,
                               waits=[tokh, tt5, hq_free])
                    tq2 = comp(p, "vector", lambda v, hb=hb: v.tensor_scalar(
                        hb[:, :], hb[:, :], C_RND, -128.0,
                        ALU.subtract, ALU.max), sm_v, waits=[tq1])
                    tq3 = comp(p, "vector", lambda v, hb=hb: v.tensor_scalar(
                        s_hq[:, :], hb[:, :], 127.0, None, ALU.min), sm_v,
                        waits=[tq2])
                    hpb_used[sl] = tq3
                    hqT = s_hqT[sl]
                    tev = None
                    for grp in range(NFF // 4):
                        pt = p_t3[grp % 2]
                        tt = None
                        for b4 in range(4):
                            bd = grp * 4 + b4
                            w = []
                            if b4 == 0:
                                w = [tq3, p.cur(sm_a)]
                                if grp == 0:
                                    w.append(hqT_used[sl])
                            tt = comp(p, "tensor",
                                      lambda t, bd=bd, b4=b4, pt=pt: t.transpose(
                                          pt[:, b4 * 128:(b4 + 1) * 128],
                                          s_hq[:, bd * 128:(bd + 1) * 128],
                                          s_id[:, :]),
                                      sm_t if b4 == 3 else None, waits=w)
                        tev = comp(p, "scalar",
                                   lambda s, grp=grp, hqT=hqT, pt=pt:
                                   s.activation(
                                       out=hqT[:, grp * 512:(grp + 1) * 512],
                                       in_=pt[:, :], func=AF.Identity), sm_a,
                                   waits=[tt])
                    hq_free = p.cur(sm_t)
                    tmm = None
                    for ch in range(NFF):
                        for os2 in range(2):
                            w = []
                            if ch == 0 and os2 == 0:
                                w = [tev, wd_tok, pd_used]
                            tmm = comp(p, "tensor",
                                       lambda t, ch=ch, os2=os2, hqT=hqT:
                                       t.matmul(
                                           p_d[:, os2 * 512:(os2 + 1) * 512],
                                           hqT[:, ch * 128:(ch + 1) * 128],
                                           s_wdT[:, ch * H + os2 * 512:
                                                 ch * H + os2 * 512 + 512],
                                           start=(ch == 0),
                                           stop=(ch == NFF - 1)),
                                       sm_t if (ch == NFF - 1 and os2 == 1)
                                       else None, waits=w)
                    hqT_used[sl] = tmm
                    otb = s_ot[sl]
                    tx2 = comp(p, "vector", lambda v, otb=otb, xb=xb, c0=c0:
                               v.scalar_tensor_tensor(
                                   otb[:, :], p_d[:, :],
                                   s_stc[:, c0 + 5:c0 + 6], xb[:, :],
                                   ALU.mult, ALU.add), sm_v,
                               waits=[tmm, tokx, ot_used[sl]])
                    pd_used = tx2
                    x1_used3[sl] = tx2
                    tvs = comp(p, "vector", lambda v, otb=otb, c0=c0:
                               v.reduce_sum(s_stc[:, c0 + 6:c0 + 7],
                                            otb[:, :], axis=AX.X), sm_v,
                               waits=[tx2])
                    tas = comp(p, "scalar", lambda s, otb=otb, c0=c0:
                               s.activation(
                                   out=s_sq3[:, :], in_=otb[:, :],
                                   func=AF.Square,
                                   accum_out=s_stc[:, c0 + 7:c0 + 8]), sm_a,
                               waits=[tvs, x1_used3[1 - sl] if False else None])
                    tv1 = comp(p, "vector", lambda v, c0=c0: v.tensor_scalar(
                        s_stc[:, c0 + 6:c0 + 7], s_stc[:, c0 + 6:c0 + 7],
                        1.0 / H, None, ALU.mult), sm_v, waits=[tas])
                    tv2 = comp(p, "vector", lambda v, c0=c0: v.tensor_tensor(
                        s_stc[:, c0:c0 + 1], s_stc[:, c0 + 6:c0 + 7],
                        s_stc[:, c0 + 6:c0 + 7], ALU.mult), sm_v, waits=[tv1])
                    tv3 = comp(p, "vector", lambda v, c0=c0:
                               v.scalar_tensor_tensor(
                                   s_stc[:, c0 + 1:c0 + 2],
                                   s_stc[:, c0 + 7:c0 + 8], 1.0 / H,
                                   s_stc[:, c0:c0 + 1],
                                   ALU.mult, ALU.subtract), sm_v, waits=[tv2])
                    ta1 = comp(p, "scalar", lambda s, c0=c0: s.activation(
                        out=s_stc[:, c0 + 2:c0 + 3],
                        in_=s_stc[:, c0 + 1:c0 + 2], func=AF.Ln, bias=s_epscol[:, :]),
                        sm_a, waits=[tv3])
                    ta2 = comp(p, "scalar", lambda s, c0=c0: s.activation(
                        out=s_stc[:, c0 + 3:c0 + 4],
                        in_=s_stc[:, c0 + 2:c0 + 3], func=AF.Exp, scale=s_mhalf[:, :]),
                        sm_a, waits=[ta1])
                    tv4 = comp(p, "vector", lambda v, c0=c0: v.tensor_tensor(
                        s_stc[:, c0 + 4:c0 + 5], s_stc[:, c0 + 6:c0 + 7],
                        s_stc[:, c0 + 3:c0 + 4], ALU.mult), sm_v, waits=[ta2])
                    txo = comp(p, "vector", lambda v, otb=otb, c0=c0:
                               v.tensor_scalar(
                                   otb[:, :], otb[:, :],
                                   s_stc[:, c0 + 3:c0 + 4],
                                   s_stc[:, c0 + 4:c0 + 5],
                                   ALU.mult, ALU.subtract), sm_v, waits=[tv4])
                    ot_used[sl] = dma(
                        p, "scalar",
                        bass.AP(out, (sbase + tb * 128) * H, [[H, 128], [1, H]]),
                        otb[:, :], sm_o, waits=[txo])
                xq_free[spar] = p.cur(sm_t)
            fwait(p, "gpsimd", [p.cur(sm_o)])
            p.run_block()

    return nc


_NC_CACHE = {}


def kernel(**inputs):
    if 8192 not in _NC_CACHE:
        _NC_CACHE[8192] = build_program(8192)
    nc = _NC_CACHE[8192]
    hidden = np.ascontiguousarray(np.asarray(inputs["L0_attn_hidden"], np.float32))
    names = {
        "ln_w": "L0_attn_ln_w", "ln_b": "L0_attn_ln_b", "in_w": "L0_attn_in_w",
        "in_b": "L0_attn_in_b", "conv_w": "L0_attn_conv_w",
        "conv_b": "L0_attn_conv_b", "xp_w": "L0_attn_xp_w",
        "dtp_w": "L0_attn_dtp_w", "dtp_b": "L0_attn_dtp_b",
        "A_log": "L0_attn_A_log", "D": "L0_attn_D", "out_w": "L0_attn_out_w",
        "out_b": "L0_attn_out_b", "w_gate": "L0_ffn_w_gate",
        "w_up": "L0_ffn_w_up", "w_down": "L0_ffn_w_down",
    }
    common = {k: np.ascontiguousarray(np.asarray(inputs[v], np.float32))
              for k, v in names.items()}
    in_maps = []
    for b in range(hidden.shape[0]):
        m = dict(common)
        m["hidden"] = np.ascontiguousarray(hidden[b])
        in_maps.append(m)
    res = run_bass_kernel_spmd(nc, in_maps, core_ids=list(range(len(in_maps))))
    return np.stack([np.asarray(r["out"], np.float32) for r in res.results])


# version 13
# speedup vs baseline: 1.0614x; 1.0048x over previous
import numpy as np
from contextlib import ExitStack

import concourse.bass as bass
import concourse.mybir as mybir
from concourse.bass_utils import run_bass_kernel_spmd

F32 = mybir.dt.float32
BF16 = mybir.dt.bfloat16
AF = mybir.ActivationFunctionType
ALU = mybir.AluOpType
AX = mybir.AxisListType

H = 1024
IN = 2048
FF = 4096
NST = 16
RK = 64
KC = 4
NCH = IN // 128
NKH = H // 128
NFF = FF // 128
C_RND = 12582912.0
EPS = 1e-5

TA = 512
TB = 1024
TC = 1024
NGRP = 4


class P:
    def __init__(self, nc):
        self.nc = nc
        self.order = ("sync", "gpsimd", "vector", "scalar", "tensor")
        self.ops = {k: [] for k in self.order}
        self.cnt = {}
        self.eng_of = {}

    def sem(self, ctx, name, engine=None):
        s = ctx.enter_context(self.nc.semaphore(name))
        self.cnt[id(s)] = 0
        if engine:
            self.eng_of[id(s)] = engine
        return s

    def emit(self, engine, fn):
        self.ops[engine].append(fn)

    def tok(self, sem, amt):
        self.cnt[id(sem)] += amt
        return (sem, self.cnt[id(sem)])

    def cur(self, sem):
        return (sem, self.cnt[id(sem)])

    def run_block(self):
        with self.nc.Block() as block:
            for eng_name in self.order:
                ops = self.ops[eng_name]
                if not ops:
                    continue

                def body(eng, ops=list(ops)):
                    for fn in ops:
                        fn(eng)

                getattr(block, eng_name)(body)
        self.ops = {k: [] for k in self.order}


def dma(p, queue, out_ap, in_ap, sem, waits=(), slow=False):
    waits = [w for w in waits if w is not None]
    tok = p.tok(sem, 16)

    def fn(eng, out_ap=out_ap, in_ap=in_ap, waits=tuple(waits)):
        for (ws, wv) in waits:
            eng.wait_ge(ws, wv)
        kw = {"allow_slow_non_contiguous": True} if slow else {}
        eng.dma_start(out=out_ap, in_=in_ap, **kw).then_inc(tok[0], 16)

    p.emit(queue, fn)
    return tok


FILTER_ENGINES = ()


def comp(p, engine, build, sem=None, waits=(), fe=False):
    waits = [w for w in waits
             if w is not None and not (fe and
                                       p.eng_of.get(id(w[0])) == engine)]
    tok = p.tok(sem, 1) if sem is not None else None

    def fn(eng, waits=tuple(waits)):
        for (ws, wv) in waits:
            eng.wait_ge(ws, wv)
        inst = build(eng)
        if tok is not None and inst is not None:
            inst.then_inc(tok[0], 1)

    p.emit(engine, fn)
    return tok


def fwait(p, engine, waits):
    def fn(eng, waits=tuple(waits)):
        for (ws, wv) in waits:
            eng.wait_ge(ws, wv)

    p.emit(engine, fn)


def vgather(t, ncols):
    return bass.AP(t, 0, [[1, 128], [128, ncols]])


def build_program(L=8192, debug=False):
    nc = bass.Bass("TRN2", target_bir_lowering=False, debug=False)
    NT_A = L // TA
    NT_B = L // TB
    NT_C = L // TC

    hid = nc.dram_tensor("hidden", [L, H], F32, kind="ExternalInput")
    ln_w = nc.dram_tensor("ln_w", [H], F32, kind="ExternalInput")
    ln_b = nc.dram_tensor("ln_b", [H], F32, kind="ExternalInput")
    in_w = nc.dram_tensor("in_w", [2 * IN, H], F32, kind="ExternalInput")
    in_b = nc.dram_tensor("in_b", [2 * IN], F32, kind="ExternalInput")
    conv_w = nc.dram_tensor("conv_w", [IN, 1, KC], F32, kind="ExternalInput")
    conv_b = nc.dram_tensor("conv_b", [IN], F32, kind="ExternalInput")
    xp_w = nc.dram_tensor("xp_w", [RK + 2 * NST, IN], F32, kind="ExternalInput")
    dtp_w = nc.dram_tensor("dtp_w", [IN, RK], F32, kind="ExternalInput")
    dtp_b = nc.dram_tensor("dtp_b", [IN], F32, kind="ExternalInput")
    A_log = nc.dram_tensor("A_log", [IN, NST], F32, kind="ExternalInput")
    D_vec = nc.dram_tensor("D", [IN], F32, kind="ExternalInput")
    out_w = nc.dram_tensor("out_w", [H, IN], F32, kind="ExternalInput")
    out_b = nc.dram_tensor("out_b", [H], F32, kind="ExternalInput")
    w_gate = nc.dram_tensor("w_gate", [FF, H], F32, kind="ExternalInput")
    w_up = nc.dram_tensor("w_up", [FF, H], F32, kind="ExternalInput")
    w_down = nc.dram_tensor("w_down", [H, FF], F32, kind="ExternalInput")
    out = nc.dram_tensor("out", [L, H], F32, kind="ExternalOutput")

    dk = "ExternalOutput" if debug else "Internal"
    dt_s = nc.dram_tensor("dt_s", [NCH, 128, L], F32, kind=dk)
    dtx_s = nc.dram_tensor("dtx_s", [NCH, 128, L], F32, kind=dk)
    ys_s = nc.dram_tensor("ys_s", [NCH, 128, L], BF16, kind=dk)
    g_s = nc.dram_tensor("g_s", [NCH, 128, L], BF16, kind=dk)
    x1_s = nc.dram_tensor("x1_s", [L, H], F32, kind=dk)
    hp_s = nc.dram_tensor("hp_s", [2, TC, FF], F32, kind=dk)
    bcT_s = nc.dram_tensor("bcT_s", [32, L], F32, kind=dk)
    wgT_s = nc.dram_tensor("wgT_s", [NKH, 128, FF], BF16, kind=dk)
    if debug:
        inwT_d = nc.dram_tensor("inwT_d", [128, NKH * 2 * IN], BF16,
                                kind="ExternalOutput")
        btld_d = nc.dram_tensor("btld_d", [128, 32], F32, kind="ExternalOutput")
        mw_d = nc.dram_tensor("mw_d", [128, 3], F32, kind="ExternalOutput")
    wuT_s = nc.dram_tensor("wuT_s", [NKH, 128, FF], BF16, kind=dk)
    wdT_s = nc.dram_tensor("wdT_s", [NFF, 128, H], BF16, kind=dk)

    with ExitStack() as per:
        def sbp(name, shape, dt=F32):
            return per.enter_context(nc.sbuf_tensor(name, shape, dt))

        s_id = sbp("s_id", [128, 128], BF16)
        s_lnw = sbp("s_lnw", [128, NKH])
        s_lnb_bf = sbp("s_lnb_bf", [128, NKH], BF16)
        s_inb = sbp("s_inb", [128, 32])
        s_btld = sbp("s_btld", [128, 32])
        s_cw = sbp("s_cw", [128, NCH * KC])
        s_cb = sbp("s_cb", [128, NCH])
        s_dtpb = sbp("s_dtpb", [128, NCH])
        s_dcol = sbp("s_dcol", [128, NCH])
        s_acol = sbp("s_acol", [128, NCH * NST])
        s_mw = sbp("s_mw", [128, 3])
        s_outb_row = sbp("s_outb_row", [1, H], BF16)
        s_ones_bf = sbp("s_ones_bf", [1, 128], BF16)
        s_ones_f32 = sbp("s_ones_f32", [1, 128])
        s_onecol = sbp("s_onecol", [128, 1])
        s_epscol = sbp("s_epscol", [128, 1])
        s_mhalf = sbp("s_mhalf", [128, 1])
        s_carry = sbp("s_carry", [128, NCH * NST])
        s_halo = sbp("s_halo", [128, NCH * (KC - 1)])

        p = P(nc)
        sm_ld = p.sem(per, "ld")
        sm_lq = p.sem(per, "lq")
        sm_st = p.sem(per, "st")
        sm_v = p.sem(per, "v", engine="vector")
        sm_a = p.sem(per, "a", engine="scalar")
        sm_t = p.sem(per, "t", engine="tensor")
        sm_q = p.sem(per, "q", engine="gpsimd")
        sm_o = p.sem(per, "o")

        with ExitStack() as ph:
            t_iop = ph.enter_context(nc.sbuf_tensor("t_iop", [128, 128], F32))
            t_iof = ph.enter_context(nc.sbuf_tensor("t_iof", [128, 128], F32))
            t_al = ph.enter_context(nc.sbuf_tensor("t_al", [128, NST], F32))
            t_lnb = ph.enter_context(nc.sbuf_tensor("t_lnb", [128, NKH], F32))
            t_ob = ph.enter_context(nc.sbuf_tensor("t_ob", [1, H], F32))

            def g_init(g):
                g.iota(t_iop[:, :], [[0, 128]], channel_multiplier=1,
                       allow_small_or_imprecise_dtypes=True)
                g.iota(t_iof[:, :], [[1, 128]], channel_multiplier=0,
                       allow_small_or_imprecise_dtypes=True)
                g.memset(s_ones_bf[:, :], 1.0)
                g.memset(s_ones_f32[:, :], 1.0)
                g.memset(s_onecol[:, :], 1.0)
                g.memset(s_epscol[:, :], 1e-5)
                g.memset(s_mhalf[:, :], -0.5)
                g.memset(s_carry[:, :], 0.0)
                return g.memset(s_halo[:, :], 0.0)

            tg = comp(p, "gpsimd", g_init, sm_q)
            tl2 = dma(p, "sync", t_lnb[:, :], vgather(ln_b, NKH), sm_ld, slow=True)
            dma(p, "sync", s_lnw[:, :], vgather(ln_w, NKH), sm_ld, slow=True)
            dma(p, "sync", s_inb[:, :], vgather(in_b, 32), sm_ld, slow=True)
            dma(p, "sync", s_cb[:, :], vgather(conv_b, NCH), sm_ld, slow=True)
            dma(p, "sync", s_dtpb[:, :], vgather(dtp_b, NCH), sm_ld, slow=True)
            dma(p, "sync", s_dcol[:, :], vgather(D_vec, NCH), sm_ld, slow=True)
            dma(p, "sync", s_cw[:, :],
                bass.AP(conv_w, 0, [[KC, 128], [KC * 128, NCH], [1, KC]]),
                sm_ld, slow=True)
            tob = dma(p, "sync", t_ob[:, :],
                      bass.AP(out_b, 0, [[H, 1], [1, H]]), sm_ld)
            comp(p, "vector", lambda v: v.tensor_tensor(
                s_id[:, :], t_iop[:, :], t_iof[:, :], ALU.is_equal), sm_v,
                waits=[tg])
            comp(p, "vector", lambda v: v.tensor_copy(
                s_lnb_bf[:, :], t_lnb[:, :]), sm_v, waits=[tl2])
            comp(p, "vector", lambda v: v.tensor_copy(
                s_outb_row[:, :], t_ob[:, :]), sm_v, waits=[tob])
            for ch in range(NCH):
                tok = dma(p, "sync", t_al[:, :],
                          bass.AP(A_log, ch * 128 * NST, [[NST, 128], [1, NST]]),
                          sm_ld, waits=[p.cur(sm_a)] if ch else [])
                ta = comp(p, "scalar", lambda s, ch=ch: s.activation(
                    out=s_acol[:, ch * NST:(ch + 1) * NST], in_=t_al[:, :],
                    func=AF.Exp), sm_a, waits=[tok])
                comp(p, "vector", lambda v, ch=ch: v.tensor_scalar(
                    s_acol[:, ch * NST:(ch + 1) * NST],
                    s_acol[:, ch * NST:(ch + 1) * NST], -1.0, None, ALU.mult),
                    sm_v, waits=[ta])
            p.run_block()

        with ExitStack() as phA:
            s_inwT = phA.enter_context(
                nc.sbuf_tensor("s_inwT", [128, NKH * 2 * IN], BF16))

            with ExitStack() as ph:
                t_w = [ph.enter_context(nc.sbuf_tensor(f"t_w{i}", [128, FF], F32))
                       for i in range(2)]
                t_wq = ph.enter_context(nc.sbuf_tensor("t_wq", [128, FF], BF16))
                t_red = [ph.enter_context(nc.sbuf_tensor(f"t_red{i}", [128, 1], F32))
                         for i in range(3)]
                t_sc = ph.enter_context(nc.sbuf_tensor("t_sc", [1, 4], F32))
                t_scb = ph.enter_context(nc.sbuf_tensor("t_scb", [128, 1], F32))
                t_ev = [ph.enter_context(nc.sbuf_tensor(f"t_ev{i}", [128, 512], BF16))
                        for i in range(2)]
                p_tr = [ph.enter_context(nc.psum_tensor(f"p_tr{i}", [128, 512], BF16))
                        for i in range(2)]
                p_sc = ph.enter_context(nc.psum_tensor("p_sc", [128, 4], F32))

                def quant_matrix(w_t, rows, cols, wT_s, mw_idx):
                    nchr = rows // 128
                    for cch in range(nchr):
                        buf = t_w[cch % 2]
                        tok = dma(p, "sync", buf[:, :cols],
                                  bass.AP(w_t, cch * 128 * cols,
                                          [[cols, 128], [1, cols]]), sm_ld,
                                  waits=[p.cur(sm_v), p.cur(sm_t)]
                                  if cch < 2 else [p.cur(sm_v)])
                        red = t_red[cch % 2]
                        tr = comp(p, "vector", lambda v, buf=buf, red=red, cols=cols:
                                  v.reduce_sum(red[:, :], buf[:, :cols], axis=AX.X,
                                               apply_absolute_value=True),
                                  sm_v, waits=[tok])
                        if cch == 0:
                            comp(p, "vector", lambda v, red=red: v.tensor_copy(
                                t_red[2][:, :], red[:, :]), sm_v,
                                waits=[tr, p.cur(sm_t)])
                        else:
                            comp(p, "vector", lambda v, red=red: v.tensor_add(
                                t_red[2][:, :], t_red[2][:, :], red[:, :]), sm_v,
                                waits=[tr])
                    tm = comp(p, "tensor", lambda t: t.matmul(
                        p_sc[:1, 0:1], t_red[2][:, :], s_onecol[:, :],
                        start=True, stop=True), sm_t, waits=[p.cur(sm_v)])
                    tv1 = comp(p, "vector", lambda v, rows=rows, cols=cols:
                               v.tensor_scalar(
                                   t_sc[:1, 0:1], p_sc[:1, 0:1],
                                   1.0 / (rows * cols), 1e-5, ALU.mult, ALU.max),
                               sm_v, waits=[tm])
                    tv2 = comp(p, "vector", lambda v: v.reciprocal(
                        t_sc[:1, 1:2], t_sc[:1, 0:1]), sm_v, waits=[tv1])
                    tb = comp(p, "tensor", lambda t: t.matmul(
                        p_sc[:, 0:2], s_ones_f32[:1, :], t_sc[:1, 0:2],
                        start=True, stop=True), sm_t, waits=[tv2])
                    tvb = comp(p, "vector", lambda v: v.tensor_copy(
                        t_scb[:, :], p_sc[:, 1:2]), sm_v, waits=[tb])
                    comp(p, "vector", lambda v, mw_idx=mw_idx: v.tensor_copy(
                        s_mw[:, mw_idx:mw_idx + 1], p_sc[:, 0:1]), sm_v)
                    for cch in range(nchr):
                        buf = t_w[cch % 2]
                        tok = dma(p, "sync", buf[:, :cols],
                                  bass.AP(w_t, cch * 128 * cols,
                                          [[cols, 128], [1, cols]]), sm_ld,
                                  waits=[p.cur(sm_v)])
                        tq1 = comp(p, "vector", lambda v, buf=buf, cols=cols:
                                   v.tensor_scalar(
                                       buf[:, :cols], buf[:, :cols], t_scb[:, :],
                                       C_RND, ALU.mult, ALU.add), sm_v,
                                   waits=[tok, tvb, p.cur(sm_t)])
                        tq2 = comp(p, "vector", lambda v, buf=buf, cols=cols:
                                   v.tensor_scalar(
                                       buf[:, :cols], buf[:, :cols], C_RND, -1.0,
                                       ALU.subtract, ALU.max), sm_v, waits=[tq1])
                        tq3 = comp(p, "vector", lambda v, buf=buf, cols=cols:
                                   v.tensor_scalar(
                                       t_wq[:, :cols], buf[:, :cols], 1.0, None,
                                       ALU.min), sm_v, waits=[tq2])
                        for grp in range(cols // 512):
                            pt = p_tr[grp % 2]
                            tt = None
                            for b4 in range(4):
                                bd = grp * 4 + b4
                                tt = comp(p, "tensor",
                                          lambda t, bd=bd, b4=b4, pt=pt: t.transpose(
                                              pt[:, b4 * 128:(b4 + 1) * 128],
                                              t_wq[:, bd * 128:(bd + 1) * 128],
                                              s_id[:, :]),
                                          sm_t if b4 == 3 else None,
                                          waits=[tq3, p.cur(sm_st)]
                                          if b4 == 0 else [])
                            ev = t_ev[grp % 2]
                            ta = comp(p, "scalar", lambda s, ev=ev, pt=pt:
                                      s.activation(out=ev[:, :], in_=pt[:, :],
                                                   func=AF.Identity),
                                      sm_a, waits=[tt])
                            dma(p, "scalar",
                                bass.AP(wT_s, grp * 4 * 128 * rows + cch * 128,
                                        [[rows, 128], [128 * rows, 4], [1, 128]]),
                                ev[:, :], sm_st, waits=[ta])

                quant_matrix(w_gate, FF, H, wgT_s, 0)
                quant_matrix(w_up, FF, H, wuT_s, 1)
                quant_matrix(w_down, H, FF, wdT_s, 2)

                for cch in range(2 * IN // 128):
                    buf = t_w[cch % 2]
                    tok = dma(p, "sync", buf[:, :H],
                              bass.AP(in_w, cch * 128 * H, [[H, 128], [1, H]]),
                              sm_ld, waits=[p.cur(sm_v)])
                    tc = comp(p, "vector", lambda v, buf=buf: v.tensor_copy(
                        t_wq[:, :H], buf[:, :H]), sm_v,
                        waits=[tok, p.cur(sm_t)])
                    for grp in range(NKH // 4):
                        pt = p_tr[grp % 2]
                        tt = None
                        for b4 in range(4):
                            bd = grp * 4 + b4
                            tt = comp(p, "tensor",
                                      lambda t, bd=bd, b4=b4, pt=pt: t.transpose(
                                          pt[:, b4 * 128:(b4 + 1) * 128],
                                          t_wq[:, bd * 128:(bd + 1) * 128],
                                          s_id[:, :]),
                                      sm_t if b4 == 3 else None,
                                      waits=[tc, p.cur(sm_a)] if b4 == 0 else [])
                        for b4 in range(4):
                            bd = grp * 4 + b4
                            comp(p, "scalar",
                                 lambda s, bd=bd, b4=b4, pt=pt, cch=cch:
                                 s.activation(
                                     out=s_inwT[:, bd * 2 * IN + cch * 128:
                                                bd * 2 * IN + (cch + 1) * 128],
                                     in_=pt[:, b4 * 128:(b4 + 1) * 128],
                                     func=AF.Identity),
                                 sm_a, waits=[tt] if b4 == 0 else [])
                for ob in range(32):
                    tprev = None
                    for bd in range(NKH):
                        tprev = comp(p, "tensor", lambda t, ob=ob, bd=bd: t.matmul(
                            p_sc[:, 2:3],
                            s_inwT[:, bd * 2 * IN + ob * 128:
                                   bd * 2 * IN + ob * 128 + 128],
                            s_lnb_bf[:, bd:bd + 1],
                            start=(bd == 0), stop=(bd == NKH - 1)),
                            sm_t if bd == NKH - 1 else None,
                            waits=[p.cur(sm_a), p.cur(sm_v)] if bd == 0 else [])
                    tvb2 = comp(p, "vector", lambda v, ob=ob: v.tensor_tensor(
                        s_btld[:, ob:ob + 1], p_sc[:, 2:3], s_inb[:, ob:ob + 1],
                        ALU.add), sm_v, waits=[tprev])
                    fwait(p, "tensor", [tvb2])
                for bd in range(NKH):
                    comp(p, "vector", lambda v, bd=bd: v.tensor_scalar(
                        s_inwT[:, bd * 2 * IN:(bd + 1) * 2 * IN],
                        s_inwT[:, bd * 2 * IN:(bd + 1) * 2 * IN],
                        s_lnw[:, bd:bd + 1], None, ALU.mult), sm_v,
                        waits=[p.cur(sm_t)] if bd == 0 else [])
                if debug:
                    dma(p, "scalar", inwT_d.ap(), s_inwT[:, :], sm_st,
                        waits=[p.cur(sm_v)])
                    dma(p, "scalar", btld_d.ap(), s_btld[:, :], sm_st,
                        waits=[p.cur(sm_v)])
                    dma(p, "scalar", mw_d.ap(), s_mw[:, :], sm_st,
                        waits=[p.cur(sm_v)])
                    fwait(p, "scalar", [p.cur(sm_st)])
                p.run_block()

            with ExitStack() as ph:
                s_xpT = ph.enter_context(
                    nc.sbuf_tensor("s_xpT", [128, NCH * 96], BF16))
                s_dtpT = ph.enter_context(
                    nc.sbuf_tensor("s_dtpT", [64, NCH * 128], BF16))
                NB = TA // 128
                s_h = [ph.enter_context(
                    nc.sbuf_tensor(f"s_h{i}", [128, NB * H], F32))
                    for i in range(2)]
                s_hn2 = [ph.enter_context(
                    nc.sbuf_tensor(f"s_hn{i}", [128, NB * H], BF16))
                    for i in range(2)]
                s_hn = s_hn2[0]
                s_hnT2 = [ph.enter_context(
                    nc.sbuf_tensor(f"s_hnT{i}", [128, NKH * TA], BF16))
                    for i in range(2)]
                s_hnT = s_hnT2[0]
                s_xc = ph.enter_context(
                    nc.sbuf_tensor("s_xc", [128, NCH * (TA + 3)], F32))
                s_par = ph.enter_context(nc.sbuf_tensor("s_par", [96, TA], F32))
                s_drb = ph.enter_context(nc.sbuf_tensor("s_drb", [64, TA], BF16))
                s_stat = ph.enter_context(
                    nc.sbuf_tensor("s_stat", [128, 8 * NB], F32))
                s_sq = ph.enter_context(nc.sbuf_tensor("s_sq", [128, H], F32))
                s_ca = [ph.enter_context(
                    nc.sbuf_tensor(f"s_ca{i}", [128, TA], F32)) for i in range(2)]
                s_xcb = [ph.enter_context(
                    nc.sbuf_tensor(f"s_xcb{i}", [128, TA], BF16)) for i in range(2)]
                s_dt = [ph.enter_context(
                    nc.sbuf_tensor(f"s_dt{i}", [128, TA], F32)) for i in range(2)]
                s_z = [ph.enter_context(
                    nc.sbuf_tensor(f"s_z{i}", [128, TA], F32)) for i in range(2)]
                s_sp = [ph.enter_context(
                    nc.sbuf_tensor(f"s_sp{i}", [128, TA], F32)) for i in range(2)]
                s_gt = [ph.enter_context(
                    nc.sbuf_tensor(f"s_gt{i}", [128, TA], BF16)) for i in range(2)]
                s_dx = [ph.enter_context(
                    nc.sbuf_tensor(f"s_dx{i}", [128, TA], BF16)) for i in range(2)]
                p_mm = [ph.enter_context(
                    nc.psum_tensor(f"p_mm{i}", [128, TA], F32)) for i in range(2)]
                p_xp = ph.enter_context(nc.psum_tensor("p_xp", [96, TA], F32))
                p_dt = [ph.enter_context(
                    nc.psum_tensor(f"p_dt{i}", [128, TA], F32)) for i in range(2)]
                p_t2 = ph.enter_context(nc.psum_tensor("p_t2", [128, 512], BF16))

                t_xw = s_h[0]
                t_xq2 = s_h[1]
                tok = dma(p, "sync", t_xw[:96, :IN],
                          bass.AP(xp_w, 0, [[IN, 96], [1, IN]]), sm_ld)
                tcx = comp(p, "vector", lambda v: v.tensor_copy(
                    s_hn[:96, :IN], t_xw[:96, :IN]), sm_v, waits=[tok])
                for ch in range(NCH):
                    sl4 = (ch % 2) * 128
                    tt = comp(p, "tensor", lambda t, ch=ch, sl4=sl4: t.transpose(
                        p_t2[:, sl4:sl4 + 96],
                        s_hn[:96, ch * 128:(ch + 1) * 128], s_id[:96, :96]),
                        sm_t, waits=[tcx, p.cur(sm_a)])
                    comp(p, "scalar", lambda s, ch=ch, sl4=sl4: s.activation(
                        out=s_xpT[:, ch * 96:(ch + 1) * 96],
                        in_=p_t2[:, sl4:sl4 + 96], func=AF.Identity), sm_a,
                        waits=[tt])
                tdw = dma(p, "sync", t_xq2[:, :RK * NCH],
                          bass.AP(dtp_w, 0, [[RK, 128], [128 * RK, NCH], [1, RK]]),
                          sm_ld)
                tcd = comp(p, "vector", lambda v: v.tensor_copy(
                    s_hn[:, :RK * NCH].bitcast(BF16)
                    if False else s_hn[:, :RK * NCH], t_xq2[:, :RK * NCH]),
                    sm_v, waits=[tdw, p.cur(sm_t)])
                for ch in range(NCH):
                    sl4 = (ch % 2) * 128
                    tt = comp(p, "tensor", lambda t, ch=ch, sl4=sl4: t.transpose(
                        p_t2[:RK, sl4:sl4 + 128],
                        s_hn[:, ch * RK:(ch + 1) * RK], s_id[:, :]),
                        sm_t, waits=[tcd, p.cur(sm_a)])
                    comp(p, "scalar", lambda s, ch=ch, sl4=sl4: s.activation(
                        out=s_dtpT[:, ch * 128:(ch + 1) * 128],
                        in_=p_t2[:RK, sl4:sl4 + 128], func=AF.Identity), sm_a,
                        waits=[tt])
                p.run_block()

                h_used = [None, None]
                cons_x = {}
                g_used = [None, None]
                cons_hn = [None, None]
                cons_hnT = [None, None]
                xp_evac = None
                dtp_last = None
                evac_tok = {}
                for it in range(NT_A):
                    base = it * TA
                    s_hn = s_hn2[it % 2]
                    s_hnT = s_hnT2[it % 2]
                    hb = s_h[it % 2]
                    htok = dma(
                        p, "sync", hb[:, :],
                        bass.AP(hid, base * H, [[H, 128], [128 * H, NB], [1, H]]),
                        sm_ld, waits=[h_used[it % 2]])
                    hn_toks = []
                    red_toks = []
                    sq_toks = []
                    for j in range(NB):
                        hbj = hb[:, j * H:(j + 1) * H]
                        c0 = 8 * j
                        red_toks.append(comp(
                            p, "vector", lambda v, hbj=hbj, c0=c0:
                            v.reduce_sum(s_stat[:, c0:c0 + 1], hbj, axis=AX.X),
                            sm_v, waits=[htok] + ([cons_hn[it % 2]] if j == 0 else [])))
                    for j in range(NB):
                        hbj = hb[:, j * H:(j + 1) * H]
                        c0 = 8 * j
                        sq_toks.append(comp(
                            p, "scalar", lambda s, hbj=hbj, c0=c0:
                            s.activation(out=s_sq[:, :], in_=hbj,
                                         func=AF.Square,
                                         accum_out=s_stat[:, c0 + 1:c0 + 2]),
                            sm_a, waits=[htok] if j else [htok, p.cur(sm_v)]))
                    def sv(col):
                        return bass.AP(s_stat, col, [[8 * NB, 128], [8, NB]])
                    tv1 = comp(p, "vector", lambda v: v.tensor_scalar(
                        sv(0), sv(0), 1.0 / H, None, ALU.mult), sm_v,
                        waits=[sq_toks[-1], red_toks[-1]])
                    tv2 = comp(p, "vector", lambda v: v.tensor_tensor(
                        sv(2), sv(0), sv(0), ALU.mult), sm_v, waits=[tv1])
                    tv3 = comp(p, "vector", lambda v: v.scalar_tensor_tensor(
                        sv(3), sv(1), 1.0 / H, sv(2), ALU.mult, ALU.subtract),
                        sm_v, waits=[tv2])
                    ta1 = comp(p, "scalar", lambda s: s.activation(
                        out=sv(4), in_=sv(3), func=AF.Ln,
                        bias=s_epscol[:, :]), sm_a, waits=[tv3])
                    ta2 = comp(p, "scalar", lambda s: s.activation(
                        out=sv(5), in_=sv(4), func=AF.Exp,
                        scale=s_mhalf[:, :]), sm_a, waits=[ta1])
                    tv4 = comp(p, "vector", lambda v: v.tensor_tensor(
                        sv(6), sv(0), sv(5), ALU.mult), sm_v, waits=[ta2])
                    for j in range(NB):
                        hbj = hb[:, j * H:(j + 1) * H]
                        c0 = 8 * j
                        hn_toks.append(comp(
                            p, "vector", lambda v, hbj=hbj, j=j, c0=c0, s_hn=s_hn:
                            v.tensor_scalar(
                                s_hn[:, j * H:(j + 1) * H], hbj,
                                s_stat[:, c0 + 5:c0 + 6],
                                s_stat[:, c0 + 6:c0 + 7],
                                ALU.mult, ALU.subtract), sm_v,
                            waits=[tv4, cons_hn[it % 2]] if j == 0 else []))
                    h_used[it % 2] = hn_toks[-1]
                    for j in range(NB):
                        for g2 in range(2):
                            tt = None
                            for b4 in range(4):
                                bd = g2 * 4 + b4
                                k = j * NKH + bd
                                w = []
                                if b4 == 0:
                                    w = [hn_toks[j], p.cur(sm_a),
                                         cons_hnT[it % 2] if k == 0 else None]
                                tt = comp(
                                    p, "tensor",
                                    lambda t, j=j, bd=bd, b4=b4, s_hn=s_hn:
                                    t.transpose(
                                        p_t2[:, b4 * 128:(b4 + 1) * 128],
                                        s_hn[:, j * H + bd * 128:
                                             j * H + (bd + 1) * 128],
                                        s_id[:, :]),
                                    sm_t if b4 == 3 else None, waits=w)
                            evac_tok[("hnT", it, j, g2)] = comp(
                                p, "scalar", lambda s, j=j, g2=g2, s_hnT=s_hnT:
                                s.activation(
                                    out=bass.AP(
                                        s_hnT, (g2 * 4) * TA + j * 128,
                                        [[NKH * TA, 128], [TA, 4], [1, 128]]),
                                    in_=p_t2[:, :], func=AF.Identity),
                                sm_a, waits=[tt])
                    cons_hn[it % 2] = p.cur(sm_t)
                    hnT_ready = evac_tok[("hnT", it, NB - 1, 1)]
                    for ob in range(32):
                        pm = p_mm[ob % 2]
                        tmm = None
                        for bd in range(NKH):
                            w = []
                            if bd == 0:
                                w = [hnT_ready, evac_tok.get(("ev", it, ob - 2))]
                            tmm = comp(p, "tensor",
                                       lambda t, ob=ob, bd=bd, pm=pm,
                                       s_hnT=s_hnT: t.matmul(
                                           pm[:, :],
                                           s_inwT[:, bd * 2 * IN + ob * 128:
                                                  bd * 2 * IN + ob * 128 + 128],
                                           s_hnT[:, bd * TA:(bd + 1) * TA],
                                           start=(bd == 0), stop=(bd == NKH - 1)),
                                       sm_t if bd == NKH - 1 else None, waits=w)
                        if ob < 16:
                            ch = ob
                            evac_tok[("ev", it, ob)] = comp(
                                p, "scalar", lambda s, ch=ch, pm=pm, ob=ob:
                                s.activation(
                                    out=s_xc[:, ch * (TA + 3) + 3:
                                             (ch + 1) * (TA + 3)],
                                    in_=pm[:, :], func=AF.Identity,
                                    bias=s_btld[:, ob:ob + 1]), sm_a,
                                waits=[tmm, cons_x.get(ch)])
                        else:
                            ch = ob - 16
                            gb = s_gt[ch % 2]
                            tev = comp(
                                p, "scalar", lambda s, gb=gb, pm=pm, ob=ob:
                                s.activation(out=gb[:, :], in_=pm[:, :],
                                             func=AF.Silu,
                                             bias=s_btld[:, ob:ob + 1]), sm_a,
                                waits=[tmm, g_used[ch % 2]])
                            evac_tok[("ev", it, ob)] = tev
                            g_used[ch % 2] = dma(
                                p, "scalar",
                                bass.AP(g_s, ch * 128 * L + base,
                                        [[L, 128], [1, TA]]),
                                gb[:, :], sm_st, waits=[tev])
                    for ch in range(NCH):
                        xs = ch * (TA + 3)
                        ev = evac_tok[("ev", it, ch)]
                        th0 = comp(p, "vector", lambda v, ch=ch, xs=xs:
                                   v.tensor_copy(s_xc[:, xs:xs + 3],
                                                 s_halo[:, ch * 3:(ch + 1) * 3]),
                                   sm_v, waits=[ev])
                        th1 = comp(p, "vector", lambda v, ch=ch, xs=xs:
                                   v.tensor_copy(s_halo[:, ch * 3:(ch + 1) * 3],
                                                 s_xc[:, xs + TA:xs + TA + 3]),
                                   sm_v, waits=[th0])
                        tcv = comp(p, "vector", lambda v, ch=ch, xs=xs:
                                   v.tensor_scalar(
                                       s_ca[0][:, :], s_xc[:, xs:xs + TA],
                                       s_cw[:, ch * KC:ch * KC + 1], None,
                                       ALU.mult), sm_v, waits=[th1])
                        for k in range(1, KC):
                            src = s_ca[(k - 1) % 2]
                            dst = s_ca[k % 2]
                            tcv = comp(p, "vector",
                                       lambda v, ch=ch, xs=xs, k=k, src=src,
                                       dst=dst:
                                       v.scalar_tensor_tensor(
                                           dst[:, :], s_xc[:, xs + k:xs + k + TA],
                                           s_cw[:, ch * KC + k:ch * KC + k + 1],
                                           src[:, :], ALU.mult, ALU.add),
                                       sm_v, waits=[tcv])
                        tsl = comp(p, "scalar", lambda s, ch=ch, xs=xs:
                                   s.activation(
                                       out=s_xc[:, xs + 3:xs + 3 + TA],
                                       in_=s_ca[(KC - 1) % 2][:, :],
                                       func=AF.Silu, bias=s_cb[:, ch:ch + 1]),
                                   sm_a, waits=[tcv])
                        xb = s_xcb[ch % 2]
                        tcb = comp(p, "vector", lambda v, ch=ch, xs=xs, xb=xb:
                                   v.tensor_copy(xb[:, :],
                                                 s_xc[:, xs + 3:xs + 3 + TA]),
                                   sm_v, waits=[tsl])
                        comp(p, "tensor", lambda t, ch=ch, xb=xb: t.matmul(
                            p_xp[:, :], s_xpT[:, ch * 96:(ch + 1) * 96],
                            xb[:, :], start=(ch == 0), stop=(ch == NCH - 1)),
                            sm_t if ch == NCH - 1 else None,
                            waits=[tcb] + ([xp_evac] if ch == 0 else []))
                        dxb = s_dx[ch % 2]
                        tdx = comp(p, "vector", lambda v, ch=ch, xs=xs, dxb=dxb:
                                   v.tensor_scalar(
                                       dxb[:, :], s_xc[:, xs + 3:xs + 3 + TA],
                                       s_dcol[:, ch:ch + 1], None, ALU.mult),
                                   sm_v, waits=[tsl])
                        dma(p, "scalar",
                            bass.AP(ys_s, ch * 128 * L + base,
                                    [[L, 128], [1, TA]]),
                            dxb[:, :], sm_st, waits=[tdx])
                    xp_mm_tok = p.cur(sm_t)
                    xp_evac = comp(p, "scalar", lambda s: s.activation(
                        out=s_par[:, :], in_=p_xp[:, :], func=AF.Identity),
                        sm_a, waits=[xp_mm_tok, dtp_last])
                    dma(p, "sync",
                        bass.AP(bcT_s, base, [[L, 32], [1, TA]]),
                        bass.AP(s_par, 64 * TA, [[TA, 32], [1, TA]]),
                        sm_ld, waits=[xp_evac])
                    tdr = comp(p, "vector", lambda v: v.tensor_copy(
                        s_drb[:, :], s_par[:64, :]), sm_v, waits=[xp_evac])
                    for ch in range(NCH):
                        pd = p_dt[ch % 2]
                        zb = s_z[ch % 2]
                        spb = s_sp[ch % 2]
                        db = s_dt[ch % 2]
                        tdm = comp(p, "tensor", lambda t, ch=ch, pd=pd: t.matmul(
                            pd[:, :], s_dtpT[:, ch * 128:(ch + 1) * 128],
                            s_drb[:, :], start=True, stop=True), sm_t,
                            waits=[tdr] + ([evac_tok.get(("z", it, ch - 2))]
                                           if ch >= 2 else []))
                        tz = comp(p, "vector", lambda v, pd=pd, ch=ch, zb=zb:
                                  v.tensor_scalar(zb[:, :], pd[:, :],
                                                  s_dtpb[:, ch:ch + 1], None,
                                                  ALU.add), sm_v, waits=[tdm])
                        evac_tok[("z", it, ch)] = tz
                        nzb = s_ca[ch % 2]
                        tsp0 = comp(p, "vector", lambda v, zb=zb, nzb=nzb:
                                    v.tensor_scalar(nzb[:, :], zb[:, :], -1.0,
                                                    None, ALU.mult),
                                    sm_v, waits=[tz])
                        tsp1 = comp(p, "vector", lambda v, zb=zb, spb=spb,
                                    nzb=nzb:
                                    v.tensor_tensor(spb[:, :], zb[:, :],
                                                    nzb[:, :], ALU.min),
                                    sm_v, waits=[tsp0])
                        tsp2 = comp(p, "scalar", lambda s, spb=spb: s.activation(
                            out=spb[:, :], in_=spb[:, :], func=AF.Exp), sm_a,
                            waits=[tsp1])
                        tsp3 = comp(p, "scalar", lambda s, spb=spb: s.activation(
                            out=spb[:, :], in_=spb[:, :], func=AF.Ln, bias=1.0),
                            sm_a, waits=[tsp2])
                        tsp4 = comp(p, "vector",
                                    lambda v, zb=zb, spb=spb, db=db:
                                    v.scalar_tensor_tensor(
                                        db[:, :], zb[:, :], 0.0, spb[:, :],
                                        ALU.max, ALU.add), sm_v, waits=[tsp3])
                        dma(p, "scalar",
                            bass.AP(dt_s, ch * 128 * L + base,
                                    [[L, 128], [1, TA]]),
                            db[:, :], sm_st, waits=[tsp4])
                        xs = ch * (TA + 3)
                        tdtx = comp(p, "vector",
                                    lambda v, db=db, ch=ch, xs=xs, zb=zb:
                                    v.tensor_tensor(
                                        zb[:, :], db[:, :],
                                        s_xc[:, xs + 3:xs + 3 + TA], ALU.mult),
                                    sm_v, waits=[tsp4])
                        cons_x[ch] = tdtx
                        dma(p, "scalar",
                            bass.AP(dtx_s, ch * 128 * L + base,
                                    [[L, 128], [1, TA]]),
                            zb[:, :], sm_st, waits=[tdtx])
                    dtp_last = p.cur(sm_v)
                    cons_hnT[it % 2] = p.cur(sm_t)
                fwait(p, "scalar", [p.cur(sm_st)])
                fwait(p, "sync", [p.cur(sm_ld)])
                p.run_block()

        with ExitStack() as phB:
            s_owT = phB.enter_context(
                nc.sbuf_tensor("s_owT", [128, NCH * H], BF16))
            with ExitStack() as ph:
                t_w = [ph.enter_context(nc.sbuf_tensor(f"u_w{i}", [128, IN], F32))
                       for i in range(2)]
                t_wq = ph.enter_context(nc.sbuf_tensor("u_wq", [128, IN], BF16))
                p_tr = [ph.enter_context(nc.psum_tensor(f"q_tr{i}", [128, 512], BF16))
                        for i in range(2)]
                for oc in range(NKH):
                    buf = t_w[oc % 2]
                    tok = dma(p, "sync", buf[:, :],
                              bass.AP(out_w, oc * 128 * IN, [[IN, 128], [1, IN]]),
                              sm_ld, waits=[p.cur(sm_v)] if oc >= 2 else [])
                    tc = comp(p, "vector", lambda v, buf=buf: v.tensor_copy(
                        t_wq[:, :], buf[:, :]), sm_v, waits=[tok, p.cur(sm_t)])
                    for grp in range(NCH // 4):
                        pt = p_tr[grp % 2]
                        tt = None
                        for b4 in range(4):
                            cc = grp * 4 + b4
                            tt = comp(p, "tensor",
                                      lambda t, cc=cc, b4=b4, pt=pt: t.transpose(
                                          pt[:, b4 * 128:(b4 + 1) * 128],
                                          t_wq[:, cc * 128:(cc + 1) * 128],
                                          s_id[:, :]),
                                      sm_t if b4 == 3 else None,
                                      waits=[tc, p.cur(sm_a)] if b4 == 0 else [])
                        comp(p, "scalar", lambda s, grp=grp, oc=oc, pt=pt:
                             s.activation(
                                 out=bass.AP(s_owT, (4 * grp) * H + oc * 128,
                                             [[NCH * H, 128], [H, 4], [1, 128]]),
                                 in_=pt[:, :], func=AF.Identity), sm_a,
                             waits=[tt])
                p.run_block()

            with ExitStack() as ph:
                s_br = [ph.enter_context(
                    nc.sbuf_tensor(f"s_br{i}", [128, TB], BF16))
                    for i in range(NGRP)]
                s_cr = [ph.enter_context(
                    nc.sbuf_tensor(f"s_cr{i}", [128, TB], BF16))
                    for i in range(NGRP)]
                s_dtc = [ph.enter_context(
                    nc.sbuf_tensor(f"s_dtc{i}", [128, TB], F32)) for i in range(3)]
                s_dxc = [ph.enter_context(
                    nc.sbuf_tensor(f"s_dxc{i}", [128, TB], F32)) for i in range(3)]
                s_al = [ph.enter_context(
                    nc.sbuf_tensor(f"s_al{i}", [128, TB], F32)) for i in range(4)]
                s_u = [ph.enter_context(
                    nc.sbuf_tensor(f"s_u{i}", [128, TB], F32)) for i in range(4)]
                s_sc = [ph.enter_context(
                    nc.sbuf_tensor(f"s_sc{i}", [128, TB], BF16)) for i in range(4)]
                s_tmp = [ph.enter_context(
                    nc.sbuf_tensor(f"s_tmp{i}", [128, TB], BF16)) for i in range(4)]
                s_y = [ph.enter_context(
                    nc.sbuf_tensor(f"s_y{i}", [128, TB], BF16)) for i in range(5)]
                s_gch = [ph.enter_context(
                    nc.sbuf_tensor(f"s_gch{i}", [128, TB], BF16)) for i in range(3)]
                s_ygT = ph.enter_context(
                    nc.sbuf_tensor("s_ygT", [128, NCH * TB], BF16))
                s_hres = [ph.enter_context(
                    nc.sbuf_tensor(f"s_hres{i}", [128, H], F32)) for i in range(2)]
                s_t2 = [ph.enter_context(
                    nc.sbuf_tensor(f"s_t2{i}", [128, H], F32)) for i in range(2)]
                s_x1 = [ph.enter_context(
                    nc.sbuf_tensor(f"s_x1{i}", [128, H], F32)) for i in range(2)]
                s_st2 = ph.enter_context(nc.sbuf_tensor("s_st2", [128, 16], F32))
                s_sq2 = None
                s_bcrow = ph.enter_context(
                    nc.sbuf_tensor("s_bcrow", [1, 2 * TB], F32))
                p_bcb = ph.enter_context(nc.psum_tensor("p_bcb", [128, TB], F32))
                p_bcc = ph.enter_context(nc.psum_tensor("p_bcc", [128, TB], F32))
                p_op = [ph.enter_context(
                    nc.psum_tensor(f"p_op{i}", [128, H], F32)) for i in range(2)]

                br_evac = {}
                bcrow_used = {}
                br_used = {}
                cr_used = {}
                dt_used = [None, None, None]
                dtx_used = [None, None, None]
                al_used = [None] * 4
                u_used = [None] * 4
                sc_used = [None] * 4
                tmp_used = [None] * 4
                y_used = [None] * 5
                g_used2 = [None] * 3
                ygT_cons = None
                hres_used = [None, None]
                t2_used = [None, None]
                x1_used = [None, None]
                pop_used = [None, None]
                load_toks = {}
                ystore_tok = {}
                pf = {"next": 0}
                NPOS = NT_B * NGRP * NCH

                def emit_loads_pos(pos):
                    bt2, r = divmod(pos, NGRP * NCH)
                    ng2, ch2 = divmod(r, NCH)
                    base2 = bt2 * TB
                    sl = pos % 3
                    t1 = dma(p, "sync", s_dtc[sl][:, :],
                             bass.AP(dt_s, ch2 * 128 * L + base2,
                                     [[L, 128], [1, TB]]), sm_lq,
                             waits=[dt_used[sl]])
                    t2 = dma(p, "sync", s_dxc[sl][:, :],
                             bass.AP(dtx_s, ch2 * 128 * L + base2,
                                     [[L, 128], [1, TB]]), sm_lq,
                             waits=[dtx_used[sl]])
                    t3 = dma(p, "sync", s_y[pos % 5][:, :],
                             bass.AP(ys_s, ch2 * 128 * L + base2,
                                     [[L, 128], [1, TB]]), sm_lq,
                             waits=[y_used[pos % 5],
                                    ystore_tok.get((bt2, ch2))])
                    t4 = None
                    if ng2 == NGRP - 1:
                        t4 = dma(p, "sync", s_gch[ch2 % 3][:, :],
                                 bass.AP(g_s, ch2 * 128 * L + base2,
                                         [[L, 128], [1, TB]]),
                                 sm_lq, waits=[g_used2[ch2 % 3]])
                    load_toks[pos] = (t1, t2, t3, t4)

                def ensure_pf(upto):
                    while pf["next"] <= min(upto, NPOS - 1):
                        emit_loads_pos(pf["next"])
                        pf["next"] += 1

                for bt in range(NT_B):
                    base = bt * TB
                    for ng in range(NGRP):
                        nlist = list(range(ng * NGRP, (ng + 1) * NGRP))
                        prev_evac = {0: br_evac.get((NGRP - 1, 0)),
                                     1: br_evac.get((NGRP - 1, 1))}
                        for i, n in enumerate(nlist):
                            for half, (s_dst, rowoff, pbc) in enumerate(
                                    [(s_br[i], n, p_bcb), (s_cr[i], 16 + n, p_bcc)]):
                                roff = ((2 * i + half) % 2) * TB
                                tstg = dma(
                                    p, "sync",
                                    bass.AP(s_bcrow, roff, [[2 * TB, 1], [1, TB]]),
                                    bass.AP(bcT_s, rowoff * L + base,
                                            [[L, 1], [1, TB]]),
                                    sm_ld, waits=[bcrow_used.get(roff)])
                                tmmb = None
                                for q in range(TB // 512):
                                    w = [tstg]
                                    if q == 0:
                                        w += [prev_evac[half],
                                              (br_used if half == 0 else
                                               cr_used).get(i)]
                                    tmmb = comp(
                                        p, "tensor",
                                        lambda t, roff=roff, q=q, pbc=pbc:
                                        t.matmul(
                                            pbc[:, q * 512:(q + 1) * 512],
                                            s_ones_f32[:1, :],
                                            bass.AP(s_bcrow, roff + q * 512,
                                                    [[2 * TB, 1], [1, 512]]),
                                            start=True, stop=True),
                                        sm_t if q == TB // 512 - 1 else None,
                                        waits=w)
                                bcrow_used[roff] = tmmb
                                ev = comp(
                                    p, "scalar",
                                    lambda s, s_dst=s_dst, pbc=pbc:
                                    s.activation(out=s_dst[:, :], in_=pbc[:, :],
                                                 func=AF.Identity), sm_a,
                                    waits=[tmmb])
                                br_evac[(i, half)] = ev
                                prev_evac[half] = ev
                        for ch in range(NCH):
                            pos = (bt * NGRP + ng) * NCH + ch
                            ensure_pf(pos + 2)
                            sl = pos % 3
                            dtb = s_dtc[sl]
                            dxb = s_dxc[sl]
                            tok_dt, tok_dx, tok_y, tok_g = load_toks.pop(pos)
                            yb = s_y[pos % 5]
                            ylast = tok_y
                            pend = []
                            for i, n in enumerate(nlist):
                                isl = i
                                ab = s_al[isl]
                                ta = comp(p, "scalar",
                                          lambda s, ab=ab, ch=ch, n=n, dtb=dtb:
                                          s.activation(
                                              out=ab[:, :], in_=dtb[:, :],
                                              func=AF.Exp,
                                              scale=s_acol[:, ch * NST + n:
                                                           ch * NST + n + 1]),
                                          sm_a, waits=[tok_dt, al_used[isl]])
                                last_ta = ta
                                ub = s_u[isl]
                                tu = comp(p, "gpsimd",
                                          lambda g, ub=ub, dxb=dxb, i=i:
                                          g.tensor_tensor(ub[:, :], dxb[:, :],
                                                          s_br[i][:, :],
                                                          ALU.mult), sm_q,
                                          waits=[tok_dx, br_evac[(i, 0)],
                                                 u_used[isl]])
                                last_tu = tu
                                scb = s_sc[isl]
                                cidx = ch * NST + n
                                tsc = comp(p, "vector",
                                           lambda v, scb=scb, ab=ab, ub=ub,
                                           cidx=cidx:
                                           v.tensor_tensor_scan(
                                               scb[:, :], ab[:, :], ub[:, :],
                                               s_carry[:, cidx:cidx + 1],
                                               ALU.mult, ALU.add), sm_v,
                                           waits=[ta, tu, sc_used[isl]],
                                           fe=True)
                                al_used[isl] = tsc
                                u_used[isl] = tsc
                                comp(p, "vector",
                                     lambda v, scb=scb, cidx=cidx:
                                     v.tensor_copy(s_carry[:, cidx:cidx + 1],
                                                   scb[:, TB - 1:TB]), sm_v)
                                tmb = s_tmp[isl]
                                tcm = comp(p, "vector",
                                           lambda v, tmb=tmb, scb=scb, i=i:
                                           v.tensor_tensor(tmb[:, :], scb[:, :],
                                                           s_cr[i][:, :],
                                                           ALU.mult), sm_v,
                                           waits=[br_evac[(i, 1)],
                                                  tmp_used[isl]], fe=True)
                                sc_used[isl] = tcm
                                br_used[i] = tcm
                                cr_used[i] = tcm
                                pend.append((i, tmb))
                                if len(pend) >= 2:
                                    i0, tmb0 = pend.pop(0)
                                    ylast = comp(
                                        p, "vector",
                                        lambda v, yb=yb, tmb0=tmb0:
                                        v.tensor_add(yb[:, :], yb[:, :],
                                                     tmb0[:, :]), sm_v,
                                        waits=[ylast, tok_y], fe=True)
                                    tmp_used[i0] = ylast
                            while pend:
                                i0, tmb0 = pend.pop(0)
                                ylast = comp(
                                    p, "vector",
                                    lambda v, yb=yb, tmb0=tmb0:
                                    v.tensor_add(yb[:, :], yb[:, :],
                                                 tmb0[:, :]), sm_v,
                                    waits=[ylast, tok_y], fe=True)
                                tmp_used[i0] = ylast
                            dt_used[sl] = last_ta
                            dtx_used[sl] = last_tu
                            if ng < NGRP - 1:
                                y_used[pos % 5] = dma(
                                    p, "sync",
                                    bass.AP(ys_s, ch * 128 * L + base,
                                            [[L, 128], [1, TB]]),
                                    yb[:, :], sm_st, waits=[ylast])
                                ystore_tok[(bt, ch)] = y_used[pos % 5]
                            else:
                                gb = s_gch[ch % 3]
                                tyg = comp(p, "vector",
                                           lambda v, yb=yb, gb=gb, ch=ch:
                                           v.tensor_tensor(
                                               s_ygT[:, ch * TB:(ch + 1) * TB],
                                               yb[:, :], gb[:, :], ALU.mult),
                                           sm_v,
                                           waits=[ylast, tok_g,
                                                  ygT_cons if ch == 0 else None])
                                g_used2[ch % 3] = tyg
                                y_used[pos % 5] = tyg
                    yg_ready = p.cur(sm_v)
                    for tb in range(TB // 128):
                        sl = tb % 2
                        po = p_op[sl]
                        hr = s_hres[sl]
                        tok_h = dma(p, "sync", hr[:, :],
                                    bass.AP(hid, (base + tb * 128) * H,
                                            [[H, 128], [1, H]]), sm_ld,
                                    waits=[hres_used[sl]])
                        tmm = None
                        for os_ in range(2):
                            for ch in range(NCH):
                                w = []
                                if ch == 0 and os_ == 0:
                                    w = [yg_ready, pop_used[sl]]
                                tmm = comp(
                                    p, "tensor",
                                    lambda t, tb=tb, os_=os_, ch=ch, po=po:
                                    t.matmul(
                                        po[:, os_ * 512:(os_ + 1) * 512],
                                        s_ygT[:, ch * TB + tb * 128:
                                              ch * TB + tb * 128 + 128],
                                        s_owT[:, ch * H + os_ * 512:
                                              ch * H + os_ * 512 + 512],
                                        start=(ch == 0), stop=False),
                                    None, waits=w)
                            tmm = comp(p, "tensor",
                                       lambda t, os_=os_, po=po: t.matmul(
                                           po[:, os_ * 512:(os_ + 1) * 512],
                                           s_ones_bf[:1, :],
                                           s_outb_row[:1, os_ * 512:
                                                      (os_ + 1) * 512],
                                           start=False, stop=True),
                                       sm_t if os_ == 1 else None)
                        t2b = s_t2[sl]
                        tt2 = comp(p, "vector", lambda v, t2b=t2b, hr=hr, po=po:
                                   v.scalar_tensor_tensor(
                                       t2b[:, :], hr[:, :], 2.0, po[:, :],
                                       ALU.mult, ALU.add), sm_v,
                                   waits=[tmm, tok_h, t2_used[sl]])
                        pop_used[sl] = tt2
                        c0 = 8 * sl
                        tvs = comp(p, "vector", lambda v, t2b=t2b, c0=c0:
                                   v.reduce_sum(s_st2[:, c0:c0 + 1], t2b[:, :],
                                                axis=AX.X), sm_v, waits=[tt2])
                        tas = comp(p, "scalar", lambda s, t2b=t2b, c0=c0,
                                   hr=hr:
                                   s.activation(
                                       out=hr[:, :], in_=t2b[:, :],
                                       func=AF.Square,
                                       accum_out=s_st2[:, c0 + 1:c0 + 2]),
                                   sm_a, waits=[tvs])
                        hres_used[sl] = tas
                        tv1 = comp(p, "vector", lambda v, c0=c0: v.tensor_scalar(
                            s_st2[:, c0:c0 + 1], s_st2[:, c0:c0 + 1],
                            1.0 / H, None, ALU.mult), sm_v, waits=[tas])
                        tv2 = comp(p, "vector", lambda v, c0=c0: v.tensor_tensor(
                            s_st2[:, c0 + 2:c0 + 3], s_st2[:, c0:c0 + 1],
                            s_st2[:, c0:c0 + 1], ALU.mult), sm_v, waits=[tv1])
                        tv3 = comp(p, "vector", lambda v, c0=c0:
                                   v.scalar_tensor_tensor(
                                       s_st2[:, c0 + 3:c0 + 4],
                                       s_st2[:, c0 + 1:c0 + 2], 1.0 / H,
                                       s_st2[:, c0 + 2:c0 + 3],
                                       ALU.mult, ALU.subtract), sm_v,
                                   waits=[tv2])
                        ta1 = comp(p, "scalar", lambda s, c0=c0: s.activation(
                            out=s_st2[:, c0 + 4:c0 + 5],
                            in_=s_st2[:, c0 + 3:c0 + 4], func=AF.Ln, bias=s_epscol[:, :]),
                            sm_a, waits=[tv3])
                        ta2 = comp(p, "scalar", lambda s, c0=c0: s.activation(
                            out=s_st2[:, c0 + 5:c0 + 6],
                            in_=s_st2[:, c0 + 4:c0 + 5], func=AF.Exp,
                            scale=s_mhalf[:, :]), sm_a, waits=[ta1])
                        tv4 = comp(p, "vector", lambda v, c0=c0: v.tensor_tensor(
                            s_st2[:, c0 + 6:c0 + 7], s_st2[:, c0:c0 + 1],
                            s_st2[:, c0 + 5:c0 + 6], ALU.mult), sm_v,
                            waits=[ta2])
                        x1b = s_x1[sl]
                        tx1 = comp(p, "vector", lambda v, x1b=x1b, t2b=t2b, c0=c0:
                                   v.tensor_scalar(
                                       x1b[:, :], t2b[:, :],
                                       s_st2[:, c0 + 5:c0 + 6],
                                       s_st2[:, c0 + 6:c0 + 7],
                                       ALU.mult, ALU.subtract), sm_v,
                                   waits=[tv4, x1_used[sl]])
                        t2_used[sl] = tx1
                        x1_used[sl] = dma(
                            p, "scalar",
                            bass.AP(x1_s, (base + tb * 128) * H,
                                    [[H, 128], [1, H]]),
                            x1b[:, :], sm_st, waits=[tx1])
                    ygT_cons = p.cur(sm_t)
                fwait(p, "scalar", [p.cur(sm_st)])
                fwait(p, "sync", [p.cur(sm_ld)])
                p.run_block()

        with ExitStack() as ph:
            NTB = TC // 128
            s_wdT = ph.enter_context(
                nc.sbuf_tensor("c_wdT", [128, NFF * H], BF16))
            s_xqT2 = [ph.enter_context(
                nc.sbuf_tensor(f"c_xqT{i}", [128, NKH * TC], BF16))
                for i in range(2)]
            s_xqT = s_xqT2[0]
            s_wg = [ph.enter_context(
                nc.sbuf_tensor("c_wg0", [128, NKH * 512], BF16))]
            s_wu = [ph.enter_context(
                nc.sbuf_tensor("c_wu0", [128, NKH * 512], BF16))]
            s_x1c = [ph.enter_context(
                nc.sbuf_tensor(f"c_x1{i}", [128, H], F32)) for i in range(2)]
            s_xq = ph.enter_context(nc.sbuf_tensor("c_xq", [128, H], BF16))
            s_tg = [ph.enter_context(
                nc.sbuf_tensor(f"c_tg{i}", [128, 512], F32)) for i in range(2)]
            s_hp = [ph.enter_context(
                nc.sbuf_tensor(f"c_hp{i}", [128, 512], F32)) for i in range(2)]
            s_hpb = [ph.enter_context(
                nc.sbuf_tensor(f"c_hpb{i}", [128, FF], F32)) for i in range(2)]
            s_hq = ph.enter_context(nc.sbuf_tensor("c_hq", [128, FF], BF16))
            s_hqT = [ph.enter_context(
                nc.sbuf_tensor(f"c_hqT{i}", [128, NFF * 128], BF16))
                for i in range(2)]
            s_rmx2 = [ph.enter_context(
                nc.sbuf_tensor(f"c_rmx{i}", [128, NTB * 8], F32))
                for i in range(2)]
            s_rmx = s_rmx2[0]
            s_fcols2 = [ph.enter_context(
                nc.sbuf_tensor(f"c_fc{i}", [128, NTB * 2], F32))
                for i in range(2)]
            s_fcols = s_fcols2[0]
            s_stc = ph.enter_context(nc.sbuf_tensor("c_st", [128, 32], F32))
            s_sq3 = ph.enter_context(nc.sbuf_tensor("c_sq3", [128, H], F32))
            s_ot = [ph.enter_context(
                nc.sbuf_tensor(f"c_ot{i}", [128, H], F32)) for i in range(2)]
            p_g = [ph.enter_context(
                nc.psum_tensor(f"c_pg{i}", [128, 512], F32)) for i in range(2)]
            p_u2 = [ph.enter_context(
                nc.psum_tensor(f"c_pu{i}", [128, 512], F32)) for i in range(2)]
            p_t3 = [ph.enter_context(
                nc.psum_tensor(f"c_pt{i}", [128, 512], BF16)) for i in range(2)]
            p_d = ph.enter_context(nc.psum_tensor("c_pd", [128, H], F32))

            wd_tok = dma(p, "sync",
                         bass.AP(s_wdT, 0, [[NFF * H, 128], [H, NFF], [1, H]]),
                         bass.AP(wdT_s, 0, [[H, 128], [128 * H, NFF], [1, H]]),
                         sm_ld)

            x1_used3 = [None, None]
            xq_free = [None, None]
            tg_used = [None, None]
            hp_ps = [None, None]
            hp_st = [None, None]
            hpb_used = [None, None]
            hq_free = None
            hqT_used = [None, None]
            ot_used = [None, None]
            pd_used = None
            wg_used = [None, None]
            wu_used = [None, None]
            for sc_i in range(NT_C):
                sbase = sc_i * TC
                spar = sc_i % 2
                s_xqT = s_xqT2[spar]
                s_rmx = s_rmx2[spar]
                s_fcols = s_fcols2[spar]
                hp_off = spar * TC * FF
                for tb in range(NTB):
                    sl = tb % 2
                    c0 = 16 + 8 * sl
                    xb = s_x1c[sl]
                    tokx = dma(p, "sync", xb[:, :],
                               bass.AP(x1_s, (sbase + tb * 128) * H,
                                       [[H, 128], [1, H]]), sm_ld,
                               waits=[x1_used3[sl]])
                    trm = comp(p, "vector", lambda v, xb=xb, c0=c0: v.reduce_max(
                        s_stc[:, c0:c0 + 1], xb[:, :], axis=AX.X,
                        apply_absolute_value=True), sm_v, waits=[tokx])
                    tm2 = comp(p, "vector", lambda v, c0=c0: v.tensor_scalar(
                        s_stc[:, c0 + 1:c0 + 2], s_stc[:, c0:c0 + 1], 1e-5,
                        None, ALU.max), sm_v, waits=[trm])
                    trc = comp(p, "vector", lambda v, c0=c0: v.reciprocal(
                        s_stc[:, c0 + 2:c0 + 3], s_stc[:, c0 + 1:c0 + 2]),
                        sm_v, waits=[tm2])
                    tsc2 = comp(p, "vector", lambda v, c0=c0: v.tensor_scalar(
                        s_stc[:, c0 + 3:c0 + 4], s_stc[:, c0 + 2:c0 + 3],
                        127.0, None, ALU.mult), sm_v, waits=[trc])
                    tf1 = comp(p, "vector", lambda v, tb=tb, c0=c0,
                               s_fcols=s_fcols:
                               v.scalar_tensor_tensor(
                                   s_fcols[:, 2 * tb:2 * tb + 1],
                                   s_mw[:, 0:1], 1.0 / 127.0,
                                   s_stc[:, c0 + 1:c0 + 2],
                                   ALU.mult, ALU.mult), sm_v, waits=[tsc2])
                    tf2 = comp(p, "vector", lambda v, tb=tb, c0=c0,
                               s_fcols=s_fcols:
                               v.scalar_tensor_tensor(
                                   s_fcols[:, 2 * tb + 1:2 * tb + 2],
                                   s_mw[:, 1:2], 1.0 / 127.0,
                                   s_stc[:, c0 + 1:c0 + 2],
                                   ALU.mult, ALU.mult), sm_v, waits=[tf1])
                    tq1 = comp(p, "vector", lambda v, xb=xb, c0=c0:
                               v.tensor_scalar(
                                   xb[:, :], xb[:, :],
                                   s_stc[:, c0 + 3:c0 + 4], C_RND,
                                   ALU.mult, ALU.add), sm_v,
                               waits=[tf2] +
                               ([hq_free, p.cur(sm_a)] if tb == 0 else []))
                    tq2 = comp(p, "vector", lambda v, xb=xb: v.tensor_scalar(
                        xb[:, :], xb[:, :], C_RND, -128.0,
                        ALU.subtract, ALU.max), sm_v, waits=[tq1])
                    tq3 = comp(p, "vector", lambda v, xb=xb: v.tensor_scalar(
                        s_xq[:, :], xb[:, :], 127.0, None, ALU.min),
                        sm_v, waits=[tq2])
                    x1_used3[sl] = tq3
                    for grp in range(2):
                        pt = p_t3[grp]
                        tt = None
                        for b4 in range(4):
                            bd = grp * 4 + b4
                            w = []
                            if b4 == 0:
                                w = [tq3, p.cur(sm_a)]
                                if tb == 0 and grp == 0:
                                    w.append(xq_free[spar])
                            tt = comp(p, "tensor",
                                      lambda t, bd=bd, b4=b4, pt=pt: t.transpose(
                                          pt[:, b4 * 128:(b4 + 1) * 128],
                                          s_xq[:, bd * 128:(bd + 1) * 128],
                                          s_id[:, :]),
                                      sm_t if b4 == 3 else None, waits=w)
                        comp(p, "scalar", lambda s, grp=grp, tb=tb, pt=pt,
                             s_xqT=s_xqT:
                             s.activation(
                                 out=bass.AP(s_xqT, (4 * grp) * TC + tb * 128,
                                             [[NKH * TC, 128], [TC, 4],
                                              [1, 128]]),
                                 in_=pt[:, :], func=AF.Identity), sm_a,
                             waits=[tt])
                xq_ready = p.cur(sm_a)
                for os_ in range(8):
                    wsl = 0
                    wgb = s_wg[wsl]
                    wub = s_wu[wsl]
                    tokwg = dma(p, "sync", wgb[:, :],
                                bass.AP(wgT_s, os_ * 512,
                                        [[FF, 128], [128 * FF, NKH], [1, 512]]),
                                sm_ld, waits=[wg_used[wsl]])
                    tokwu = dma(p, "sync", wub[:, :],
                                bass.AP(wuT_s, os_ * 512,
                                        [[FF, 128], [128 * FF, NKH], [1, 512]]),
                                sm_ld, waits=[wu_used[wsl]])
                    for tb in range(NTB):
                        sl = tb % 2
                        pg = p_g[sl]
                        pu = p_u2[sl]
                        tmm = None
                        for ch in range(NKH):
                            w = []
                            if ch == 0:
                                w = [xq_ready, tokwg, tokwu, hp_ps[sl]]
                            tmm = comp(p, "tensor",
                                       lambda t, ch=ch, tb=tb, pg=pg, wgb=wgb,
                                       s_xqT=s_xqT:
                                       t.matmul(
                                           pg[:, :],
                                           s_xqT[:, ch * TC + tb * 128:
                                                 ch * TC + tb * 128 + 128],
                                           wgb[:, ch * 512:(ch + 1) * 512],
                                           start=(ch == 0),
                                           stop=(ch == NKH - 1)),
                                       None, waits=w)
                            tmm = comp(p, "tensor",
                                       lambda t, ch=ch, tb=tb, pu=pu, wub=wub,
                                       s_xqT=s_xqT:
                                       t.matmul(
                                           pu[:, :],
                                           s_xqT[:, ch * TC + tb * 128:
                                                 ch * TC + tb * 128 + 128],
                                           wub[:, ch * 512:(ch + 1) * 512],
                                           start=(ch == 0),
                                           stop=(ch == NKH - 1)),
                                       sm_t if ch == NKH - 1 else None)
                        wg_used[wsl] = tmm
                        wu_used[wsl] = tmm
                        tgb = s_tg[sl]
                        ttg = comp(p, "scalar", lambda s, tgb=tgb, pg=pg, tb=tb,
                                   s_fcols=s_fcols:
                                   s.activation(
                                       out=tgb[:, :], in_=pg[:, :],
                                       func=AF.Silu,
                                       scale=s_fcols[:, 2 * tb:2 * tb + 1]),
                                   sm_a, waits=[tmm, tg_used[sl]])
                        hpb = s_hp[sl]
                        thp = comp(p, "vector",
                                   lambda v, hpb=hpb, tgb=tgb, pu=pu:
                                   v.tensor_tensor(hpb[:, :], tgb[:, :],
                                                   pu[:, :], ALU.mult), sm_v,
                                   waits=[ttg, hp_st[sl]])
                        tg_used[sl] = thp
                        hp_ps[sl] = thp
                        trm2 = comp(p, "vector",
                                    lambda v, hpb=hpb, tb=tb, os_=os_,
                                    s_rmx=s_rmx:
                                    v.reduce_max(
                                        s_rmx[:, tb * 8 + os_:
                                              tb * 8 + os_ + 1],
                                        hpb[:, :], axis=AX.X,
                                        apply_absolute_value=True), sm_v,
                                    waits=[thp])
                        hp_st[sl] = dma(
                            p, "scalar",
                            bass.AP(hp_s, hp_off + (tb * 128) * FF + os_ * 512,
                                    [[FF, 128], [1, 512]]),
                            hpb[:, :], sm_st, waits=[trm2])
                hp_done = p.cur(sm_st)
                for tb in range(NTB):
                    sl = tb % 2
                    c0 = 8 * sl
                    hb = s_hpb[sl]
                    tokh = dma(p, "sync", hb[:, :],
                               bass.AP(hp_s, hp_off + (tb * 128) * FF,
                                       [[FF, 128], [1, FF]]), sm_ld,
                               waits=[hp_done if tb == 0 else None,
                                      hpb_used[sl]])
                    xb = s_x1c[sl]
                    tokx = dma(p, "sync", xb[:, :],
                               bass.AP(x1_s, (sbase + tb * 128) * H,
                                       [[H, 128], [1, H]]), sm_ld,
                               waits=[x1_used3[sl]])
                    trm = comp(p, "vector", lambda v, tb=tb, c0=c0,
                               s_rmx=s_rmx: v.reduce_max(
                        s_stc[:, c0:c0 + 1], s_rmx[:, tb * 8:(tb + 1) * 8],
                        axis=AX.X), sm_v)
                    tt1 = comp(p, "vector", lambda v, tb=tb, c0=c0,
                               s_fcols=s_fcols:
                               v.tensor_tensor(
                                   s_stc[:, c0 + 1:c0 + 2],
                                   s_fcols[:, 2 * tb + 1:2 * tb + 2],
                                   s_stc[:, c0:c0 + 1], ALU.mult), sm_v,
                               waits=[trm])
                    tt2 = comp(p, "vector", lambda v, c0=c0: v.tensor_scalar(
                        s_stc[:, c0 + 2:c0 + 3], s_stc[:, c0 + 1:c0 + 2],
                        1e-5, None, ALU.max), sm_v, waits=[tt1])
                    tt3 = comp(p, "vector", lambda v, c0=c0: v.reciprocal(
                        s_stc[:, c0 + 3:c0 + 4], s_stc[:, c0 + 2:c0 + 3]),
                        sm_v, waits=[tt2])
                    tt4 = comp(p, "vector", lambda v, tb=tb, c0=c0,
                               s_fcols=s_fcols:
                               v.scalar_tensor_tensor(
                                   s_stc[:, c0 + 4:c0 + 5],
                                   s_fcols[:, 2 * tb + 1:2 * tb + 2], 127.0,
                                   s_stc[:, c0 + 3:c0 + 4],
                                   ALU.mult, ALU.mult), sm_v, waits=[tt3])
                    tt5 = comp(p, "vector", lambda v, c0=c0:
                               v.scalar_tensor_tensor(
                                   s_stc[:, c0 + 5:c0 + 6],
                                   s_stc[:, c0 + 2:c0 + 3], 1.0 / 127.0,
                                   s_mw[:, 2:3], ALU.mult, ALU.mult), sm_v,
                               waits=[tt4])
                    tq1 = comp(p, "vector", lambda v, hb=hb, c0=c0:
                               v.tensor_scalar(
                                   hb[:, :], hb[:, :],
                                   s_stc[:, c0 + 4:c0 + 5], C_RND,
                                   ALU.mult, ALU.add), sm_v,
                               waits=[tokh, tt5, hq_free])
                    tq2 = comp(p, "vector", lambda v, hb=hb: v.tensor_scalar(
                        hb[:, :], hb[:, :], C_RND, -128.0,
                        ALU.subtract, ALU.max), sm_v, waits=[tq1])
                    tq3 = comp(p, "vector", lambda v, hb=hb: v.tensor_scalar(
                        s_hq[:, :], hb[:, :], 127.0, None, ALU.min), sm_v,
                        waits=[tq2])
                    hpb_used[sl] = tq3
                    hqT = s_hqT[sl]
                    tev = None
                    for grp in range(NFF // 4):
                        pt = p_t3[grp % 2]
                        tt = None
                        for b4 in range(4):
                            bd = grp * 4 + b4
                            w = []
                            if b4 == 0:
                                w = [tq3, p.cur(sm_a)]
                                if grp == 0:
                                    w.append(hqT_used[sl])
                            tt = comp(p, "tensor",
                                      lambda t, bd=bd, b4=b4, pt=pt: t.transpose(
                                          pt[:, b4 * 128:(b4 + 1) * 128],
                                          s_hq[:, bd * 128:(bd + 1) * 128],
                                          s_id[:, :]),
                                      sm_t if b4 == 3 else None, waits=w)
                        tev = comp(p, "scalar",
                                   lambda s, grp=grp, hqT=hqT, pt=pt:
                                   s.activation(
                                       out=hqT[:, grp * 512:(grp + 1) * 512],
                                       in_=pt[:, :], func=AF.Identity), sm_a,
                                   waits=[tt])
                    hq_free = p.cur(sm_t)
                    tmm = None
                    for ch in range(NFF):
                        for os2 in range(2):
                            w = []
                            if ch == 0 and os2 == 0:
                                w = [tev, wd_tok, pd_used]
                            tmm = comp(p, "tensor",
                                       lambda t, ch=ch, os2=os2, hqT=hqT:
                                       t.matmul(
                                           p_d[:, os2 * 512:(os2 + 1) * 512],
                                           hqT[:, ch * 128:(ch + 1) * 128],
                                           s_wdT[:, ch * H + os2 * 512:
                                                 ch * H + os2 * 512 + 512],
                                           start=(ch == 0),
                                           stop=(ch == NFF - 1)),
                                       sm_t if (ch == NFF - 1 and os2 == 1)
                                       else None, waits=w)
                    hqT_used[sl] = tmm
                    otb = s_ot[sl]
                    tx2 = comp(p, "vector", lambda v, otb=otb, xb=xb, c0=c0:
                               v.scalar_tensor_tensor(
                                   otb[:, :], p_d[:, :],
                                   s_stc[:, c0 + 5:c0 + 6], xb[:, :],
                                   ALU.mult, ALU.add), sm_v,
                               waits=[tmm, tokx, ot_used[sl]])
                    pd_used = tx2
                    x1_used3[sl] = tx2
                    tvs = comp(p, "vector", lambda v, otb=otb, c0=c0:
                               v.reduce_sum(s_stc[:, c0 + 6:c0 + 7],
                                            otb[:, :], axis=AX.X), sm_v,
                               waits=[tx2])
                    tas = comp(p, "scalar", lambda s, otb=otb, c0=c0:
                               s.activation(
                                   out=s_sq3[:, :], in_=otb[:, :],
                                   func=AF.Square,
                                   accum_out=s_stc[:, c0 + 7:c0 + 8]), sm_a,
                               waits=[tvs, x1_used3[1 - sl] if False else None])
                    tv1 = comp(p, "vector", lambda v, c0=c0: v.tensor_scalar(
                        s_stc[:, c0 + 6:c0 + 7], s_stc[:, c0 + 6:c0 + 7],
                        1.0 / H, None, ALU.mult), sm_v, waits=[tas])
                    tv2 = comp(p, "vector", lambda v, c0=c0: v.tensor_tensor(
                        s_stc[:, c0:c0 + 1], s_stc[:, c0 + 6:c0 + 7],
                        s_stc[:, c0 + 6:c0 + 7], ALU.mult), sm_v, waits=[tv1])
                    tv3 = comp(p, "vector", lambda v, c0=c0:
                               v.scalar_tensor_tensor(
                                   s_stc[:, c0 + 1:c0 + 2],
                                   s_stc[:, c0 + 7:c0 + 8], 1.0 / H,
                                   s_stc[:, c0:c0 + 1],
                                   ALU.mult, ALU.subtract), sm_v, waits=[tv2])
                    ta1 = comp(p, "scalar", lambda s, c0=c0: s.activation(
                        out=s_stc[:, c0 + 2:c0 + 3],
                        in_=s_stc[:, c0 + 1:c0 + 2], func=AF.Ln, bias=s_epscol[:, :]),
                        sm_a, waits=[tv3])
                    ta2 = comp(p, "scalar", lambda s, c0=c0: s.activation(
                        out=s_stc[:, c0 + 3:c0 + 4],
                        in_=s_stc[:, c0 + 2:c0 + 3], func=AF.Exp, scale=s_mhalf[:, :]),
                        sm_a, waits=[ta1])
                    tv4 = comp(p, "vector", lambda v, c0=c0: v.tensor_tensor(
                        s_stc[:, c0 + 4:c0 + 5], s_stc[:, c0 + 6:c0 + 7],
                        s_stc[:, c0 + 3:c0 + 4], ALU.mult), sm_v, waits=[ta2])
                    txo = comp(p, "vector", lambda v, otb=otb, c0=c0:
                               v.tensor_scalar(
                                   otb[:, :], otb[:, :],
                                   s_stc[:, c0 + 3:c0 + 4],
                                   s_stc[:, c0 + 4:c0 + 5],
                                   ALU.mult, ALU.subtract), sm_v, waits=[tv4])
                    ot_used[sl] = dma(
                        p, "scalar",
                        bass.AP(out, (sbase + tb * 128) * H, [[H, 128], [1, H]]),
                        otb[:, :], sm_o, waits=[txo])
                xq_free[spar] = p.cur(sm_t)
            fwait(p, "gpsimd", [p.cur(sm_o)])
            p.run_block()

    return nc


_NC_CACHE = {}


def kernel(**inputs):
    if 8192 not in _NC_CACHE:
        _NC_CACHE[8192] = build_program(8192)
    nc = _NC_CACHE[8192]
    hidden = np.ascontiguousarray(np.asarray(inputs["L0_attn_hidden"], np.float32))
    names = {
        "ln_w": "L0_attn_ln_w", "ln_b": "L0_attn_ln_b", "in_w": "L0_attn_in_w",
        "in_b": "L0_attn_in_b", "conv_w": "L0_attn_conv_w",
        "conv_b": "L0_attn_conv_b", "xp_w": "L0_attn_xp_w",
        "dtp_w": "L0_attn_dtp_w", "dtp_b": "L0_attn_dtp_b",
        "A_log": "L0_attn_A_log", "D": "L0_attn_D", "out_w": "L0_attn_out_w",
        "out_b": "L0_attn_out_b", "w_gate": "L0_ffn_w_gate",
        "w_up": "L0_ffn_w_up", "w_down": "L0_ffn_w_down",
    }
    common = {k: np.ascontiguousarray(np.asarray(inputs[v], np.float32))
              for k, v in names.items()}
    in_maps = []
    for b in range(hidden.shape[0]):
        m = dict(common)
        m["hidden"] = np.ascontiguousarray(hidden[b])
        in_maps.append(m)
    res = run_bass_kernel_spmd(nc, in_maps, core_ids=list(range(len(in_maps))))
    return np.stack([np.asarray(r["out"], np.float32) for r in res.results])
